# version 29
# speedup vs baseline: 1.0555x; 1.0001x over previous
import math
import sys
from contextlib import ExitStack

import numpy as np

import concourse.bass as bass
import concourse.mybir as mybir
from concourse.bass_utils import run_bass_kernel_spmd

dt = mybir.dt
AF = mybir.ActivationFunctionType
OP = mybir.AluOpType
AX = mybir.AxisListType
F32 = dt.float32
BF16 = dt.bfloat16

MAGIC = 12582912.0

B, S, D = 4, 4096, 1024
NCORE = 8
TPC = S // 2
NT = TPC // 128
H, HD = 16, 64
K_CL, FC = 4, 8
TK = 38
NSUB = 5
NBLK = 384
NSEL = 1728
DI = 3072
LN_EPS = 1e-5
ALL8 = [list(range(8))]
PAIRS = [[0, 1], [2, 3], [4, 5], [6, 7]]
QMAX0 = 127.0
QMAX1 = 32767.0



def _host_prep(inp):
    import ml_dtypes
    bf = ml_dtypes.bfloat16

    clusters = np.asarray(inp["L0_attn_clusters"]).astype(np.int64)
    keyframes = np.asarray(inp["L0_attn_keyframes"]).astype(np.int64)
    assert clusters.shape == (K_CL, FC)
    assert np.array_equal(clusters.ravel(), np.arange(K_CL * FC)), clusters
    assert np.array_equal(keyframes, clusters[:, 0]), keyframes

    x = np.ascontiguousarray(np.asarray(inp["L0_attn_x"], dtype=np.float32))

    def t(a):
        return np.ascontiguousarray(np.asarray(a, dtype=np.float32).T)

    def c(a):
        return np.ascontiguousarray(np.asarray(a, dtype=np.float32))

    def fmaj(a, nt=8):
        return np.ascontiguousarray(
            np.asarray(a, dtype=np.float32).reshape(nt, 128).T)

    def bcast(a, dtype=np.float32):
        a = np.asarray(a, dtype=np.float32)
        return np.ascontiguousarray(
            np.broadcast_to(a[None, :], (128, a.shape[0])).astype(dtype))

    id128 = np.eye(128, dtype=np.float32)
    lts = np.tril(np.ones((128, 128), dtype=np.float32), -1)
    iota38 = np.ascontiguousarray(
        np.broadcast_to(np.arange(TK, dtype=np.float32)[None, :], (128, TK)))
    ones_row = np.ones((1, 128), dtype=np.float32)
    h16 = np.zeros((D, H), dtype=np.float32)
    for hh in range(H):
        h16[hh * HD:(hh + 1) * HD, hh] = 1.0
    meanw = np.zeros((16, K_CL), dtype=np.float32)
    for kf in range(K_CL):
        for bb in range(B):
            meanw[kf * 4 + bb, kf] = 0.25
    e2 = np.zeros((2, 128), dtype=np.float32)
    e2[0, 0:64] = 1.0
    e2[1, 64:128] = 1.0

    wq = c(inp["L0_attn_W_qkv"])
    bqkv = np.asarray(inp["L0_attn_b_qkv"], dtype=np.float32)
    wg = c(inp["L1_ffn_w_gate"])
    wu = c(inp["L1_ffn_w_up"])
    wd = c(inp["L1_ffn_w_down"])

    kf_cand = sorted({int(kf) % 16 for kf in keyframes})

    consts = {
        "id_f32": id128,
        "id_bf": id128.astype(bf),
        "lts": lts,
        "iota38": iota38,
        "ones_f32": ones_row,
        "ones_bf": ones_row.astype(bf),
        "e2_bf": e2.astype(bf),
        "h16": h16,
        "meanw": meanw,
        "zeros_bf": np.zeros((128, 2048), dtype=bf),
    }

    in_maps = []
    for cidx in range(NCORE):
        b_, h_ = cidx // 2, cidx % 2
        kfsel = np.zeros((1, len(kf_cand) * 16), dtype=np.float32)
        for ii, lt in enumerate(kf_cand):
            gf = 16 * h_ + lt
            w = np.where(keyframes == gf)[0]
            if len(w):
                kfsel[0, 16 * ii + int(w[0]) * 4 + b_] = 1.0
        selmask = np.zeros((128, 2 * K_CL * 2), dtype=np.float32)
        for cig in range(2):
            ci = 2 * h_ + cig
            for ncl in range(K_CL):
                if not (ncl == 0 or ncl == ci):
                    col1 = (cig * K_CL + ncl) * 2
                    selmask[190 - 128:, col1] = 1.0
                    selmask[0:48, col1 + 1] = 1.0
        m = {
            "x_in": np.ascontiguousarray(x[b_, h_ * TPC:(h_ + 1) * TPC]),
            "wqkv": wq,
            "bqkv_fm": fmaj(bqkv, 24),
            "bk_bc": bcast(bqkv[D:2 * D], bf),
            "bv_bc": bcast(bqkv[2 * D:3 * D], bf),
            "wproj": c(inp["L0_attn_W_proj"]),
            "bproj_bc": bcast(inp["L0_attn_b_proj"]),
            "wf0T": t(inp["L0_ffn_weight"]),
            "f0g_fm": fmaj(inp["L0_ffn_gamma"]),
            "f0b_fm": fmaj(inp["L0_ffn_beta"]),
            "f0g_bc": bcast(inp["L0_ffn_gamma"]),
            "f0b_bc": bcast(inp["L0_ffn_beta"]),
            "adiag_fm": fmaj(inp["L1_attn_A_diag"]),
            "bmatT": t(inp["L1_attn_Bmat"]),
            "cmatT": t(inp["L1_attn_Cmat"]),
            "dmatT": t(inp["L1_attn_Dmat"]),
            "h0_fm": fmaj(inp["L1_attn_h0"]) if h_ == 0
                     else np.zeros((128, 8), dtype=np.float32),
            "oddflag": np.full((128, 8), float(h_), dtype=np.float32),
            "lng_bc": bcast(inp["L1_attn_ln_gamma"]),
            "lnb_bc": bcast(inp["L1_attn_ln_beta"]),
            "wg_sh": np.ascontiguousarray(wg[cidx * 384:(cidx + 1) * 384]),
            "wu_sh": np.ascontiguousarray(wu[cidx * 384:(cidx + 1) * 384]),
            "wd_sh": np.ascontiguousarray(wd[cidx * 128:(cidx + 1) * 128]),
            "kfsel": kfsel,
            "selmask": selmask,
            "wqk32": np.ascontiguousarray(
                wq[:, 0:2048].reshape(8, 128, 16, 128).transpose(2, 0, 1, 3)
                .reshape(16, 1024, 128)),
            "cigflag": np.ascontiguousarray(np.broadcast_to(
                np.array([[1.0 if ci == 2 * h_ + g else 0.0
                           for g in range(2) for ci in range(K_CL)]],
                         dtype=np.float32), (128, 8))),
        }
        m.update(consts)
        in_maps.append(m)
    return in_maps, {"kf_cand": kf_cand}



ENGS = ("T", "V", "S", "G", "Y")


class Prog:
    def __init__(self, nc):
        self.nc = nc
        self.stack = ExitStack()
        self.items = {e: [] for e in ENGS}
        self.sems = {}
        self.cnt = {}
        self._ccn = 0
        self._bar = []
        self._bar_epoch = 0
        self._eng_epoch = {e: 0 for e in ENGS}
        self.NSLOT = 6
        for e in ENGS:
            self._mksem("s" + e)
            for k in range(self.NSLOT):
                self._mksem("d%s%d" % (e, k))
        self.didx = {e: 0 for e in ENGS}
        self.sb = {}
        self.ps = {}

    def _mksem(self, name):
        self.sems[name] = self.stack.enter_context(self.nc.semaphore(name))
        self.cnt[name] = 0

    _uniq = 0

    def alloc(self, name, shape, dtype, side="left"):
        Prog._uniq += 1
        cm = self.nc.sbuf_tensor("%s_u%d" % (name, Prog._uniq), list(shape),
                                 dtype, side=side)
        h = cm.__enter__()
        self.sb[name] = (cm, h)
        return h

    def barrier(self):
        bar = [("s" + e, self.cnt["s" + e]) for e in ENGS]
        for e in ENGS:
            bar.extend(self.dma_all(e))
        self._bar = [b for b in bar if b[1] > 0]
        self._bar_epoch += 1

    def free(self, *names):
        for name in names:
            cm, h = self.sb.pop(name)
            cm.__exit__(None, None, None)
        self.barrier()

    def psum(self, name, shape, dtype=F32):
        cm = self.nc.psum_tensor(name, list(shape), dtype)
        h = cm.__enter__()
        self.ps[name] = (cm, h)
        return h

    def do(self, e, fn, deps=(), mark=False, dma=False, cc=False):
        incs = []
        ev = None
        if e == "T":
            dd = [("sT", max(0, self.cnt["sT"] - 3))]
        else:
            dd = [("s" + e, self.cnt["s" + e])]
        if self._eng_epoch[e] < self._bar_epoch:
            dd.extend(self._bar)
            self._eng_epoch[e] = self._bar_epoch
        for d in deps:
            if d is None:
                continue
            if isinstance(d, list):
                dd.extend(x for x in d if x is not None)
            else:
                dd.append(d)
        if mark:
            self.cnt["s" + e] += 1
            incs.append(("s" + e, 1))
            ev = ("s" + e, self.cnt["s" + e])
        if dma:
            nm = "d%s%d" % (e, self.didx[e] % self.NSLOT)
            self.didx[e] += 1
            dd.append((nm, self.cnt[nm]))
            self.cnt[nm] += 16
            incs.append((nm, 16))
            ev = (nm, self.cnt[nm])
        if cc:
            nm = "cc%d" % self._ccn
            self._ccn += 1
            self._mksem(nm)
            self.cnt[nm] = 1
            incs.append((nm, 1))
            ev = (nm, 1)
        fr = sys._getframe(1)
        site = "%s:%d" % (fr.f_code.co_name, fr.f_lineno)
        if fr.f_code.co_name in ("dma", "ts", "tt", "stt", "red", "act", "mm",
                                 "copy", "transp", "memset", "recip", "scan",
                                 "coll", "dmat"):
            fr2 = fr.f_back
            site = "%s:%d>%s" % (fr2.f_code.co_name, fr2.f_lineno, fr.f_code.co_name)
        self.items[e].append((tuple(dd), fn, tuple(incs), site))
        return ev

    def dma_all(self, e):
        return [("d%s%d" % (e, k), self.cnt["d%s%d" % (e, k)])
                for k in range(self.NSLOT)]

    def now(self, e):
        return ("s" + e, self.cnt["s" + e])

    def finalize(self):
        with self.nc.Block() as block:
            self.sitemap = {e: [] for e in ENGS}
            def runner(e):
                def go(eng):
                    seen = {}
                    for deps, fn, incs, site in self.items[e]:
                        nw = 0
                        for s_, v_ in deps:
                            if seen.get(s_, -1) < v_:
                                eng.wait_ge(self.sems[s_], v_)
                                seen[s_] = v_
                                nw += 1
                        inst = fn(eng)
                        for s_, n_ in incs:
                            inst.then_inc(self.sems[s_], n_)
                        self.sitemap[e].append((site, nw + 1))
                return go
            block.tensor(runner("T"))
            block.vector(runner("V"))
            block.scalar(runner("S"))
            block.gpsimd(runner("G"))
            block.sync(runner("Y"))



class Builder:
    def __init__(self, meta, taps=()):
        self.meta = meta
        self.taps = set(taps)
        nc = bass.Bass("TRN2", target_bir_lowering=False)
        self.nc = nc
        self.p = Prog(nc)
        self._declare_io()
        self._alloc_base()

    def _declare_io(self):
        nc = self.nc

        def din(name, shape, dtype=F32):
            return nc.dram_tensor(name, list(shape), dtype, kind="ExternalInput")

        def dint(name, shape, dtype=F32):
            return nc.dram_tensor(name, list(shape), dtype)

        self.d = d = {}
        d["x_in"] = din("x_in", (TPC, D))
        d["wqkv"] = din("wqkv", (D, 3 * D))
        d["bqkv_fm"] = din("bqkv_fm", (128, 24))
        d["bk_bc"] = din("bk_bc", (128, D), BF16)
        d["bv_bc"] = din("bv_bc", (128, D), BF16)
        d["wproj"] = din("wproj", (D, D))
        d["bproj_bc"] = din("bproj_bc", (128, D))
        d["wf0T"] = din("wf0T", (D, D))
        for nm in ("f0g_fm", "f0b_fm", "adiag_fm", "h0_fm", "oddflag"):
            d[nm] = din(nm, (128, 8))
        for nm in ("f0g_bc", "f0b_bc", "lng_bc", "lnb_bc"):
            d[nm] = din(nm, (128, D))
        d["bmatT"] = din("bmatT", (D, D))
        d["cmatT"] = din("cmatT", (D, D))
        d["dmatT"] = din("dmatT", (D, D))
        d["wg_sh"] = din("wg_sh", (384, D))
        d["wu_sh"] = din("wu_sh", (384, D))
        d["wd_sh"] = din("wd_sh", (128, DI))
        d["id_f32"] = din("id_f32", (128, 128))
        d["id_bf"] = din("id_bf", (128, 128), BF16)
        d["lts"] = din("lts", (128, 128))
        d["iota38"] = din("iota38", (128, TK))
        d["ones_f32"] = din("ones_f32", (1, 128))
        d["ones_bf"] = din("ones_bf", (1, 128), BF16)
        d["e2_bf"] = din("e2_bf", (2, 128), BF16)
        d["h16"] = din("h16", (D, H))
        d["meanw"] = din("meanw", (16, K_CL))
        d["zeros_bf"] = din("zeros_bf", (128, 2048), BF16)
        d["kfsel"] = din("kfsel", (1, len(self.meta["kf_cand"]) * 16))
        d["selmask"] = din("selmask", (128, 2 * K_CL * 2))
        d["cigflag"] = din("cigflag", (128, 8))
        d["wqk32"] = din("wqk32", (16, 1024, 128))

        d["y"] = nc.dram_tensor("y", [TPC, D], F32, kind="ExternalOutput")

        d["park_x"] = dint("park_x", (TPC, D))
        d["park_g0"] = dint("park_g0", (TPC, D), BF16)
        d["park_up"] = dint("park_up", (DI, TPC))
        d["park_gate"] = dint("park_gate", (DI, TPC))
        d["park_g2"] = dint("park_g2", (DI, TPC))
        d["park_op"] = dint("park_op", (DI, TPC))
        d["park_s"] = dint("park_s", (TPC, D))
        for nm in ("f0", "sx", "ug", "g2", "oo"):
            d["cc_%s_i" % nm] = dint("cc_%s_i" % nm, (1, 8))
            d["cc_%s_o" % nm] = dint("cc_%s_o" % nm, (1, 8))
        d["cc_sc_i"] = dint("cc_sc_i", (16, 128))
        d["cc_sc_o"] = dint("cc_sc_o", (16, 128))
        d["cc_kn_i"] = dint("cc_kn_i", (16, 1))
        d["cc_kn_o"] = dint("cc_kn_o", (32, 1))
        d["cc_kT_i"] = dint("cc_kT_i", (D, 16 * TK), BF16)
        d["cc_kT_o"] = dint("cc_kT_o", (2 * D, 16 * TK), BF16)
        d["cc_v_i"] = dint("cc_v_i", (16 * TK, 1288), BF16)
        d["cc_v_o"] = dint("cc_v_o", (2 * 16 * TK, 1288), BF16)
        d["cc_hm_i"] = dint("cc_hm_i", (128, 8))
        d["cc_hm_o"] = dint("cc_hm_o", (256, 8))
        d["cc_wg_i"] = dint("cc_wg_i", (384, D), BF16)
        d["cc_wg_o"] = dint("cc_wg_o", (DI, D), BF16)
        d["cc_wu_i"] = dint("cc_wu_i", (384, D), BF16)
        d["cc_wu_o"] = dint("cc_wu_o", (DI, D), BF16)
        d["cc_wd_i"] = dint("cc_wd_i", (128, DI), BF16)
        d["cc_wd_o"] = dint("cc_wd_o", (D, DI), BF16)

    def _alloc_base(self):
        p = self.p
        self.idf = p.alloc("idf", (128, 128), F32)
        self.idb = p.alloc("idb", (128, 128), BF16)
        self.ltsb = p.alloc("ltsb", (128, 128), F32)
        self.iotab = p.alloc("iotab", (128, TK), F32)
        self.onesf = p.alloc("onesf", (1, 128), F32)
        self.onesb = p.alloc("onesb", (1, 128), BF16)
        self.e2b = p.alloc("e2b", (2, 128), BF16)
        self.h16b = p.alloc("h16b", (128, 8 * H), F32)
        self.meanwb = p.alloc("meanwb", (16, K_CL), F32)
        self.kfselb = p.alloc("kfselb", (1, len(self.meta["kf_cand"]) * 16), F32)
        self.selmb = p.alloc("selmb", (128, 2 * K_CL * 2), F32)
        self.cigfb = p.alloc("cigfb", (128, 8), F32)
        self.fm8 = p.alloc("fm8", (128, 5 * 8), F32)
        self.bqfm = p.alloc("bqfm", (128, 24), F32)
        self.stats = p.alloc("stats", (128, 64), F32)
        self.smallf = p.alloc("smallf", (128, 96), F32)
        self.scr = p.alloc("scr", (128, 2048), F32)
        self.scrb = p.alloc("scrb", (128, 3360), BF16)
        self.zbf = p.alloc("zbf", (128, 2048), BF16)
        self.fcb = p.alloc("fcb", (128, 8), F32)
        self._fvals = [LN_EPS, LN_EPS * LN_EPS, 1.0 + LN_EPS, 1.0, MAGIC,
                       65536.0, 0.125, -MAGIC]
        e_fc = None
        for k, v in enumerate(self._fvals):
            col = self.fcb[:, k:k + 1]
            e_fc = self.memset(col, v)
            self.nc.const_aps.register(F32, v, col)
        self.e_fc = self.memset(self.smallf[0:1, 56:64], 0.0)
        self.xsb = p.alloc("xsb", (128, NT * D), F32, side="right")
        self.psA = p.psum("psA", (128, 1024))
        self.psB = p.psum("psB", (128, 1024))
        self.psC = p.psum("psC", (128, 1024))
        self.psD = p.psum("psD", (128, 1024))

    def dma(self, e, out, in_, deps=()):
        return self.p.do(e, lambda eng, o=out, i=in_: eng.dma_start(out=o, in_=i),
                         deps=deps, dma=True)

    def dmat(self, e, out, in_, deps=()):
        return self.p.do(e, lambda eng, o=out, i=in_:
                         eng.dma_start_transpose(o, i), deps=deps, dma=True)

    def ts(self, out, in0, s1, s2, op0, op1=OP.bypass, deps=(), e="V"):
        return self.p.do(e, lambda eng, o=out, i=in0, a=s1, b=s2, p0=op0, p1=op1:
                         eng.tensor_scalar(o, i, a, b, p0, p1), deps=deps, mark=True)

    def tt(self, out, in0, in1, op, deps=(), e="V"):
        return self.p.do(e, lambda eng, o=out, a=in0, b=in1, pp=op:
                         eng.tensor_tensor(o, a, b, pp), deps=deps, mark=True)

    def stt(self, out, in0, scal, in1, op0, op1, deps=(), e="V"):
        return self.p.do(e, lambda eng, o=out, a=in0, s=scal, b=in1, p0=op0, p1=op1:
                         eng.scalar_tensor_tensor(o, a, s, b, p0, p1),
                         deps=deps, mark=True)

    def red(self, out, in_, op, deps=(), absv=None, e="V"):
        return self.p.do(e, lambda eng, o=out, i=in_, pp=op, av=absv:
                         eng.tensor_reduce(o, i, AX.X, pp, apply_absolute_value=av),
                         deps=deps, mark=True)

    def act(self, out, in_, func, bias=0.0, scale=1.0, deps=(), accum=None):
        def f(eng, o=out, i=in_, fu=func, bi=bias, sc=scale, ac=accum):
            if ac is not None:
                return eng.activation(o, i, fu, bias=bi, scale=sc, accum_out=ac)
            return eng.activation(o, i, fu, bias=bi, scale=sc)
        return self.p.do("S", f, deps=deps, mark=True)

    def mm(self, out, lhsT, rhs, start, stop, deps=(), mark=False, tp=None):
        return self.p.do("T", lambda eng, o=out, l=lhsT, r=rhs, st=start, sp=stop,
                         t=tp: eng.matmul(o, l, r, start=st, stop=sp,
                                          tile_position=t), deps=deps, mark=mark)

    def copy(self, out, in_, deps=(), e="V"):
        return self.p.do(e, lambda eng, o=out, i=in_: eng.tensor_copy(o, i),
                         deps=deps, mark=True)

    def transp(self, out, in_, ident, deps=(), mark=False):
        return self.p.do("T", lambda eng, o=out, i=in_, d_=ident:
                         eng.transpose(o, i, d_), deps=deps, mark=mark)

    def memset(self, ap_, val, deps=(), e="V"):
        return self.p.do(e, lambda eng, a=ap_, v=val: eng.memset(a, v),
                         deps=deps, mark=True)

    def recip(self, out, in_, deps=(), lp=False):
        def f(eng, o=out, i=in_, lp=lp):
            if lp:
                with self.nc.allow_low_precision(reason="bf16 softmax normalizer"):
                    return eng.reciprocal(o, i)
            return eng.reciprocal(o, i)
        return self.p.do("V", f, deps=deps, mark=True)

    def scan(self, out, d0, d1, init, deps=()):
        return self.p.do("V", lambda eng, o=out, a=d0, b=d1, i=init:
                         eng.tensor_tensor_scan(o, a, b, i, OP.mult, OP.add),
                         deps=deps, mark=True)

    def coll(self, kind, op, ins_, outs_, groups, deps=()):
        return self.p.do("G", lambda eng, k=kind, o=op, i=ins_, u=outs_, g=groups:
                         eng.collective_compute(k, o, replica_groups=g, ins=i, outs=u),
                         deps=deps, cc=True)

    _pr_last = None
    _bc_last = None

    def part_reduce(self, col_ap, dst, op, dep, absv=None):
        e1 = self.transp(self.psD[0:1, 640:768], col_ap, self.idf[:, :],
                         deps=[dep, Builder._pr_last], mark=True)
        Builder._pr_last = self.red(self.smallf[0:1, dst:dst + 1],
                                    self.psD[0:1, 640:768], op,
                                    deps=[e1], absv=absv)
        return Builder._pr_last

    def bcast_col(self, src, dst, dep, scale=None, bias_add=None):
        e1 = self.mm(self.psD[:, 800:801], self.onesf[0:1, :],
                     self.smallf[0:1, src:src + 1], True, True,
                     deps=[dep, Builder._bc_last], mark=True)
        if scale is None and bias_add is None:
            Builder._bc_last = self.copy(self.smallf[:, dst:dst + 1],
                                         self.psD[:, 800:801], deps=[e1])
        else:
            Builder._bc_last = self.ts(
                self.smallf[:, dst:dst + 1], self.psD[:, 800:801],
                1.0 if scale is None else scale, bias_add,
                OP.mult, OP.bypass if bias_add is None else OP.add, deps=[e1])
        return Builder._bc_last

    def ar_scalar(self, tag, src, dst, op, dep):
        d = self.d
        e0 = self.copy(self.smallf[0:1, 56:57], self.smallf[0:1, src:src + 1],
                       deps=[dep])
        e1 = self.dma("G", d["cc_%s_i" % tag][0:1, 0:8], self.smallf[0:1, 56:64],
                      deps=[e0])
        e2 = self.coll("AllReduce", op, [d["cc_%s_i" % tag].ap().opt()],
                       [d["cc_%s_o" % tag].ap().opt()], ALL8, deps=[e1])
        return self.dma("G", self.smallf[0:1, dst:dst + 1],
                        d["cc_%s_o" % tag][0:1, 0:1], deps=[e2])

    def ln_stats(self, dep):
        evs = []
        for t in range(NT):
            xt = self.xsb[:, D * t:D * (t + 1)]
            evs.append(self.red(self.stats[:, t:t + 1], xt, OP.add, deps=[dep]))
        e_m = self.ts(self.stats[:, 0:16], self.stats[:, 0:16], 1.0 / D, None,
                      OP.mult, deps=[evs[-1]])
        e_q = None
        for t in range(NT):
            xt = self.xsb[:, D * t:D * (t + 1)]
            buf = self.scr[:, 0:D] if t % 2 == 0 else self.scr[:, D:2 * D]
            e1 = self.act(buf, xt, AF.Square,
                          deps=[e_m if t < 2 else e_q, self.e_fc])
            e_q = self.red(self.stats[:, 48 + t:49 + t], buf, OP.add, deps=[e1])
        e_q2 = self.ts(self.stats[:, 48:64], self.stats[:, 48:64], 1.0 / D, None,
                       OP.mult, deps=[e_q])
        e_m2 = self.tt(self.smallf[:, 32:48], self.stats[:, 0:16],
                       self.stats[:, 0:16], OP.mult, deps=[e_q2])
        e_v = self.tt(self.smallf[:, 32:48], self.stats[:, 48:64],
                      self.smallf[:, 32:48], OP.subtract, deps=[e_m2])
        e_s1 = self.act(self.smallf[:, 48:64], self.smallf[:, 32:48], AF.Sqrt,
                        bias=LN_EPS, deps=[e_v])
        e_r1 = self.recip(self.stats[:, 16:32], self.smallf[:, 48:64], deps=[e_s1])
        e_s2 = self.act(self.smallf[:, 48:64], self.smallf[:, 32:48], AF.Sqrt,
                        bias=LN_EPS * LN_EPS, scale=1.0 + LN_EPS, deps=[e_r1])
        e_r2 = self.recip(self.stats[:, 32:48], self.smallf[:, 48:64], deps=[e_s2])
        return e_r1, e_r2

    def phase_loads(self):
        d = self.d
        x_t = d["x_in"].ap().rearrange("(n p) m -> n p m", p=128)
        for t in range(NT):
            self.e_x = self.dma("Y", self.xsb[:, D * t:D * (t + 1)], x_t[t])
        self.dma("Y", self.idf[:, :], d["id_f32"][:, :])
        self.dma("Y", self.idb[:, :], d["id_bf"][:, :])
        self.dma("Y", self.ltsb[:, :], d["lts"][:, :])
        self.dma("Y", self.iotab[:, :], d["iota38"][:, :])
        self.dma("Y", self.onesf[:, :], d["ones_f32"][:, :])
        self.dma("Y", self.onesb[:, :], d["ones_bf"][:, :])
        self.dma("Y", self.e2b[:, :], d["e2_bf"][:, :])
        h16t = d["h16"].ap().rearrange("(n p) m -> n p m", p=128)
        for i in range(8):
            self.dma("Y", self.h16b[:, H * i:H * (i + 1)], h16t[i])
        self.dma("Y", self.meanwb[:, :], d["meanw"][:, :])
        self.dma("Y", self.kfselb[:, :], d["kfsel"][:, :])
        self.dma("Y", self.selmb[:, :], d["selmask"][:, :])
        self.dma("Y", self.cigfb[:, :], d["cigflag"][:, :])
        for i, nm in enumerate(("f0g_fm", "f0b_fm", "adiag_fm", "h0_fm", "oddflag")):
            self.dma("Y", self.fm8[:, 8 * i:8 * i + 8], d[nm][:, :])
        self.dma("Y", self.bqfm[:, :], d["bqkv_fm"][:, :])
        self.e_consts = self.p.dma_all("Y")
        self.e_x = self.p.dma_all("Y")

    def phase_wdequant(self, dep):
        d, p = self.d, self.p
        wqsh = p.alloc("wqsh", (128, DI), F32)
        wqbm = p.alloc("wqbm", (128, 192), F32)
        parts = []
        e_prev = dep
        for (src, dst, ilen, ntile) in ((d["wg_sh"], d["cc_wg_i"], D, 3),
                                        (d["wu_sh"], d["cc_wu_i"], D, 3),
                                        (d["wd_sh"], d["cc_wd_i"], DI, 1)):
            nblk = ilen // 32
            for tt_ in range(ntile):
                e_l = self.dma("Y", wqsh[:, 0:ilen],
                               src[128 * tt_:128 * (tt_ + 1), :],
                               deps=[e_prev] if e_prev else ())
                in3 = bass.AP(wqsh, 0, [[DI, 128], [32, nblk], [1, 32]])
                e_bm = self.red(wqbm[:, 0:nblk], in3, OP.max, deps=[e_l], absv=True)
                e_s = self.ts(wqbm[:, 0:nblk], wqbm[:, 0:nblk], 1.0 / 7.0, 1e-12,
                              OP.mult, OP.max, deps=[e_bm])
                e_rs = self.recip(wqbm[:, 96:96 + nblk], wqbm[:, 0:nblk], deps=[e_s])
                sblk = bass.AP(wqbm, 0, [[192, 128], [1, nblk], [0, 32]])
                rsblk = bass.AP(wqbm, 96, [[192, 128], [1, nblk], [0, 32]])
                w3 = bass.AP(wqsh, 0, [[DI, 128], [32, nblk], [1, 32]])
                s3 = bass.AP(self.scr, 0, [[2048, 128], [32, nblk], [1, 32]])
                if ilen <= 2048:
                    e_t1 = self.tt(s3, w3, rsblk, OP.mult, deps=[e_rs])
                    e_t2 = self.ts(self.scr[:, 0:ilen], self.scr[:, 0:ilen],
                                   MAGIC, MAGIC, OP.add, OP.subtract, deps=[e_t1])
                    e_t3 = self.tt(s3, s3, sblk, OP.mult, deps=[e_t2])
                    e_c = self.copy(self.scrb[:, 0:ilen], self.scr[:, 0:ilen],
                                    deps=[e_t3])
                else:
                    e_c = None
                    for hh in range(2):
                        o0 = hh * (ilen // 2)
                        nb2 = nblk // 2
                        w3h = bass.AP(wqsh, o0, [[DI, 128], [32, nb2], [1, 32]])
                        s3h = bass.AP(self.scr, 0, [[2048, 128], [32, nb2], [1, 32]])
                        sblkh = bass.AP(wqbm, nb2 * hh, [[192, 128], [1, nb2], [0, 32]])
                        rsblkh = bass.AP(wqbm, 96 + nb2 * hh,
                                         [[192, 128], [1, nb2], [0, 32]])
                        e_t1 = self.tt(s3h, w3h, rsblkh, OP.mult, deps=[e_rs])
                        e_t2 = self.ts(self.scr[:, 0:ilen // 2],
                                       self.scr[:, 0:ilen // 2],
                                       MAGIC, MAGIC, OP.add, OP.subtract, deps=[e_t1])
                        e_t3 = self.tt(s3h, s3h, sblkh, OP.mult, deps=[e_t2])
                        e_c = self.copy(self.scrb[:, o0:o0 + ilen // 2],
                                        self.scr[:, 0:ilen // 2], deps=[e_t3])
                e_prev = self.dma("Y", dst[128 * tt_:128 * (tt_ + 1), 0:ilen],
                                  self.scrb[:, 0:ilen], deps=[e_c])
                parts.append(e_prev)
        self.e_wq_ag = {}
        for nm in ("wg", "wu", "wd"):
            self.e_wq_ag[nm] = self.coll(
                "AllGather", OP.bypass, [d["cc_%s_i" % nm].ap().opt()],
                [d["cc_%s_o" % nm].ap().opt()], ALL8, deps=parts)
        p.free("wqbm", "wqsh")

    def phase_f0_absmax(self, e_r2):
        p, d = self.p, self.d
        gb = p.alloc("f0bc", (128, 2 * D), F32)
        e_g = self.dma("Y", gb[:, 0:D], d["f0g_bc"][:, :])
        e_b = self.dma("Y", gb[:, D:2 * D], d["f0b_bc"][:, :])
        e_last = None
        for t in range(NT):
            xt = self.xsb[:, D * t:D * (t + 1)]
            buf = self.scr[:, 0:D] if t % 2 == 0 else self.scr[:, D:2 * D]
            e1 = self.ts(buf, xt, self.stats[:, t:t + 1],
                         self.stats[:, 32 + t:33 + t], OP.subtract, OP.mult,
                         deps=[e_r2, e_g, e_b])
            e2 = self.tt(buf, buf, gb[:, 0:D], OP.mult, deps=[e1])
            e3 = self.tt(buf, buf, gb[:, D:2 * D], OP.add, deps=[e2])
            e_last = self.red(self.smallf[:, 8 + t:9 + t], buf, OP.max,
                              deps=[e3], absv=True)
        e5 = self.red(self.smallf[:, 24:25], self.smallf[:, 8:24], OP.max,
                      deps=[e_last])
        e6 = self.part_reduce(self.smallf[:, 24:25], 0, OP.max, e5)
        e_rd = self.ar_scalar("f0", 0, 1, OP.max, e6)
        e_s = self.ts(self.smallf[0:1, 2:3], self.smallf[0:1, 1:2], 1e-8,
                      1.0 / QMAX0, OP.max, OP.mult, deps=[e_rd])
        e_r = self.recip(self.smallf[0:1, 3:4], self.smallf[0:1, 2:3], deps=[e_s])
        self.e_rf0 = self.bcast_col(3, 4, e_r)
        self.e_sf0 = self.bcast_col(2, 5, self.e_rf0)
        p.free("f0bc")

    def phase_xnT(self, e_r1):
        p = self.p
        self.xnT = p.alloc("xnT", (128, 8 * TPC), BF16)
        e_tp = {}
        for t in range(NT):
            xt = self.xsb[:, D * t:D * (t + 1)]
            buf = self.scr[:, 0:D] if (t % 2 == 0) else self.scr[:, D:2 * D]
            dep = [e_r1]
            if t >= 2:
                dep.append(e_tp[t - 2]["tp"])
            e1 = self.ts(buf, xt, self.stats[:, t:t + 1],
                         self.stats[:, 16 + t:17 + t], OP.subtract, OP.mult,
                         deps=dep)
            pt = self.psA if (t % 2 == 0) else self.psB
            e_tpl = None
            for i in range(8):
                dep2 = [e1]
                if t >= 2:
                    dep2.append(e_tp[t - 2]["ev"])
                e_tpl = self.transp(pt[:, 128 * i:128 * (i + 1)],
                                    buf[:, 128 * i:128 * (i + 1)],
                                    self.idf[:, :], deps=dep2 if i == 0 else (),
                                    mark=(i == 7))
            e_ev = None
            for i in range(8):
                e_ev = self.act(self.xnT[:, TPC * i + 128 * t:TPC * i + 128 * (t + 1)],
                                pt[:, 128 * i:128 * (i + 1)], AF.Identity, deps=[e_tpl])
            e_tp[t] = {"tp": e_tpl, "ev": e_ev}
        self.e_xnT = e_tp[NT - 1]["ev"]

    def phase_q(self):
        p, d = self.p, self.d
        wqw = p.alloc("wqw", (128, 8 * D), BF16)
        wqt = d["wqkv"].ap().rearrange("(n p) m -> n p m", p=128)
        for i in range(8):
            self.dma("G", wqw[:, D * i:D * (i + 1)], wqt[i][:, 0:D])
        e_wq = self.p.dma_all("G")
        nqc = TPC // 512
        e_ev_prev = {}
        for qi in range(8):
            for tc in range(nqc):
                k = (qi * nqc + tc) % 4
                pt = self.psA if k < 2 else self.psB
                bank = pt[:, 0:512] if (k % 2 == 0) else pt[:, 512:1024]
                deps = [self.e_xnT, e_wq]
                if k in e_ev_prev:
                    deps.append(e_ev_prev[k])
                ee = None
                for i in range(8):
                    ee = self.mm(bank, wqw[:, D * i + 128 * qi:D * i + 128 * (qi + 1)],
                                 self.xnT[:, TPC * i + 512 * tc:TPC * i + 512 * (tc + 1)],
                                 start=(i == 0), stop=(i == 7),
                                 deps=deps if i == 0 else (), mark=(i == 7))
                e_ev = self.act(self.qT[:, TPC * qi + 512 * tc:TPC * qi + 512 * (tc + 1)],
                                bank, AF.Identity, bias=self.bqfm[:, qi:qi + 1], deps=[ee])
                e_ev_prev[k] = e_ev
        self.e_qT = self.p.now("S")
        p.free("wqw")

    def _scores_impl(self):
        p, d = self.p, self.d
        kf_cand = self.meta["kf_cand"]
        nkf = len(kf_cand)
        kfx = p.alloc("kfx", (128, nkf * 8 * 128), F32)
        wqf = p.alloc("wqf32", (128, 3 * 1024), F32)
        wqt = d["wqkv"].ap().rearrange("(n p) m -> n p m", p=128)

        e_ev = None
        for ii, lt in enumerate(kf_cand):
            buf = self.scr[:, 0:D] if ii % 2 == 0 else self.scr[:, D:2 * D]
            e1 = self.ts(buf, self.xsb[:, D * lt:D * (lt + 1)],
                         self.stats[:, lt:lt + 1], self.stats[:, 16 + lt:17 + lt],
                         OP.subtract, OP.mult, deps=[self.e_xnT, e_ev])
            e_tpl = None
            for i in range(8):
                e_tpl = self.transp(self.psA[:, 128 * i:128 * (i + 1)],
                                    buf[:, 128 * i:128 * (i + 1)], self.idf[:, :],
                                    deps=[e1, e_ev] if i == 0 else (),
                                    mark=(i == 7))
            for i in range(8):
                e_ev = self.copy(kfx[:, 1024 * ii + 128 * i:1024 * ii + 128 * (i + 1)],
                                 self.psA[:, 128 * i:128 * (i + 1)], deps=[e_tpl])
        qkkf = p.alloc("qkkf", (128, 16 * (nkf * 128)), F32)
        e_prev = {}
        e_mmo = {}
        W = nkf * 128
        for oc in range(16):
            k2 = oc % 3
            e_wf = self.dma(
                "YGS"[oc % 3], wqf[:, 1024 * k2:1024 * (k2 + 1)],
                bass.AP(d["wqk32"], 1024 * 128 * oc,
                        [[128, 128], [128 * 128, 8], [1, 128]]),
                deps=[e_mmo.get(k2)])
            assert W <= 512
            bank = (self.psB[:, 0:W], self.psB[:, 512:512 + W],
                    self.psA[:, 0:W])[k2]
            ee = None
            for i in range(8):
                mv = bass.AP(kfx, 128 * i,
                             [[nkf * 8 * 128, 128], [1024, nkf], [1, 128]])
                ee = self.mm(bank,
                             wqf[:, 1024 * k2 + 128 * i:1024 * k2 + 128 * (i + 1)],
                             mv, start=(i == 0), stop=(i == 7),
                             deps=[e_wf, e_ev, e_prev.get(k2)] if i == 0 else (),
                             mark=(i == 7))
            e_mmo[k2] = ee
            e_pr = self.act(qkkf[:, W * oc:W * (oc + 1)], bank, AF.Identity,
                            bias=self.bqfm[:, oc:oc + 1], deps=[ee])
            e_prev[k2] = e_pr
        e_qk = self.p.now("S")

        e_ar_parts = []
        for ii in range(nkf):
            e_pr = None
            for i in range(8):
                e_pr = self.tt(self.scr[:, 128 * i:128 * (i + 1)],
                               qkkf[:, W * i + 128 * ii:W * i + 128 * (ii + 1)],
                               qkkf[:, W * (8 + i) + 128 * ii:W * (8 + i) + 128 * (ii + 1)],
                               OP.mult, deps=[e_qk] if i == 0 else ())
            ee = None
            for i in range(8):
                ee = self.mm(self.psB[0:16, 0:128], self.h16b[:, H * i:H * (i + 1)],
                             self.scr[:, 128 * i:128 * (i + 1)],
                             start=(i == 0), stop=(i == 7),
                             deps=[e_pr] if i == 0 else (), mark=(i == 7))
            e_hs = self.copy(self.scr[0:16, 1024:1152], self.psB[0:16, 0:128],
                             deps=[ee])
            e_tp = self.transp(self.psB[0:128, 512:528], self.scr[0:16, 1024:1152],
                               self.idf[0:16, 0:16], deps=[e_hs], mark=True)
            e_mx = self.red(self.scr[:, 1152:1153], self.psB[:, 512:528], OP.max,
                            deps=[e_tp])
            e_tp2 = self.transp(self.psB[0:1, 528:656], self.scr[:, 1152:1153],
                                self.idf[:, :], deps=[e_mx], mark=True)
            e_row = self.copy(self.scr[0:1, 1168:1296], self.psB[0:1, 528:656],
                              deps=[e_tp2])
            e_pl = self.mm(self.psC[0:16, 0:128],
                           self.kfselb[0:1, 16 * ii:16 * (ii + 1)],
                           self.scr[0:1, 1168:1296],
                           start=(ii == 0), stop=(ii == nkf - 1),
                           deps=[e_row], mark=True)
            e_ar_parts.append(e_pl)
        e_arb = self.copy(self.scr[0:16, 0:128], self.psC[0:16, 0:128],
                          deps=[e_ar_parts[-1], self.p.now("V")])
        e_w = self.dma("G", d["cc_sc_i"][:, :], self.scr[0:16, 0:128], deps=[e_arb])
        e_cc = self.coll("AllReduce", OP.add, [d["cc_sc_i"].ap().opt()],
                         [d["cc_sc_o"].ap().opt()], ALL8, deps=[e_w])
        self.e_scores_ar = self.dma("G", self.scr[0:16, 0:128], d["cc_sc_o"][:, :],
                                    deps=[e_cc])
        p.free("qkkf", "wqf32", "kfx")

    def phase_select(self):
        p = self.p
        e1 = self.mm(self.psB[0:4, 0:128], self.meanwb[:, :], self.scr[0:16, 0:128],
                     True, True, deps=[self.e_scores_ar], mark=True)
        e2 = self.copy(self.scr[0:4, 128:256], self.psB[0:4, 0:128], deps=[e1])
        e3 = self.transp(self.psB[0:128, 512:516], self.scr[0:4, 128:256],
                         self.idf[0:4, 0:4], deps=[e2], mark=True)
        e4 = self.copy(self.scr[:, 256:260], self.psB[:, 512:516], deps=[e3])
        self.onehot = p.alloc("onehot", (128, K_CL * TK), BF16)
        for ci in range(K_CL):
            scol = self.scr[:, 256 + ci:257 + ci]
            e4b = self.transp(self.psB[0:1, 512:640], scol, self.idf[:, :],
                              deps=[e4], mark=True)
            e4c = self.copy(self.scr[0:1, 128:256], self.psB[0:1, 512:640],
                            deps=[e4b])
            e5 = self.mm(self.psB[:, 0:128], self.onesf[0:1, :],
                         self.scr[0:1, 128:256], True, True,
                         deps=[e4c], mark=True)
            eA = self.ts(self.scr[:, 512:640], self.psB[:, 0:128], scol, None,
                         OP.is_gt, deps=[e5])
            eB = self.ts(self.scr[:, 640:768], self.psB[:, 0:128], scol, None,
                         OP.is_equal, deps=[eA])
            eC = self.tt(self.scr[:, 640:768], self.scr[:, 640:768], self.ltsb[:, :],
                         OP.mult, deps=[eB])
            eD = self.tt(self.scr[:, 512:640], self.scr[:, 512:640],
                         self.scr[:, 640:768], OP.add, deps=[eC])
            eR = self.red(self.scr[:, 260 + ci:261 + ci], self.scr[:, 512:640],
                          OP.add, deps=[eD])
            eM = self.ts(self.scr[:, 264 + ci:265 + ci], self.scr[:, 260 + ci:261 + ci],
                         TK - 0.5, None, OP.is_lt, deps=[eR])
            eT = self.transp(self.psB[0:1, 512:640], self.scr[:, 264 + ci:265 + ci],
                             self.idf[:, :], deps=[eM], mark=True)
            eRw = self.copy(self.scr[0:1, 384:512], self.psB[0:1, 512:640], deps=[eT])
            eMb = self.mm(self.psB[:, 0:128], self.onesf[0:1, :],
                          self.scr[0:1, 384:512], True, True, deps=[eRw], mark=True)
            eC2 = self.tt(self.scr[:, 512:640], self.psB[:, 0:128], self.ltsb[:, :],
                          OP.mult, deps=[eMb])
            eCs = self.red(self.scr[:, 268 + ci:269 + ci], self.scr[:, 512:640],
                           OP.add, deps=[eC2])
            eOh = self.ts(self.scr[:, 512:512 + TK], self.iotab[:, :],
                          self.scr[:, 268 + ci:269 + ci], None, OP.is_equal,
                          deps=[eCs])
            self.e_onehot = self.ts(self.onehot[:, TK * ci:TK * (ci + 1)],
                                    self.scr[:, 512:512 + TK],
                                    self.scr[:, 264 + ci:265 + ci], None, OP.mult,
                                    deps=[eOh])

    def phase_kv_kmm(self):
        p, d = self.p, self.d
        self.kst = p.alloc("kstore", (128, NT * D), BF16)
        e_kb, e_vb = self.e_kvb
        self.e_kstev = {}
        e_gevac = {}
        for t in range(NT):
            e_ev = None
            for half in range(2):
                bank = self.psA[:, 512 * half:512 * (half + 1)]
                ee = None
                for i in range(8):
                    ee = self.mm(bank,
                                 self.xnT[:, TPC * i + 128 * t:TPC * i + 128 * (t + 1)],
                                 self.wqw2[:, D * i + 512 * half:D * i + 512 * (half + 1)],
                                 start=(i == 0), stop=(i == 7),
                                 deps=[self.e_xnT, e_kb, self.e_wk,
                                       e_gevac.get(half)] if i == 0 else (),
                                 mark=(i == 7))
                e_ev = self.stt(self.kst[:, D * t + 512 * half:
                                         D * t + 512 * (half + 1)], bank,
                                1.0, self.kvb[:, 512 * half:512 * (half + 1)],
                                OP.mult, OP.add, deps=[ee])
                e_gevac[half] = e_ev
            self.e_kstev[t] = e_ev

    def phase_kv(self):
        p, d = self.p, self.d
        wqw = self.wqw2
        wqt = d["wqkv"].ap().rearrange("(n p) m -> n p m", p=128)

        self.ohl = p.alloc("ohlocal", (128, 2 * TK), BF16)
        self.e_ohl2 = {}
        self.mksel = p.alloc("mksel", (128, 2), F32)
        for g in range(2):
            e1 = self.ts(self.scr[:, 0:TK], self.onehot[:, 0:TK],
                         self.cigfb[:, 4 * g:4 * g + 1], None, OP.mult,
                         deps=[self.e_onehot])
            for ci in range(1, K_CL):
                e1 = self.stt(self.scr[:, 0:TK],
                              self.onehot[:, TK * ci:TK * (ci + 1)],
                              self.cigfb[:, 4 * g + ci:4 * g + ci + 1],
                              self.scr[:, 0:TK], OP.mult, OP.add, deps=[e1])
            e2 = self.copy(self.ohl[:, TK * g:TK * (g + 1)], self.scr[:, 0:TK],
                           deps=[e1])
            self.e_ohl2[g] = e2
            self.e_mksel = self.red(self.mksel[:, g:g + 1], self.scr[:, 0:TK],
                                    OP.add, deps=[e2])

        knmax = p.alloc("knmax", (128, H), F32)
        e_kn = self.memset(knmax[:, :], 0.0)
        e_kb, e_vb = self.e_kvb

        e_gdp = {0: None, 1: None}
        e_gc = None
        e_n1p = None
        for t in range(NT):
            g = 0 if t < 8 else 1
            off = 2 * D + 304 * (t % 2)
            ktile = self.kst[:, D * t:D * (t + 1)]
            e_ev_t = self.e_kstev[t]
            ee = None
            for qi in range(8):
                ee = self.mm(self.psB[:, TK * qi:TK * (qi + 1)],
                             ktile[:, 128 * qi:128 * (qi + 1)],
                             self.ohl[:, TK * g:TK * (g + 1)],
                             start=True, stop=True,
                             deps=[e_ev_t, self.e_ohl2[g], self.e_mksel,
                                   e_gc] if qi == 0 else (),
                             mark=(qi == 7))
            e_gc = self.copy(self.scrb[:, off:off + 8 * TK],
                             self.psB[:, 0:8 * TK], deps=[ee, e_gdp[t % 2]])
            src = bass.AP(self.scrb, off, [[3360, 128], [TK, 8], [1, TK]])
            dst = bass.AP(d["cc_kT_i"], TK * t,
                          [[16 * TK, 128], [16 * TK * 128, 8], [1, TK]])
            e_gdp[t % 2] = self.dma("Y", dst, src, deps=[e_gc])
            e_sq = self.act(self.scr[:, 0:D], ktile, AF.Square,
                            deps=[e_ev_t, self.e_mksel, e_n1p])
            nrm3 = bass.AP(self.scr, 0, [[2048, 128], [HD, H], [1, HD]])
            e_n1 = self.red(self.scr[:, 1024:1024 + H], nrm3, OP.add,
                            deps=[e_sq])
            e_n1p = e_n1
            e_n2 = self.ts(self.scr[:, 1024:1024 + H],
                           self.scr[:, 1024:1024 + H],
                           self.mksel[:, g:g + 1], None, OP.mult,
                           deps=[e_n1])
            e_kn = self.tt(knmax[:, :], knmax[:, :],
                           self.scr[:, 1024:1024 + H], OP.max, deps=[e_n2])
        e_wv = None
        e_gevac = {0: None, 1: None, "v": [e_gdp[0], e_gdp[1]]}
        for which in (1,):
            if which == 1:
                for i in range(8):
                    self.dma("G", wqw[:, D * i:D * (i + 1)],
                             wqt[i][:, 2 * D:3 * D],
                             deps=[self.p.now("T")])
                e_wv = self.p.dma_all("G")
            for t in range(NT):
                g = 0 if t < 8 else 1
                ktile = self.scrb[:, 0:D] if t % 2 == 0 else self.scrb[:, D:2 * D]
                e_ev_t = None
                for half in range(2):
                    bank = self.psA[:, 512 * half:512 * (half + 1)]
                    ee = None
                    dep0 = [self.e_xnT, e_kb, e_vb, self.e_mksel, e_kn]
                    if which == 1:
                        dep0.append(e_wv)
                    dep0.append(e_gevac.get(half))
                    for i in range(8):
                        ee = self.mm(bank,
                                     self.xnT[:, TPC * i + 128 * t:TPC * i + 128 * (t + 1)],
                                     wqw[:, D * i + 512 * half:D * i + 512 * (half + 1)],
                                     start=(i == 0), stop=(i == 7),
                                     deps=dep0 if i == 0 else (), mark=(i == 7))
                    bias = self.kvb[:, D * which + 512 * half:
                                    D * which + 512 * (half + 1)]
                    e_ev_t = self.stt(ktile[:, 512 * half:512 * (half + 1)], bank,
                                      1.0, bias, OP.mult, OP.add, deps=[ee])
                    e_gevac[half] = e_ev_t
                if True:
                    ee = None
                    for half in range(2):
                        ee = self.mm(self.psB[0:TK, 512 * half:512 * (half + 1)],
                                     self.ohl[:, TK * g:TK * (g + 1)],
                                     ktile[:, 512 * half:512 * (half + 1)],
                                     start=True, stop=True,
                                     deps=[e_ev_t] if half == 0 else (),
                                     mark=(half == 1))
                    e_z0 = self.memset(self.scrb[0:TK, 2 * D:2 * D + 1288], 0.0,
                                       deps=[e_gevac.get("v")])
                    e_on = self.memset(
                        bass.AP(self.scrb, 2 * D + 64, [[3360, TK], [161, 8]]),
                        1.0, deps=[e_z0])
                    e_on = self.memset(
                        bass.AP(self.scrb, 2 * D + 65, [[3360, TK], [161, 8]]),
                        1.0, deps=[e_on])
                    srcE = bass.AP(self.psB, 0, [[1024, TK], [128, 8], [1, HD]])
                    dstE = bass.AP(self.scrb, 2 * D, [[3360, TK], [161, 8], [1, HD]])
                    e_vc = self.copy(dstE, srcE, deps=[ee, e_on])
                    srcO = bass.AP(self.psB, HD, [[1024, TK], [128, 8], [1, HD]])
                    dstO = bass.AP(self.scrb, 2 * D + 97,
                                   [[3360, TK], [161, 8], [1, HD]])
                    e_vc = self.copy(dstO, srcO, deps=[e_vc])
                    e_vd = self.dma("Y", d["cc_v_i"][TK * t:TK * (t + 1), :],
                                    self.scrb[0:TK, 2 * D:2 * D + 1288],
                                    deps=[e_vc])
                    e_gevac["v"] = e_vd
        e_t1 = self.transp(self.psB[0:H, 0:128], knmax[:, :], self.idf[:, :],
                           deps=[e_kn, self.p.now("V"), self.p.now("Y")],
                           mark=True)
        e_t2 = self.copy(self.scr[0:H, 0:128], self.psB[0:H, 0:128], deps=[e_t1])
        e_t3 = self.red(self.scr[0:H, 128:129], self.scr[0:H, 0:128], OP.max,
                        deps=[e_t2])
        e_t4 = self.dma("G", d["cc_kn_i"][:, :], self.scr[0:H, 128:129], deps=[e_t3])
        self.e_kn_ag = self.coll("AllGather", OP.bypass, [d["cc_kn_i"].ap().opt()],
                                 [d["cc_kn_o"].ap().opt()], PAIRS, deps=[e_t4])
        self.e_kv_ag = []
        dmadone = self.p.dma_all("Y")
        self.e_kv_ag.append(self.coll("AllGather", OP.bypass,
                                      [d["cc_kT_i"].ap().opt()],
                                      [d["cc_kT_o"].ap().opt()], PAIRS,
                                      deps=dmadone))
        self.e_kv_ag.append(self.coll("AllGather", OP.bypass,
                                      [d["cc_v_i"].ap().opt()],
                                      [d["cc_v_o"].ap().opt()], PAIRS,
                                      deps=dmadone))
        p.free("knmax")

    def phase_ffn0(self):
        p, d = self.p, self.d
        tT = p.alloc("tT", (128, 8 * D), BF16)
        wft = d["wf0T"].ap().rearrange("(n p) m -> n p m", p=128)
        e_red = []
        for i in range(8):
            buf = self.scr[:, 0:D] if i % 2 == 0 else self.scr[:, D:2 * D]
            dep = [e_red[i - 2]] if i >= 2 else ()
            e_l = self.dma("YG"[i % 2], buf, wft[i], deps=dep)
            e_red.append(self.red(self.smallf[:, 8 + i:9 + i], buf, OP.add,
                                  deps=[e_l], absv=True))
        e_s = self.red(self.smallf[:, 16:17], self.smallf[:, 8:16], OP.add,
                       deps=[e_red[-1]])
        e_s1 = self.part_reduce(self.smallf[:, 16:17], 6, OP.add, e_s)
        e_d = self.ts(self.smallf[0:1, 7:8], self.smallf[0:1, 6:7],
                      0.7 / (D * D), None, OP.mult, deps=[e_s1])
        e_dc = self.bcast_col(7, 8, e_d)
        awb = p.alloc("awb", (128, D), F32)
        e_cnt = []
        e_s2 = []
        e_tt = None
        e_prev = e_dc
        for i in range(8):
            buf = self.scr[:, 0:D] if i % 2 == 0 else self.scr[:, D:2 * D]
            e_l = self.dma("YG"[i % 2], buf, wft[i], deps=[e_prev])
            e_aw = self.stt(awb[:, :], buf, -1.0, buf, OP.mult, OP.max,
                            deps=[e_l])
            e_m = self.ts(self.scrb[:, 0:D], awb[:, :], self.smallf[:, 8:9], None,
                          OP.is_gt, deps=[e_aw])
            e_c = self.red(self.smallf[:, 64 + i:65 + i], self.scrb[:, 0:D],
                           OP.add, deps=[e_m])
            e_p = self.tt(awb[:, :], awb[:, :], self.scrb[:, 0:D], OP.mult,
                          deps=[e_c])
            e_s2i = self.red(self.smallf[:, 72 + i:73 + i], awb[:, :], OP.add,
                             deps=[e_p])
            e_sg = self.act(buf, buf, AF.Sign, deps=[e_s2i])
            e_tt = self.tt(tT[:, D * i:D * (i + 1)], buf, self.scrb[:, 0:D],
                           OP.mult, deps=[e_sg])
            e_cnt.append(e_c)
            e_s2.append(e_s2i)
            e_prev = e_tt
        e_c2 = self.red(self.smallf[:, 17:18], self.smallf[:, 64:72], OP.add,
                        deps=[e_tt])
        e_cS = self.part_reduce(self.smallf[:, 17:18], 9, OP.add, e_c2)
        e_s3 = self.red(self.smallf[:, 18:19], self.smallf[:, 72:80], OP.add,
                        deps=[e_cS])
        e_sS = self.part_reduce(self.smallf[:, 18:19], 10, OP.add, e_s3)
        e_mx = self.ts(self.smallf[0:1, 11:12], self.smallf[0:1, 9:10], 1.0, None,
                       OP.max, deps=[e_sS])
        e_rc = self.recip(self.smallf[0:1, 12:13], self.smallf[0:1, 11:12],
                          deps=[e_mx])
        e_al = self.tt(self.smallf[0:1, 13:14], self.smallf[0:1, 10:11],
                       self.smallf[0:1, 12:13], OP.mult, deps=[e_rc])
        e_sa = self.tt(self.smallf[0:1, 14:15], self.smallf[0:1, 13:14],
                       self.smallf[0:1, 2:3], OP.mult, deps=[e_al, self.e_sf0])
        e_sac = self.bcast_col(14, 15, e_sa)
        p.free("awb")

        gcol = p.alloc("gcol", (128, 16), F32)
        e_g1 = self.ts(gcol[:, 0:8], self.fm8[:, 0:8], self.smallf[:, 4:5], None,
                       OP.mult, deps=[self.e_rf0])
        e_g2 = self.ts(gcol[:, 8:16], self.fm8[:, 8:16], self.smallf[:, 4:5],
                       None, OP.mult, deps=[e_g1])

        e_ev_prev = {0: None, 1: None}
        e_tp_prev = {0: None, 1: None}
        e_mm2 = {0: None, 1: None}
        e_rn_t = {}
        e_pk_prev = None
        e_sl = None

        def ints_of(t):
            return (self.scrb[:, 0:D] if t % 2 == 0
                    else self.scrb[:, 2048:2048 + D])

        for t in range(NT + 1):
            if t < NT:
                xt = self.xsb[:, D * t:D * (t + 1)]
                buf = self.scr[:, 0:D] if t % 2 == 0 else self.scr[:, D:2 * D]
                e1 = self.ts(buf, xt, self.stats[:, t:t + 1],
                             self.stats[:, 32 + t:33 + t], OP.subtract, OP.mult,
                             deps=[e_g2, e_tp_prev[t % 2]])
                pt = self.psA if t % 2 == 0 else self.psB
                e_tpl = None
                for i in range(8):
                    e_tpl = self.transp(pt[:, 128 * i:128 * (i + 1)],
                                        buf[:, 128 * i:128 * (i + 1)],
                                        self.idf[:, :],
                                        deps=[e1, e_ev_prev[t % 2]]
                                        if i == 0 else (), mark=(i == 7))
                e_tp_prev[t % 2] = e_tpl
                e_ev = None
                for i in range(8):
                    e_ev = self.act(buf[:, 128 * i:128 * (i + 1)],
                                    pt[:, 128 * i:128 * (i + 1)], AF.Identity,
                                    bias=gcol[:, 8 + i:9 + i],
                                    scale=gcol[:, i:i + 1],
                                    deps=[e_tpl])
                e_ev_prev[t % 2] = e_ev
                e_rn_t[t] = self.ts(ints_of(t), buf, MAGIC, MAGIC,
                                    OP.add, OP.subtract,
                                    deps=[e_ev, e_mm2[t % 2]])
            if t >= 1:
                u = t - 1
                ints = ints_of(u)
                ees = []
                for half in range(2):
                    bank = self.psC[:, 512 * half:512 * (half + 1)]
                    ee = None
                    for i in range(8):
                        ee = self.mm(bank, ints[:, 128 * i:128 * (i + 1)],
                                     tT[:, D * i + 512 * half:
                                        D * i + 512 * (half + 1)],
                                     start=(i == 0), stop=(i == 7),
                                     deps=[e_rn_t[u], e_pk_prev]
                                     if i == 0 else (), mark=(i == 7))
                    ees.append(ee)
                    e_sl = self.act(self.scrb[:, D + 512 * half:
                                              D + 512 * (half + 1)],
                                    bank, AF.Silu, scale=self.smallf[:, 15:16],
                                    deps=[ee, e_sac, e_pk_prev])
                e_mm2[u % 2] = ees
                e_pk_prev = self.dma("YG"[u % 2],
                                     d["park_g0"][128 * u:128 * (u + 1), :],
                                     self.scrb[:, D:2 * D], deps=[e_sl])
        self.e_g0 = self.p.dma_all("Y") + self.p.dma_all("G")
        p.free("gcol", "tT")

    def phase_parkx(self):
        d = self.d
        for t in range(NT):
            e = self.dma("YG"[t % 2], d["park_x"][128 * t:128 * (t + 1), :],
                         self.xsb[:, D * t:D * (t + 1)],
                         deps=[self.p.now("S"), self.p.now("V")])
        self.e_parkx = self.p.dma_all("Y") + self.p.dma_all("G")
        self.p.free("xsb")

    def phase_assemble(self):
        p, d = self.p, self.d
        self.kselT = p.alloc("kselT", (128, 8 * NSEL), BF16)
        self.vsel = p.alloc("vsel", (128, 10 * 1288), BF16)
        asw = p.alloc("asw", (128, 300), F32)
        asb = p.alloc("asb", (128, TPC), BF16)
        evs = []
        for r in range(2):
            for q in range(2):
                ncl = 2 * r + q
                for i in range(8):
                    evs.append(self.dma(
                        "Y",
                        self.kselT[:, NSEL * i + NBLK * ncl:
                                   NSEL * i + NBLK * ncl + 8 * TK],
                        d["cc_kT_o"][D * r + 128 * i:D * r + 128 * (i + 1),
                                     8 * TK * q:8 * TK * (q + 1)],
                        deps=[self.e_kv_ag[0]]))
        e_kvd = self.p.dma_all("Y")
        for i in range(8):
            for ncl in range(4):
                evs.append(self.copy(
                    self.kselT[:, NSEL * i + 1536 + 48 * ncl:
                               NSEL * i + 1536 + 48 * (ncl + 1)],
                    self.kselT[:, NSEL * i + NBLK * ncl + 256:
                               NSEL * i + NBLK * ncl + 304],
                    deps=[e_kvd] if (i == 0 and ncl == 0) else ()))
        for r in range(2):
            for q in range(2):
                ncl = 2 * r + q
                for pi, (p0, n) in enumerate(((0, 128), (128, 128))):
                    evs.append(self.dma(
                        "Y",
                        self.vsel[0:n, 1288 * (2 * ncl + pi):
                                  1288 * (2 * ncl + pi) + 1288],
                        d["cc_v_o"][608 * r + 304 * q + p0:
                                    608 * r + 304 * q + p0 + n, :],
                        deps=[self.e_kv_ag[1]]))
                evs.append(self.dma(
                    "Y",
                    self.vsel[48 * (ncl % 2):48 * (ncl % 2) + 48,
                              1288 * (8 + ncl // 2):1288 * (8 + ncl // 2) + 1288],
                    d["cc_v_o"][608 * r + 304 * q + 256:
                                608 * r + 304 * q + 304, :],
                    deps=[self.e_kv_ag[1]]))
        self.e_ksel = evs

        e_qm = []
        for i in range(8):
            e_sq = self.act(asb[:, 0:TPC], self.qT[:, TPC * i:TPC * (i + 1)],
                            AF.Square,
                            deps=[self.e_qT, e_qm[i - 1] if i else None])
            e_qm.append(self.red(self.smallf[:, 24 + i:25 + i],
                                 asb[:, 0:TPC], OP.max, deps=[e_sq]))
        e_t = self.transp(self.psD[0:8, 640:768], self.smallf[:, 24:32],
                          self.idf[:, :], deps=[e_qm[-1]], mark=True)
        e_c = self.copy(asw[0:8, 0:128], self.psD[0:8, 640:768], deps=[e_t])
        q3 = bass.AP(asw, 0, [[300, 8], [64, 2], [1, 64]])
        e_z1 = self.memset(asw[:, 258:259], 0.0, deps=[e_c])
        e_h1 = self.red(asw[0:8, 256:258], q3, OP.add, deps=[e_z1])
        e_h2 = self.red(asw[0:8, 258:259], asw[0:8, 256:258], OP.max,
                        deps=[e_h1])
        e_qg = self.part_reduce(asw[:, 258:259], 20, OP.max, e_h2)
        e_k1 = self.dma("Y", asw[0:H, 129:130], d["cc_kn_o"][0:H, :],
                        deps=[self.e_kn_ag])
        e_k2 = self.dma("Y", asw[0:H, 130:131], d["cc_kn_o"][H:2 * H, :],
                        deps=[self.e_kn_ag])
        e_z3 = self.memset(asw[:, 131:132], 0.0, deps=[self.e_kn_ag])
        e_k3 = self.tt(asw[0:H, 131:132], asw[0:H, 129:130],
                       asw[0:H, 130:131], OP.max, deps=[e_z3, e_k1, e_k2])
        e_kg = self.part_reduce(asw[:, 131:132], 21, OP.max, e_k3)
        e_q6 = self.tt(self.smallf[0:1, 22:23], self.smallf[0:1, 20:21],
                       self.smallf[0:1, 21:22], OP.mult, deps=[e_qg, e_kg])
        e_q7 = self.act(self.smallf[0:1, 20:21], self.smallf[0:1, 22:23], AF.Sqrt,
                        deps=[e_q6])
        self.e_negc = self.bcast_col(20, 21, e_q7, scale=-0.125)
        e_bias = None
        for idx in range(16):
            e_bias = self.stt(self.smallf[:, 40 + idx:41 + idx],
                              self.selmb[:, idx:idx + 1], -1e9,
                              self.smallf[:, 21:22], OP.mult, OP.add,
                              deps=[self.e_negc])
        ebl = [e_bias]
        for g in range(2):
            for P in range(2):
                c = 84 + 2 * g + P
                ebl.append(self.dma("Y", self.smallf[0:48, c:c + 1],
                                    self.smallf[0:48, 41 + 8 * g + 4 * P:
                                                42 + 8 * g + 4 * P],
                                    deps=[e_bias]))
                ebl.append(self.dma("Y", self.smallf[48:96, c:c + 1],
                                    self.smallf[0:48, 43 + 8 * g + 4 * P:
                                                44 + 8 * g + 4 * P],
                                    deps=[e_bias]))
        self.e_bias = ebl

    def phase_attention(self):
        p, d = self.p, self.d
        p.barrier()
        self.attnT = p.alloc("attnT", (128, 8 * TPC), BF16)
        pbuf = p.alloc("pbuf", (128, 6 * 1024), BF16)
        onesall = p.alloc("onesall", (128, 128), BF16)
        e_oa = self.memset(onesall[:, :], 1.0)
        rb = p.alloc("rbuf", (128, 1024), BF16)

        chunks = []
        for ncl in range(K_CL):
            chunks.append((NBLK * ncl, 2 * ncl, 128, None))
            chunks.append((NBLK * ncl + 128, 2 * ncl + 1, 128, ("m", 2 * ncl)))
        for P in range(2):
            chunks.append((1536 + 96 * P, 8 + P, 96, ("p", P)))
        NCH = len(chunks)

        deps0 = self.e_ksel + [self.e_bias, e_oa, self.e_qT]
        e_exp = {}
        e_avh = {}
        e_attn_ev = None
        e_pt0_prev = None
        e_rbm0_p = None
        e_rbm1_p = None
        e_rb = {0: None, 1: None}
        first = True
        for i in range(8):
            for g in range(2):
                e_av = {}
                avst = {"S": None, "prev": e_attn_ev}

                def av_emit(cc, i=i):
                    col0c, strip, nc_, _m = chunks[cc]
                    base0 = 1288 * strip + 161 * i
                    for sub in range(2):
                        pbi = (2 * cc + sub) % 6
                        pb = pbuf[:, 1024 * pbi:1024 * pbi + 1024]
                        e_ex = e_exp[(sub, cc)]
                        for qc in range(2):
                            dep = [e_ex]
                            if cc == 0:
                                dep.append(avst["prev"])
                            if sub == 0:
                                e_av[0] = self.mm(
                                    self.psC[0:65, 512 * qc:512 * (qc + 1)],
                                    self.vsel[0:nc_, base0:base0 + 65],
                                    pb[0:nc_, 512 * qc:512 * (qc + 1)],
                                    start=(cc == 0), stop=(cc == NCH - 1),
                                    deps=dep if qc == 0 else (), mark=(qc == 1))
                            else:
                                e_av[1] = self.mm(
                                    self.psD[64:128, 512 * qc:512 * (qc + 1)],
                                    self.vsel[0:nc_, base0 + 97:base0 + 161],
                                    pb[0:nc_, 512 * qc:512 * (qc + 1)],
                                    start=(cc == 0), stop=(cc == NCH - 1),
                                    deps=dep if qc == 0 else (), tp=(0, 64),
                                    mark=(qc == 1))
                                avst["S"] = self.mm(
                                    self.psD[0:1, 512 * qc:512 * (qc + 1)],
                                    self.vsel[0:nc_, base0 + 65:base0 + 66],
                                    pb[0:nc_, 512 * qc:512 * (qc + 1)],
                                    start=(cc == 0), stop=(cc == NCH - 1),
                                    deps=(), tp=(0, 0), mark=(qc == 1))
                        e_avh[pbi] = ([e_av[sub], avst["S"]] if sub
                                      else e_av[0])

                for c, (col0, strip, n, bspec) in enumerate(chunks):
                    pts = (self.psA, self.psB)
                    e_log = {}
                    for sub in range(2):
                        p0 = 64 * sub
                        ee = None
                        for qc in range(2):
                            dep = []
                            if first:
                                dep = list(deps0)
                            if (sub, (c - 1)) in e_exp:
                                dep.append(e_exp[(sub, c - 1)])
                            if c == 0:
                                dep.append(e_rb[sub])
                            ee = self.mm(
                                pts[sub][0:n, 512 * qc:512 * (qc + 1)],
                                self.kselT[p0:p0 + 64,
                                           NSEL * i + col0:NSEL * i + col0 + n],
                                self.qT[p0:p0 + 64,
                                        TPC * i + 1024 * g + 512 * qc:
                                        TPC * i + 1024 * g + 512 * (qc + 1)],
                                start=True, stop=True, deps=dep, mark=True)
                        first = False
                        e_log[sub] = ee
                    if c >= 1:
                        av_emit(c - 1)
                    if bspec is None:
                        bcol = self.smallf[:, 21:22]
                    elif bspec[0] == "m":
                        bcol = self.smallf[:, 40 + 8 * g + bspec[1]:
                                           41 + 8 * g + bspec[1]]
                    else:
                        bcol = self.smallf[:, 84 + 2 * g + bspec[1]:
                                           85 + 2 * g + bspec[1]]
                    for sub in range(2):
                        pbi = (2 * c + sub) % 6
                        pb = pbuf[:, 1024 * pbi:1024 * pbi + 1024]
                        e_ex = self.act(pb[0:n, :], pts[sub][0:n, 0:1024], AF.Exp,
                                        bias=bcol[0:n, :], scale=0.125,
                                        deps=[e_log[sub], e_avh.get(pbi)])
                        e_exp[(sub, c)] = e_ex
                av_emit(NCH - 1)
                e_avS = avst["S"]
                e_r0 = self.recip(rb[64:65, 0:1024], self.psC[64:65, 0:1024],
                                  deps=[e_av[0], e_rbm0_p], lp=True)
                e_r1 = self.recip(rb[0:1, 0:1024], self.psD[0:1, 0:1024],
                                  deps=[e_avS, e_av[1], e_rbm1_p], lp=True)
                e_rbm0 = None
                e_rbm1 = None
                for qc in range(2):
                    e_rbm0 = self.mm(self.psD[0:64, 512 * qc:512 * (qc + 1)],
                                     onesall[64:65, 0:64],
                                     rb[64:65, 512 * qc:512 * (qc + 1)],
                                     start=True, stop=True,
                                     deps=[e_r0, e_r1, e_pt0_prev]
                                     if qc == 0 else (),
                                     mark=(qc == 1))
                    e_rbm1 = self.mm(self.psC[64:128, 512 * qc:512 * (qc + 1)],
                                     onesall[0:1, 0:64],
                                     rb[0:1, 512 * qc:512 * (qc + 1)],
                                     start=True, stop=True,
                                     deps=[e_r0, e_attn_ev] if qc == 0 else (),
                                     tp=(0, 64), mark=(qc == 1))
                e_rbm0_p = e_rbm0
                e_rbm1_p = e_rbm1
                e_rb[0] = e_exp[(0, NCH - 1)]
                e_rb[1] = e_exp[(1, NCH - 1)]
                e_c0 = self.copy(rb[0:64, 0:1024], self.psD[0:64, 0:1024],
                                 deps=[e_rbm0, e_rbm1])
                e_c1 = self.copy(rb[64:128, 0:1024], self.psC[64:128, 0:1024],
                                 deps=[e_rbm1, e_rbm0])
                e_t0 = self.tt(self.attnT[0:64,
                                          TPC * i + 1024 * g:TPC * i + 1024 * (g + 1)],
                               self.psC[0:64, 0:1024], rb[0:64, 0:1024],
                               OP.mult, deps=[e_c0])
                e_pt0_prev = e_t0
                e_attn_ev = self.tt(
                    self.attnT[64:128,
                               TPC * i + 1024 * g:TPC * i + 1024 * (g + 1)],
                    self.psD[64:128, 0:1024], rb[64:128, 0:1024],
                    OP.mult, deps=[e_c1, e_t0])
                e_exp = {}
        self.e_attn = e_attn_ev
        self.e_attn = e_attn_ev
        p.free("rbuf", "onesall", "pbuf")

    def phase_proj(self):
        p, d = self.p, self.d
        p.barrier()
        wps = p.alloc("wps", (128, 8 * D), BF16)
        bpb = p.alloc("bpb", (128, D), F32)
        e_bp = self.dma("Y", bpb[:, :], d["bproj_bc"][:, :])
        wpt = d["wproj"].ap().rearrange("(n p) m -> n p m", p=128)
        for i in range(8):
            self.dma("G", wps[:, D * i:D * (i + 1)], wpt[i])
        e_wp = self.p.dma_all("G")
        prx = p.alloc("prx", (128, 2048), F32)
        e_bk = {0: None, 1: None}
        e_tr = {}
        e_gr = {}
        e_pw = {}
        e_xd = {}
        for t in range(NT):
            o = 1024 * (t % 2)
            e_xl = self.dma("Y", prx[:, o:o + D],
                            d["park_x"][128 * t:128 * (t + 1), :],
                            deps=[self.e_parkx, e_pw.get(t % 2)])
            e_gl = self.dma("G", self.scrb[:, o:o + D],
                            d["park_g0"][128 * t:128 * (t + 1), :],
                            deps=[self.e_g0, e_gr.get((t % 2, 1))])
            e_x1 = {}
            for half in range(2):
                bank = self.psA[:, 0:512] if half == 0 else self.psA[:, 512:1024]
                ee = None
                for j in range(8):
                    ee = self.mm(bank,
                                 self.attnT[:, TPC * j + 128 * t:TPC * j + 128 * (t + 1)],
                                 wps[:, D * j + 512 * half:D * j + 512 * (half + 1)],
                                 start=(j == 0), stop=(j == 7),
                                 deps=[self.e_attn, e_wp, e_bp, e_bk[half]]
                                 if j == 0 else (), mark=(j == 7))
                tmp = self.scr[:, o + 512 * half:o + 512 * (half + 1)]
                e_s0 = self.stt(tmp, bank, 1.0,
                                bpb[:, 512 * half:512 * (half + 1)],
                                OP.mult, OP.add,
                                deps=[ee, e_tr.get((t % 2, half))])
                e_bk[half] = e_s0
                e_t2 = self.tt(tmp, tmp,
                               self.scrb[:, o + 512 * half:o + 512 * (half + 1)],
                               OP.mult, deps=[e_s0, e_gl])
                e_gr[(t % 2, half)] = e_t2
                e_x1[half] = self.tt(prx[:, o + 512 * half:o + 512 * (half + 1)],
                                     prx[:, o + 512 * half:o + 512 * (half + 1)],
                                     tmp, OP.add, deps=[e_t2, e_xl])
                e_tr[(t % 2, half)] = e_x1[half]
            e_pw[t % 2] = self.dma("YS"[t % 2],
                                   d["park_x"][128 * t:128 * (t + 1), :],
                                   prx[:, o:o + D], deps=[e_x1[0], e_x1[1]])
        self.e_x1w = self.p.dma_all("Y") + self.p.dma_all("S")
        p.free("prx", "bpb", "wps")

    def phase_l1_mixer(self, e_r1):
        p, d = self.p, self.d
        e_a1 = self.act(self.smallf[:, 24:32], self.fm8[:, 16:24], AF.Exp,
                        deps=[e_r1])
        e_a2 = self.act(self.smallf[:, 24:32], self.smallf[:, 24:32], AF.Ln,
                        bias=1.0, deps=[e_a1])
        e_a = self.ts(self.smallf[:, 24:32], self.smallf[:, 24:32], -1.0, None,
                      OP.mult, deps=[e_a2])

        self.phase_xnT(e_r1)
        p.barrier()
        self.xsF = p.alloc("xsF", (128, 8 * TPC), BF16)
        zf32 = self.zbf
        e_z = self.memset(zf32[:, :], 0.0)
        abc = p.alloc("abc", (128, 2048), F32)
        hmid = p.alloc("hmid", (128, 8), F32)
        bmsb = p.alloc("bmsb", (128, 8 * D), BF16)
        bmt = d["bmatT"].ap().rearrange("(n p) m -> n p m", p=128)
        for i in range(8):
            self.dma("G", bmsb[:, D * i:D * (i + 1)], bmt[i])
        e_bm = self.p.dma_all("G")

        e_am = None
        for t in range(NT):
            buf = self.scr[:, 0:D] if t % 2 == 0 else self.scr[:, D:2 * D]
            e1 = self.ts(buf, self.xsb[:, D * t:D * (t + 1)],
                         self.stats[:, t:t + 1], self.stats[:, 16 + t:17 + t],
                         OP.subtract, OP.mult, deps=[e_r1, e_am])
            e_am = self.red(self.smallf[:, 40 + t:41 + t], buf, OP.max,
                            deps=[e1], absv=True)
        e_am2 = self.red(self.smallf[:, 16:17], self.smallf[:, 40:56], OP.max,
                         deps=[e_am])
        e_am3 = self.part_reduce(self.smallf[:, 16:17], 0, OP.max, e_am2)
        e_ar = self.ar_scalar("sx", 0, 1, OP.max, e_am3)

        e_scan = {}
        e_hl = None
        for nch in range(8):
            e_bu = None
            for tc in range(4):
                pt = self.psA if tc % 2 == 0 else self.psB
                bank = pt[:, 0:512] if tc < 2 else pt[:, 512:1024]
                ee = None
                for i in range(8):
                    ee = self.mm(bank,
                                 bmsb[:, D * i + 128 * nch:D * i + 128 * (nch + 1)],
                                 self.xnT[:, TPC * i + 512 * tc:TPC * i + 512 * (tc + 1)],
                                 start=(i == 0), stop=(i == 7),
                                 deps=[self.e_xnT, e_bm, e_scan.get(nch - 1)]
                                 if i == 0 else (), mark=(i == 7))
                e_bu = self.copy(self.scr[:, 512 * tc:512 * (tc + 1)], bank,
                                 deps=[ee])
            e_ab = self.ts(abc[:, :], zf32[:, :], self.smallf[:, 24 + nch:25 + nch],
                           None, OP.add, deps=[e_z, e_a, e_bu])
            e_sc1 = self.scan(self.xsF[:, TPC * nch:TPC * (nch + 1)], abc[:, :],
                              self.scr[:, 0:2048], self.fm8[:, 24 + nch:25 + nch],
                              deps=[e_ab])
            e_scan[nch] = e_sc1
            e_hl = self.copy(hmid[:, nch:nch + 1],
                             self.xsF[:, TPC * nch + 2047:TPC * nch + 2048],
                             deps=[e_sc1])
        e_hd = self.dma("G", d["cc_hm_i"][:, :], hmid[:, :], deps=[e_hl])
        e_hc = self.coll("AllGather", OP.bypass, [d["cc_hm_i"].ap().opt()],
                         [d["cc_hm_o"].ap().opt()], PAIRS, deps=[e_hd])
        e_hr = self.dma("G", hmid[:, :], d["cc_hm_o"][0:128, :], deps=[e_hc])
        e_hm = self.tt(hmid[:, :], hmid[:, :], self.fm8[:, 32:40], OP.mult,
                       deps=[e_hr])
        for nch in range(8):
            e_ab = self.ts(abc[:, :], zf32[:, :], self.smallf[:, 24 + nch:25 + nch],
                           None, OP.add, deps=[e_hm])
            e_c1 = self.scan(self.scr[:, 0:2048], abc[:, :], zf32[:, :],
                             hmid[:, nch:nch + 1], deps=[e_ab])
            e_scan[nch] = self.tt(self.xsF[:, TPC * nch:TPC * (nch + 1)],
                                  self.xsF[:, TPC * nch:TPC * (nch + 1)],
                                  self.scr[:, 0:2048], OP.add, deps=[e_c1])
        self.e_xs = e_scan[7]
        e_s = self.ts(self.smallf[0:1, 2:3], self.smallf[0:1, 1:2], 1.0 / QMAX1,
                      1e-12, OP.mult, OP.max, deps=[e_ar])
        e_r = self.recip(self.smallf[0:1, 3:4], self.smallf[0:1, 2:3], deps=[e_s])
        e_rc = self.bcast_col(3, 4, e_r)
        self.e_sxcols = self.bcast_col(2, 5, e_rc)
        p.free("bmsb", "hmid", "abc")

    def phase_l1_cd_setup(self):
        p, d = self.p, self.d
        cmsb = p.alloc("cmsb", (128, 8 * D), BF16)
        dmsb = p.alloc("dmsb", (128, 8 * D), BF16)
        lnb = p.alloc("lnb", (128, 2 * D), F32)
        cmt = d["cmatT"].ap().rearrange("(n p) m -> n p m", p=128)
        dmt = d["dmatT"].ap().rearrange("(n p) m -> n p m", p=128)
        for i in range(8):
            self.dma("G", cmsb[:, D * i:D * (i + 1)], cmt[i])
            self.dma("G", dmsb[:, D * i:D * (i + 1)], dmt[i])
        self.cd = dict(cmsb=cmsb, dmsb=dmsb, lnb=lnb,
                       e_wc=self.p.dma_all("G"),
                       e_lg=self.dma("Y", lnb[:, 0:D], d["lng_bc"][:, :]),
                       e_lb=self.dma("Y", lnb[:, D:2 * D], d["lnb_bc"][:, :]),
                       e_prev={0: None, 1: None})

    def phase_l1_cd(self, t0, t1):
        p, d = self.p, self.d
        cmsb, dmsb, lnb = self.cd["cmsb"], self.cd["dmsb"], self.cd["lnb"]
        e_wc, e_lg, e_lb = self.cd["e_wc"], self.cd["e_lg"], self.cd["e_lb"]
        e_prev = self.cd["e_prev"]
        e_xs = self.e_xs
        for t in range(t0, t1):
            for half in range(2):
                bank = self.psC[:, 0:512] if half == 0 else self.psC[:, 512:1024]
                ee = None
                for i in range(8):
                    ee = self.mm(bank,
                                 self.xsF[:, TPC * i + 128 * t:TPC * i + 128 * (t + 1)],
                                 cmsb[:, D * i + 512 * half:D * i + 512 * (half + 1)],
                                 start=(i == 0), stop=False,
                                 deps=[e_xs, e_wc, e_prev[half]] if i == 0 else (),
                                 mark=False)
                for i in range(8):
                    ee = self.mm(bank,
                                 self.xnT[:, TPC * i + 128 * t:TPC * i + 128 * (t + 1)],
                                 dmsb[:, D * i + 512 * half:D * i + 512 * (half + 1)],
                                 start=False, stop=(i == 7),
                                 deps=(), mark=(i == 7))
                e_prev[half] = self.act(self.scr[:, 512 * half:512 * (half + 1)],
                                        bank, AF.Gelu, deps=[ee])
            e_m = self.red(self.smallf[:, 40:41], self.scr[:, 0:D], OP.add,
                           deps=[e_prev[0], e_prev[1]])
            e_m2 = self.ts(self.smallf[:, 40:41], self.smallf[:, 40:41], 1.0 / D,
                           None, OP.mult, deps=[e_m])
            e_sq = self.act(self.scrb[:, 0:D], self.scr[:, 0:D], AF.Square,
                            deps=[e_m2])
            e_q = self.red(self.smallf[:, 41:42], self.scrb[:, 0:D], OP.add,
                           deps=[e_sq])
            e_q2 = self.ts(self.smallf[:, 41:42], self.smallf[:, 41:42], 1.0 / D,
                           None, OP.mult, deps=[e_q])
            e_v1 = self.tt(self.smallf[:, 42:43], self.smallf[:, 40:41],
                           self.smallf[:, 40:41], OP.mult, deps=[e_q2])
            e_v2 = self.tt(self.smallf[:, 42:43], self.smallf[:, 41:42],
                           self.smallf[:, 42:43], OP.subtract, deps=[e_v1])
            e_sd = self.act(self.smallf[:, 43:44], self.smallf[:, 42:43], AF.Sqrt,
                            bias=LN_EPS, deps=[e_v2])
            e_rr = self.recip(self.smallf[:, 44:45], self.smallf[:, 43:44],
                              deps=[e_sd])
            e_n1 = self.ts(self.scr[:, 0:D], self.scr[:, 0:D],
                           self.smallf[:, 40:41], self.smallf[:, 44:45],
                           OP.subtract, OP.mult, deps=[e_rr])
            e_n2 = self.tt(self.scr[:, 0:D], self.scr[:, 0:D], lnb[:, 0:D],
                           OP.mult, deps=[e_n1, e_lg])
            e_n3 = self.tt(self.scr[:, 0:D], self.scr[:, 0:D], lnb[:, D:2 * D],
                           OP.add, deps=[e_n2, e_lb])
            e_pk = self.dma("YG"[t % 2], d["park_s"][128 * t:128 * (t + 1), :],
                            self.scr[:, 0:D], deps=[e_n3])
            e_prev[0] = e_pk
            e_prev[1] = e_pk
        if t1 == NT:
            self.e_s_parked = self.p.dma_all("Y") + self.p.dma_all("G")

    def phase_xq(self):
        p = self.p
        self.xqT = p.alloc("xqT", (128, 8 * TPC), BF16)
        e_tp = {}
        for t in range(NT):
            xt = self.xsb[:, D * t:D * (t + 1)]
            buf = self.scr[:, 0:D] if (t % 2 == 0) else self.scr[:, D:2 * D]
            dep = [self.e_r1b]
            if t >= 2:
                dep.append(e_tp[t - 2]["tp"])
            e1 = self.ts(buf, xt, self.stats[:, t:t + 1],
                         self.stats[:, 16 + t:17 + t], OP.subtract, OP.mult,
                         deps=dep)
            pt = self.psA if (t % 2 == 0) else self.psB
            e_tpl = None
            for i in range(8):
                dep2 = [e1]
                if t >= 2:
                    dep2.append(e_tp[t - 2]["ev"])
                e_tpl = self.transp(pt[:, 128 * i:128 * (i + 1)],
                                    buf[:, 128 * i:128 * (i + 1)],
                                    self.idf[:, :], deps=dep2 if i == 0 else (),
                                    mark=(i == 7))
            e_ev = None
            for i in range(8):
                e_ev = self.act(buf[:, 128 * i:128 * (i + 1)],
                                pt[:, 128 * i:128 * (i + 1)], AF.Identity,
                                bias=MAGIC, scale=self.smallf[:, 4:5],
                                deps=[e_tpl, self.e_sxcols])
            e_rn = None
            for i in range(8):
                e_rn = self.ts(self.xqT[:, TPC * i + 128 * t:TPC * i + 128 * (t + 1)],
                               buf[:, 128 * i:128 * (i + 1)], MAGIC, None,
                               OP.subtract, deps=[e_ev])
            e_tp[t] = {"tp": e_tpl, "ev": e_rn}
        self.e_xqT = self.p.now("V")
        self.p.free("xsb2")

    def phase_l1_p1(self):
        p, d = self.p, self.d
        self.l1w = p.alloc("l1w", (128, 8192), F32)
        l1w = self.l1w
        wq = p.alloc("wqT", (128, 8 * DI), BF16)
        self.wqT = wq
        e_amax = {}
        e_parked = {0: None, 1: None}
        e_pv = {}
        ccw, park = d["cc_wu_o"], d["park_up"]
        for i in range(8):
            self.dmat("Y", wq[:, DI * i:DI * (i + 1)],
                      ccw[:, 128 * i:128 * (i + 1)],
                      deps=[self.e_wq_ag["wu"]])
        e_w = self.p.dma_all("Y")
        e_am = self.memset(self.smallf[:, 46:47], 0.0)
        for R in range(24):
            buf = self.scr[:, 0:2048] if R % 2 == 0 else l1w[:, 0:2048]
            ee4 = [None] * 4
            for i in range(8):
                for tc in range(4):
                    pt = (self.psA, self.psB, self.psC, self.psD)[tc]
                    bank = pt[:, 0:512]
                    ee4[tc] = self.mm(
                        bank,
                        wq[:, DI * i + 128 * R:DI * i + 128 * (R + 1)],
                        self.xqT[:, TPC * i + 512 * tc:TPC * i + 512 * (tc + 1)],
                        start=(i == 0), stop=(i == 7),
                        deps=[self.e_xqT, e_w, e_pv.get(tc)]
                        if i == 0 else (), mark=(i == 7))
            for tc in range(4):
                pt = (self.psA, self.psB, self.psC, self.psD)[tc]
                e_c = self.act(buf[:, 512 * tc:512 * (tc + 1)],
                               pt[:, 0:512], AF.Identity,
                               deps=[ee4[tc], e_parked[R % 2]])
                e_pv[tc] = e_c
            e_r = self.red(self.smallf[:, 48:49], buf, OP.max,
                           deps=[e_pv[0], e_pv[1], e_pv[2], e_pv[3]], absv=True)
            e_mx = self.tt(self.smallf[:, 46:47],
                           self.smallf[:, 46:47],
                           self.smallf[:, 48:49], OP.max, deps=[e_r, e_am])
            e_pk = self.dma("YG"[R % 2], park[128 * R:128 * (R + 1), :],
                            buf, deps=[e_mx])
            e_parked[R % 2] = e_pk
            e_am = e_mx
        e_amax[0] = e_am
        e_amax["park0"] = e_pk
        e_up_last = list(ee4)
        for i in range(8):
            self.dmat("Y", wq[:, DI * i:DI * (i + 1)],
                      d["cc_wg_o"][:, 128 * i:128 * (i + 1)],
                      deps=[self.e_wq_ag["wg"], e_up_last])
        e_w = self.p.dma_all("Y")
        self.ff = dict(e_amax=e_amax, e_wg=e_w,
                       e_up_park=[e_parked[0], e_parked[1]])
        e_am = self.memset(self.smallf[:, 47:48], 0.0)
        e_acc = {}
        for R in range(24):
            ee4 = [None] * 4
            for i in range(8):
                for tc in range(4):
                    pt = (self.psA, self.psB, self.psC, self.psD)[tc]
                    bank = pt[:, 0:512]
                    ee4[tc] = self.mm(
                        bank,
                        wq[:, DI * i + 128 * R:DI * i + 128 * (R + 1)],
                        self.xqT[:, TPC * i + 512 * tc:TPC * i + 512 * (tc + 1)],
                        start=(i == 0), stop=(i == 7),
                        deps=[e_w, e_pv.get(tc)]
                        if i == 0 else (), mark=(i == 7))
            for tc in range(4):
                pt = (self.psA, self.psB, self.psC, self.psD)[tc]
                e_rr = self.red(self.smallf[:, 41 + tc:42 + tc], pt[:, 0:512],
                                OP.max, deps=[ee4[tc], e_acc.get(tc)], absv=True)
                e_pv[tc] = e_rr
                e_acc[tc] = self.tt(self.smallf[:, 47:48], self.smallf[:, 47:48],
                                    self.smallf[:, 41 + tc:42 + tc], OP.max,
                                    deps=[e_rr, e_am])
                e_am = e_acc[tc]
        e_amax[1] = e_am
        self.ff["e_pv_p1"] = dict(e_pv)
        self.ff["e_amax"] = e_amax

    def phase_l1_wqfree(self):
        self.p.free("wqT")

    def phase_l1_ugar(self):
        d, e_amax = self.d, self.ff["e_amax"]
        e_u1 = self.part_reduce(self.smallf[:, 46:47], 24, OP.max, e_amax[0])
        e_g1 = self.part_reduce(self.smallf[:, 47:48], 25, OP.max, e_amax[1])
        e_p0 = self.copy(self.smallf[0:1, 56:58], self.smallf[0:1, 24:26],
                         deps=[e_u1, e_g1])
        e_p1 = self.dma("G", d["cc_ug_i"][0:1, 0:8], self.smallf[0:1, 56:64],
                        deps=[e_p0])
        e_c1 = self.coll("AllReduce", OP.max, [d["cc_ug_i"].ap().opt()],
                         [d["cc_ug_o"].ap().opt()], ALL8, deps=[e_p1])
        self.ff["e_ug_r"] = self.dma("G", self.smallf[0:1, 26:28],
                                     d["cc_ug_o"][0:1, 0:2], deps=[e_c1])

    def phase_l1_p2(self):
        p, d = self.p, self.d
        l1w, e_amax = self.l1w, self.ff["e_amax"]
        e_f1 = self.ts(self.smallf[0:1, 26:27], self.smallf[0:1, 26:27],
                       self.smallf[0:1, 2:3], None, OP.mult,
                       deps=[self.ff["e_ug_r"]])
        e_f2 = self.ts(self.smallf[0:1, 27:28], self.smallf[0:1, 27:28],
                       self.smallf[0:1, 2:3], None, OP.mult, deps=[e_f1])
        e_su = self.ts(self.smallf[0:1, 28:29], self.smallf[0:1, 26:27],
                       1.0 / QMAX1, 1e-12, OP.mult, OP.max, deps=[e_f2])
        e_sg = self.ts(self.smallf[0:1, 30:31], self.smallf[0:1, 27:28],
                       1.0 / QMAX1, 1e-12, OP.mult, OP.max, deps=[e_su])
        e_ru = self.recip(self.smallf[0:1, 29:30], self.smallf[0:1, 28:29],
                          deps=[e_sg])
        e_rg = self.recip(self.smallf[0:1, 31:32], self.smallf[0:1, 30:31],
                          deps=[e_ru])
        e_ru2 = self.ts(self.smallf[0:1, 29:30], self.smallf[0:1, 29:30],
                        self.smallf[0:1, 2:3], None, OP.mult, deps=[e_rg])
        e_rg2 = self.ts(self.smallf[0:1, 31:32], self.smallf[0:1, 31:32],
                        self.smallf[0:1, 2:3], None, OP.mult, deps=[e_ru2])
        self.ff["e_ruc"] = self.bcast_col(29, 33, e_rg2)
        e_rgc = self.bcast_col(31, 34, self.ff["e_ruc"])
        e_sgc = self.bcast_col(30, 35, e_rgc)

        e_am0 = self.memset(self.smallf[:, 49:50], 0.0, deps=[e_rgc])
        st = {"am": e_am0, "pv": {0: self.ff["e_up_park"],
                                  1: self.ff["e_up_park"]},
              "t2": {0: None, 1: None}, "mm": {}, "ev": {}, "h": {}}
        st["ev"][-1] = self.ff["e_pv_p1"]
        wq = self.wqT
        banks = (self.psA, self.psB, self.psC, self.psD)

        def p2_slots(R):
            return (l1w[:, 2048 * (R % 2):2048 * (R % 2) + 2048],
                    l1w[:, 4096 + 2048 * (R % 2):4096 + 2048 * (R % 2) + 2048])

        def p2_mm(R):
            ee4 = [None] * 4
            for i in range(8):
                for tc in range(4):
                    ee4[tc] = self.mm(
                        banks[tc][:, 0:512],
                        wq[:, DI * i + 128 * R:DI * i + 128 * (R + 1)],
                        self.xqT[:, TPC * i + 512 * tc:TPC * i + 512 * (tc + 1)],
                        start=(i == 0), stop=(i == 7),
                        deps=[st["ev"][R - 1][tc]] if i == 0 else (),
                        mark=(i == 7))
            st["mm"][R] = ee4

        def p2_head(R):
            gi, t2 = p2_slots(R)
            evs = []
            for tc in range(4):
                evs.append(self.act(gi[:, 512 * tc:512 * (tc + 1)],
                                    banks[tc][:, 0:512],
                                    AF.Identity, bias=MAGIC,
                                    scale=self.smallf[:, 34:35],
                                    deps=[st["mm"][R][tc], e_sgc,
                                          st["pv"][R % 2]]))
            st["ev"][R] = evs
            e_2 = self.ts(gi, gi, MAGIC, None, OP.subtract, deps=[evs[3]])
            e_3 = self.act(t2, gi, AF.Sigmoid,
                           scale=self.smallf[:, 35:36],
                           deps=[e_2, st["t2"][R % 2]])
            st["h"][R] = self.act(t2, t2,
                                  AF.Identity, bias=MAGIC, scale=65536.0,
                                  deps=[e_3])

        def p2_tail(R):
            gi, t2 = p2_slots(R)
            e_5 = self.ts(t2, t2, MAGIC,
                          65535.0, OP.subtract, OP.min, deps=[st["h"][R]])
            e_6 = self.tt(gi, gi, t2, OP.mult, deps=[e_5])
            st["t2"][R % 2] = e_6
            e_7 = self.red(self.smallf[:, 48:49], gi, OP.max,
                           deps=[e_6], absv=True)
            e_8 = self.tt(self.smallf[:, 49:50], self.smallf[:, 49:50],
                          self.smallf[:, 48:49], OP.max, deps=[e_7, st["am"]])
            st["am"] = e_8
            st["pv"][R % 2] = self.dma("GY"[R % 2],
                                       d["park_g2"][128 * R:128 * (R + 1), :],
                                       gi, deps=[e_8])

        p2_mm(0)
        for R in range(24):
            if R >= 1:
                p2_tail(R - 1)
            p2_head(R)
            if R + 1 < 24:
                p2_mm(R + 1)
        p2_tail(23)
        self.ff["e_g2park"] = [st["pv"][0], st["pv"][1]]
        self.ff["e_t2f"] = [st["t2"][0], st["t2"][1]]
        self.ff["e_g2am"] = st["am"]

    def phase_l1_g2ar(self):
        e_g2m = self.part_reduce(self.smallf[:, 49:50], 26, OP.max,
                                 self.ff["e_g2am"])
        self.ff["e_g2r"] = self.ar_scalar("g2", 26, 27, OP.max, e_g2m)

    def phase_l1_p3(self):
        p, d = self.p, self.d
        l1w, e_amax = self.l1w, self.ff["e_amax"]
        e_ruc = self.ff["e_ruc"]
        e_c2 = self.tt(self.smallf[0:1, 36:37], self.smallf[0:1, 27:28],
                       self.smallf[0:1, 30:31], OP.mult, deps=[self.ff["e_g2r"]])
        e_c3 = self.ts(self.smallf[0:1, 36:37], self.smallf[0:1, 36:37],
                       2.0 ** -16 / QMAX1, 1e-12, OP.mult, OP.max, deps=[e_c2])
        e_c4 = self.recip(self.smallf[0:1, 37:38], self.smallf[0:1, 36:37],
                          deps=[e_c3])
        e_c5 = self.tt(self.smallf[0:1, 38:39], self.smallf[0:1, 37:38],
                       self.smallf[0:1, 30:31], OP.mult, deps=[e_c4])
        e_c6 = self.ts(self.smallf[0:1, 38:39], self.smallf[0:1, 38:39],
                       2.0 ** -16, None, OP.mult, deps=[e_c5])
        e_cc = self.bcast_col(38, 39, e_c6)

        e_am0 = self.memset(self.smallf[:, 50:51], 0.0, deps=[e_cc])
        e_pv2 = e_amax["park0"]
        st = {"am": e_am0, "pv": {0: self.ff["e_g2park"], 1: self.ff["e_g2park"]},
              "ui": {0: self.ff["e_t2f"][0], 1: self.ff["e_t2f"][1]},
              "l1": {}, "l2": {}, "h": {}}

        def p3_slots(R):
            return (l1w[:, 2048 * (R % 2):2048 * (R % 2) + 2048],
                    l1w[:, 4096 + 2048 * (R % 2):4096 + 2048 * (R % 2) + 2048])

        def p3_load(R):
            gi, ui = p3_slots(R)
            st["l1"][R] = self.dma("YG"[R % 2], gi,
                                   d["park_g2"][128 * R:128 * (R + 1), :],
                                   deps=[st["pv"][R % 2]])
            st["l2"][R] = self.dma("S", ui,
                                   d["park_up"][128 * R:128 * (R + 1), :],
                                   deps=[st["ui"][R % 2], e_pv2])

        def p3_head(R):
            gi, ui = p3_slots(R)
            e_1 = self.act(gi, gi,
                           AF.Identity, bias=MAGIC, scale=self.smallf[:, 39:40],
                           deps=[st["l1"][R], e_cc])
            e_3 = self.act(ui, ui,
                           AF.Identity, bias=MAGIC, scale=self.smallf[:, 33:34],
                           deps=[st["l2"][R], e_ruc])
            e_2 = self.ts(gi, gi, MAGIC, None, OP.subtract, deps=[e_1])
            e_4 = self.ts(ui, ui, MAGIC, None, OP.subtract, deps=[e_3])
            st["h"][R] = (e_2, e_4)

        def p3_tail(R):
            gi, ui = p3_slots(R)
            e_5 = self.tt(gi, gi, ui, OP.mult, deps=list(st["h"][R]))
            st["ui"][R % 2] = e_5
            e_6 = self.red(self.smallf[:, 48:49], gi, OP.max,
                           deps=[e_5], absv=True)
            e_7 = self.tt(self.smallf[:, 50:51], self.smallf[:, 50:51],
                          self.smallf[:, 48:49], OP.max, deps=[e_6, st["am"]])
            st["am"] = e_7
            st["pv"][R % 2] = self.dma("GY"[R % 2],
                                       d["park_op"][128 * R:128 * (R + 1), :],
                                       gi, deps=[e_7])

        p3_load(0)
        p3_load(1)
        for R in range(24):
            if R >= 1:
                p3_tail(R - 1)
                if R + 1 < 24:
                    p3_load(R + 1)
            p3_head(R)
        p3_tail(23)
        self.ff["e_oppark"] = [st["pv"][0], st["pv"][1]]
        e_om = self.part_reduce(self.smallf[:, 50:51], 80, OP.max, st["am"])
        self.ff["e_oor"] = self.ar_scalar("oo", 80, 81, OP.max, e_om)

    def phase_l1_p4(self):
        p, d = self.p, self.d
        e_oppark = self.ff["e_oppark"]
        p.free("lnb", "dmsb", "cmsb", "l1w", "xqT", "xsF", "xnT")
        wdq = p.alloc("wdT", (128, 24 * D), BF16)
        for j in range(24):
            self.dmat("S", wdq[:, D * j:D * (j + 1)],
                      d["cc_wd_o"][:, 128 * j:128 * (j + 1)],
                      deps=[self.e_wq_ag["wd"]])
        e_wd = self.p.dma_all("S")
        oiF = p.alloc("oiF", (128, 2 * 24 * 1024), BF16)
        ois = p.alloc("oistg", (128, 2048), F32)
        mrg = p.alloc("mrg", (128, 2048), F32)
        e_o1 = self.tt(self.smallf[0:1, 42:43], self.smallf[0:1, 81:82],
                       self.smallf[0:1, 36:37], OP.mult, deps=[self.ff["e_oor"]])
        e_o2 = self.tt(self.smallf[0:1, 42:43], self.smallf[0:1, 42:43],
                       self.smallf[0:1, 28:29], OP.mult, deps=[e_o1])
        e_o3 = self.ts(self.smallf[0:1, 42:43], self.smallf[0:1, 42:43],
                       1.0 / QMAX1, 1e-12, OP.mult, OP.max, deps=[e_o2])
        e_o4 = self.recip(self.smallf[0:1, 43:44], self.smallf[0:1, 42:43],
                          deps=[e_o3])
        e_o5 = self.tt(self.smallf[0:1, 44:45], self.smallf[0:1, 43:44],
                       self.smallf[0:1, 36:37], OP.mult, deps=[e_o4])
        e_o6 = self.tt(self.smallf[0:1, 44:45], self.smallf[0:1, 44:45],
                       self.smallf[0:1, 28:29], OP.mult, deps=[e_o5])
        e_roc = self.bcast_col(44, 51, e_o6)
        e_soc = self.bcast_col(42, 52, e_roc)

        e_bk = {0: None, 1: None}
        e_ymrg = {0: None, 1: None}
        e_sb = {}
        e_pr = {}
        for hf in range(2):
            of = 24576 * hf
            e_oi = None
            e_oid = {0: None, 1: None}
            e_ld = {}
            e_h = {}

            def oi_load(R, hf=hf, of=of):
                e_ld[R] = self.dma("YG"[R % 2],
                                   ois[:, 1024 * (R % 2):1024 * (R % 2) + 1024],
                                   d["park_op"][128 * R:128 * (R + 1),
                                                1024 * hf:1024 * (hf + 1)],
                                   deps=[e_oid[R % 2], e_oppark])

            oi_load(0)
            oi_load(1)
            for R in range(24):
                o0 = 1024 * (R % 2)
                if R >= 1:
                    op = 1024 * ((R - 1) % 2)
                    e_oi = self.ts(oiF[:, of + 1024 * (R - 1):of + 1024 * R],
                                   ois[:, op:op + 1024], MAGIC, None,
                                   OP.subtract, deps=[e_h[R - 1]])
                    e_oid[(R - 1) % 2] = e_oi
                    if R + 1 < 24:
                        oi_load(R + 1)
                e_h[R] = self.act(ois[:, o0:o0 + 1024],
                                  ois[:, o0:o0 + 1024],
                                  AF.Identity, bias=MAGIC,
                                  scale=self.smallf[:, 51:52],
                                  deps=[e_ld[R], e_roc])
            e_oi = self.ts(oiF[:, of + 1024 * 23:of + 1024 * 24],
                           ois[:, 1024:2048], MAGIC, None,
                           OP.subtract, deps=[e_h[23]])
            e_oid[1] = e_oi
            for tt_ in range(8):
                t = 8 * hf + tt_
                xb = mrg[:, 1024 * (t % 2):1024 * (t % 2) + 1024]
                e_xl = self.dma("S", xb, d["park_x"][128 * t:128 * (t + 1), :],
                                deps=[e_ymrg[t % 2], self.e_x1w])
                for half in range(2):
                    sb = self.scrb[:, 1024 * (t % 2) + 512 * half:
                                   1024 * (t % 2) + 512 * (half + 1)]
                    pr = self.scr[:, 512 * (2 * (t % 2) + half):
                                  512 * (2 * (t % 2) + half) + 512]
                    bank = (self.psA if half == 0 else self.psB)[:, 0:512]
                    ee = None
                    for j in range(24):
                        ee = self.mm(bank,
                                     oiF[:, of + 1024 * j + 128 * tt_:
                                         of + 1024 * j + 128 * (tt_ + 1)],
                                     wdq[:, D * j + 512 * half:
                                         D * j + 512 * (half + 1)],
                                     start=(j == 0), stop=(j == 23),
                                     deps=[e_oi, e_wd, e_bk[half]]
                                     if j == 0 else (), mark=(j == 23))
                    e_g = self.act(sb, bank, AF.Silu, scale=self.smallf[:, 52:53],
                                   deps=[ee, e_soc, e_sb.get((t % 2, half))])
                    e_bk[half] = e_g
                    e_sl = self.dma("YG"[half], pr,
                                    d["park_s"][128 * t:128 * (t + 1),
                                                512 * half:512 * (half + 1)],
                                    deps=[self.e_s_parked, e_oi,
                                          e_pr.get((t % 2, half))])
                    e_t2 = self.tt(pr, pr, sb, OP.mult, deps=[e_g, e_sl])
                    e_sb[(t % 2, half)] = e_t2
                    e_x2 = self.tt(xb[:, 512 * half:512 * (half + 1)],
                                   xb[:, 512 * half:512 * (half + 1)],
                                   pr, OP.add, deps=[e_t2, e_xl])
                    e_pr[(t % 2, half)] = e_x2
                e_ymrg[t % 2] = self.dma("YG"[t % 2],
                                         d["y"][128 * t:128 * (t + 1), :],
                                         xb, deps=[e_pr[(t % 2, 0)],
                                                   e_pr[(t % 2, 1)]])
        p.free("mrg", "oistg", "oiF", "wdT")

    def build_all(self, phase_limit=99):
        self.phase_loads()
        e_r1, e_r2 = self.ln_stats(self.e_x)
        if phase_limit >= 2:
            self.phase_f0_absmax(e_r2)
        if phase_limit >= 3:
            self.qT = self.p.alloc("qT", (128, 8 * TPC), BF16)
            self.phase_xnT(e_r1)
        if phase_limit >= 4:
            self.phase_q()
        if phase_limit >= 1:
            self.phase_wdequant(None)
        if phase_limit >= 5:
            self._scores_impl_setup()
            self._scores_impl()
        if phase_limit >= 6:
            self.phase_kv_kmm()
            self.phase_select()
        if phase_limit >= 7:
            self.phase_kv()
            self.p.free("mksel", "ohlocal", "onehot", "kstore", "wqw2",
                        "kvb", "xnT")
        if phase_limit >= 9:
            self.phase_assemble()
        if phase_limit >= 8:
            self.phase_ffn0()
        if phase_limit >= 9:
            self.phase_parkx()
        if phase_limit >= 10:
            self.phase_attention()
        if phase_limit >= 11:
            self.phase_proj()
            self.p.free("attnT", "asb", "asw", "vsel", "kselT", "qT")
        if phase_limit >= 12:
            self.xsb = self.p.alloc("xsb2", (128, NT * D), F32, side="right")
            for t in range(NT):
                self.dma("YG"[t % 2], self.xsb[:, D * t:D * (t + 1)],
                         self.d["park_x"][128 * t:128 * (t + 1), :],
                         deps=[self.e_x1w])
            e_r1b, _ = self.ln_stats(self.p.dma_all("Y") + self.p.dma_all("G"))
            self.e_r1b = e_r1b
            self.phase_l1_mixer(e_r1b)
        if phase_limit >= 13:
            self.phase_xq()
            self.phase_l1_p1()
            self.phase_l1_ugar()
            self.phase_l1_p2()
            self.phase_l1_wqfree()
            self.phase_l1_cd_setup()
            self.phase_l1_g2ar()
            self.phase_l1_cd(0, 12)
            self.phase_l1_p3()
            self.phase_l1_cd(12, 16)
            self.phase_l1_p4()
        else:
            for t in range(NT):
                self.dma("YG"[t % 2], self.d["y"][128 * t:128 * (t + 1), :],
                         self.xsb[:, D * t:D * (t + 1)],
                         deps=[self.p.now("V"), self.p.now("S"),
                               self.p.now("T"), self.p.dma_all("G")])
        self.p.do("G", lambda eng: eng.engine_nop(),
                  deps=self.p.dma_all("Y") + self.p.dma_all("G") +
                       self.p.dma_all("S"))
        self.p.finalize()
        self.p.stack.close()
        return self.nc

    def _scores_impl_setup(self):
        p, d = self.p, self.d
        self.kvb = p.alloc("kvb", (128, 2 * D), BF16)
        e_kb = self.dma("Y", self.kvb[:, 0:D], d["bk_bc"][:, :])
        e_vb = self.dma("Y", self.kvb[:, D:2 * D], d["bv_bc"][:, :])
        self.e_kvb = (e_kb, e_vb)
        wqw = p.alloc("wqw2", (128, 8 * D), BF16)
        wqt = d["wqkv"].ap().rearrange("(n p) m -> n p m", p=128)
        for i in range(8):
            self.dma("G", wqw[:, D * i:D * (i + 1)], wqt[i][:, D:2 * D])
        self.wqw2 = wqw
        self.e_wk = self.p.dma_all("G")


def build(meta, taps=(), phase_limit=99):
    return Builder(meta, taps).build_all(phase_limit)


def kernel(**inputs):
    in_maps, meta = _host_prep(inputs)
    nc = build(meta)
    res = run_bass_kernel_spmd(nc, in_maps, core_ids=list(range(NCORE)))
    y = np.zeros((B, S, D), dtype=np.float32)
    for c in range(NCORE):
        b_, h_ = c // 2, c % 2
        y[b_, h_ * TPC:(h_ + 1) * TPC] = res.results[c]["y"]
    return y


def locate_instr(p, idx):
    n = 0
    for e in ENGS:
        for site, cnt in p.sitemap[e]:
            if idx < n + cnt:
                return e, site
            n += cnt
    return None, "beyond program (%d instrs)" % n


# version 31 (final)
# speedup vs baseline: 1.0566x; 1.0010x over previous
import math
import sys
from contextlib import ExitStack

import numpy as np

import concourse.bass as bass
import concourse.mybir as mybir
from concourse.bass_utils import run_bass_kernel_spmd

dt = mybir.dt
AF = mybir.ActivationFunctionType
OP = mybir.AluOpType
AX = mybir.AxisListType
F32 = dt.float32
BF16 = dt.bfloat16

MAGIC = 12582912.0

B, S, D = 4, 4096, 1024
NCORE = 8
TPC = S // 2
NT = TPC // 128
H, HD = 16, 64
K_CL, FC = 4, 8
TK = 38
NSUB = 5
NBLK = 384
NSEL = 1728
DI = 3072
LN_EPS = 1e-5
ALL8 = [list(range(8))]
PAIRS = [[0, 1], [2, 3], [4, 5], [6, 7]]
QMAX0 = 127.0
QMAX1 = 32767.0



def _host_prep(inp):
    import ml_dtypes
    bf = ml_dtypes.bfloat16

    clusters = np.asarray(inp["L0_attn_clusters"]).astype(np.int64)
    keyframes = np.asarray(inp["L0_attn_keyframes"]).astype(np.int64)
    assert clusters.shape == (K_CL, FC)
    assert np.array_equal(clusters.ravel(), np.arange(K_CL * FC)), clusters
    assert np.array_equal(keyframes, clusters[:, 0]), keyframes

    x = np.ascontiguousarray(np.asarray(inp["L0_attn_x"], dtype=np.float32))

    def t(a):
        return np.ascontiguousarray(np.asarray(a, dtype=np.float32).T)

    def c(a):
        return np.ascontiguousarray(np.asarray(a, dtype=np.float32))

    def fmaj(a, nt=8):
        return np.ascontiguousarray(
            np.asarray(a, dtype=np.float32).reshape(nt, 128).T)

    def bcast(a, dtype=np.float32):
        a = np.asarray(a, dtype=np.float32)
        return np.ascontiguousarray(
            np.broadcast_to(a[None, :], (128, a.shape[0])).astype(dtype))

    id128 = np.eye(128, dtype=np.float32)
    lts = np.tril(np.ones((128, 128), dtype=np.float32), -1)
    iota38 = np.ascontiguousarray(
        np.broadcast_to(np.arange(TK, dtype=np.float32)[None, :], (128, TK)))
    ones_row = np.ones((1, 128), dtype=np.float32)
    h16 = np.zeros((D, H), dtype=np.float32)
    for hh in range(H):
        h16[hh * HD:(hh + 1) * HD, hh] = 1.0
    meanw = np.zeros((16, K_CL), dtype=np.float32)
    for kf in range(K_CL):
        for bb in range(B):
            meanw[kf * 4 + bb, kf] = 0.25
    e2 = np.zeros((2, 128), dtype=np.float32)
    e2[0, 0:64] = 1.0
    e2[1, 64:128] = 1.0

    wq = c(inp["L0_attn_W_qkv"])
    bqkv = np.asarray(inp["L0_attn_b_qkv"], dtype=np.float32)
    wg = c(inp["L1_ffn_w_gate"])
    wu = c(inp["L1_ffn_w_up"])
    wd = c(inp["L1_ffn_w_down"])

    kf_cand = sorted({int(kf) % 16 for kf in keyframes})

    consts = {
        "id_f32": id128,
        "id_bf": id128.astype(bf),
        "lts": lts,
        "iota38": iota38,
        "ones_f32": ones_row,
        "ones_bf": ones_row.astype(bf),
        "e2_bf": e2.astype(bf),
        "h16": h16,
        "meanw": meanw,
        "zeros_bf": np.zeros((128, 2048), dtype=bf),
    }

    in_maps = []
    for cidx in range(NCORE):
        b_, h_ = cidx // 2, cidx % 2
        kfsel = np.zeros((1, len(kf_cand) * 16), dtype=np.float32)
        for ii, lt in enumerate(kf_cand):
            gf = 16 * h_ + lt
            w = np.where(keyframes == gf)[0]
            if len(w):
                kfsel[0, 16 * ii + int(w[0]) * 4 + b_] = 1.0
        selmask = np.zeros((128, 2 * K_CL * 2), dtype=np.float32)
        for cig in range(2):
            ci = 2 * h_ + cig
            for ncl in range(K_CL):
                if not (ncl == 0 or ncl == ci):
                    col1 = (cig * K_CL + ncl) * 2
                    selmask[190 - 128:, col1] = 1.0
                    selmask[0:48, col1 + 1] = 1.0
        m = {
            "x_in": np.ascontiguousarray(x[b_, h_ * TPC:(h_ + 1) * TPC]),
            "wqkv": wq,
            "bqkv_fm": fmaj(bqkv, 24),
            "bk_bc": bcast(bqkv[D:2 * D], bf),
            "bv_bc": bcast(bqkv[2 * D:3 * D], bf),
            "wproj": c(inp["L0_attn_W_proj"]),
            "bproj_bc": bcast(inp["L0_attn_b_proj"]),
            "wf0T": t(inp["L0_ffn_weight"]),
            "f0g_fm": fmaj(inp["L0_ffn_gamma"]),
            "f0b_fm": fmaj(inp["L0_ffn_beta"]),
            "f0g_bc": bcast(inp["L0_ffn_gamma"]),
            "f0b_bc": bcast(inp["L0_ffn_beta"]),
            "adiag_fm": fmaj(inp["L1_attn_A_diag"]),
            "bmatT": t(inp["L1_attn_Bmat"]),
            "cmatT": t(inp["L1_attn_Cmat"]),
            "dmatT": t(inp["L1_attn_Dmat"]),
            "h0_fm": fmaj(inp["L1_attn_h0"]) if h_ == 0
                     else np.zeros((128, 8), dtype=np.float32),
            "oddflag": np.full((128, 8), float(h_), dtype=np.float32),
            "lng_bc": bcast(inp["L1_attn_ln_gamma"]),
            "lnb_bc": bcast(inp["L1_attn_ln_beta"]),
            "wg_sh": np.ascontiguousarray(wg[cidx * 384:(cidx + 1) * 384]),
            "wu_sh": np.ascontiguousarray(wu[cidx * 384:(cidx + 1) * 384]),
            "wd_sh": np.ascontiguousarray(wd[cidx * 128:(cidx + 1) * 128]),
            "kfsel": kfsel,
            "selmask": selmask,
            "wqk32": np.ascontiguousarray(
                wq[:, 0:2048].reshape(8, 128, 16, 128).transpose(2, 0, 1, 3)
                .reshape(16, 1024, 128)),
            "cigflag": np.ascontiguousarray(np.broadcast_to(
                np.array([[1.0 if ci == 2 * h_ + g else 0.0
                           for g in range(2) for ci in range(K_CL)]],
                         dtype=np.float32), (128, 8))),
        }
        m.update(consts)
        in_maps.append(m)
    return in_maps, {"kf_cand": kf_cand}



ENGS = ("T", "V", "S", "G", "Y")


class Prog:
    def __init__(self, nc):
        self.nc = nc
        self.stack = ExitStack()
        self.items = {e: [] for e in ENGS}
        self.sems = {}
        self.cnt = {}
        self._ccn = 0
        self._bar = []
        self._bar_epoch = 0
        self._eng_epoch = {e: 0 for e in ENGS}
        self.NSLOT = 6
        for e in ENGS:
            self._mksem("s" + e)
            for k in range(self.NSLOT):
                self._mksem("d%s%d" % (e, k))
        self.didx = {e: 0 for e in ENGS}
        self.sb = {}
        self.ps = {}

    def _mksem(self, name):
        self.sems[name] = self.stack.enter_context(self.nc.semaphore(name))
        self.cnt[name] = 0

    _uniq = 0

    def alloc(self, name, shape, dtype, side="left"):
        Prog._uniq += 1
        cm = self.nc.sbuf_tensor("%s_u%d" % (name, Prog._uniq), list(shape),
                                 dtype, side=side)
        h = cm.__enter__()
        self.sb[name] = (cm, h)
        return h

    def barrier(self):
        bar = [("s" + e, self.cnt["s" + e]) for e in ENGS]
        for e in ENGS:
            bar.extend(self.dma_all(e))
        self._bar = [b for b in bar if b[1] > 0]
        self._bar_epoch += 1

    def free(self, *names):
        for name in names:
            cm, h = self.sb.pop(name)
            cm.__exit__(None, None, None)
        self.barrier()

    def psum(self, name, shape, dtype=F32):
        cm = self.nc.psum_tensor(name, list(shape), dtype)
        h = cm.__enter__()
        self.ps[name] = (cm, h)
        return h

    def do(self, e, fn, deps=(), mark=False, dma=False, cc=False):
        incs = []
        ev = None
        if e == "T":
            dd = [("sT", max(0, self.cnt["sT"] - 3))]
        else:
            dd = [("s" + e, self.cnt["s" + e])]
        if self._eng_epoch[e] < self._bar_epoch:
            dd.extend(self._bar)
            self._eng_epoch[e] = self._bar_epoch
        for d in deps:
            if d is None:
                continue
            if isinstance(d, list):
                dd.extend(x for x in d if x is not None)
            else:
                dd.append(d)
        if mark:
            self.cnt["s" + e] += 1
            incs.append(("s" + e, 1))
            ev = ("s" + e, self.cnt["s" + e])
        if dma:
            nm = "d%s%d" % (e, self.didx[e] % self.NSLOT)
            self.didx[e] += 1
            dd.append((nm, self.cnt[nm]))
            self.cnt[nm] += 16
            incs.append((nm, 16))
            ev = (nm, self.cnt[nm])
        if cc:
            nm = "cc%d" % self._ccn
            self._ccn += 1
            self._mksem(nm)
            self.cnt[nm] = 1
            incs.append((nm, 1))
            ev = (nm, 1)
        fr = sys._getframe(1)
        site = "%s:%d" % (fr.f_code.co_name, fr.f_lineno)
        if fr.f_code.co_name in ("dma", "ts", "tt", "stt", "red", "act", "mm",
                                 "copy", "transp", "memset", "recip", "scan",
                                 "coll", "dmat"):
            fr2 = fr.f_back
            site = "%s:%d>%s" % (fr2.f_code.co_name, fr2.f_lineno, fr.f_code.co_name)
        self.items[e].append((tuple(dd), fn, tuple(incs), site))
        return ev

    def dma_all(self, e):
        return [("d%s%d" % (e, k), self.cnt["d%s%d" % (e, k)])
                for k in range(self.NSLOT)]

    def now(self, e):
        return ("s" + e, self.cnt["s" + e])

    def finalize(self):
        with self.nc.Block() as block:
            self.sitemap = {e: [] for e in ENGS}
            def runner(e):
                def go(eng):
                    seen = {}
                    for deps, fn, incs, site in self.items[e]:
                        nw = 0
                        for s_, v_ in deps:
                            if seen.get(s_, -1) < v_:
                                eng.wait_ge(self.sems[s_], v_)
                                seen[s_] = v_
                                nw += 1
                        inst = fn(eng)
                        for s_, n_ in incs:
                            inst.then_inc(self.sems[s_], n_)
                        self.sitemap[e].append((site, nw + 1))
                return go
            block.tensor(runner("T"))
            block.vector(runner("V"))
            block.scalar(runner("S"))
            block.gpsimd(runner("G"))
            block.sync(runner("Y"))



class Builder:
    def __init__(self, meta, taps=()):
        self.meta = meta
        self.taps = set(taps)
        nc = bass.Bass("TRN2", target_bir_lowering=False)
        self.nc = nc
        self.p = Prog(nc)
        self._declare_io()
        self._alloc_base()

    def _declare_io(self):
        nc = self.nc

        def din(name, shape, dtype=F32):
            return nc.dram_tensor(name, list(shape), dtype, kind="ExternalInput")

        def dint(name, shape, dtype=F32, shared=False):
            if shared:
                return nc.dram_tensor(name, list(shape), dtype,
                                      addr_space="Shared")
            return nc.dram_tensor(name, list(shape), dtype)

        self.d = d = {}
        d["x_in"] = din("x_in", (TPC, D))
        d["wqkv"] = din("wqkv", (D, 3 * D))
        d["bqkv_fm"] = din("bqkv_fm", (128, 24))
        d["bk_bc"] = din("bk_bc", (128, D), BF16)
        d["bv_bc"] = din("bv_bc", (128, D), BF16)
        d["wproj"] = din("wproj", (D, D))
        d["bproj_bc"] = din("bproj_bc", (128, D))
        d["wf0T"] = din("wf0T", (D, D))
        for nm in ("f0g_fm", "f0b_fm", "adiag_fm", "h0_fm", "oddflag"):
            d[nm] = din(nm, (128, 8))
        for nm in ("f0g_bc", "f0b_bc", "lng_bc", "lnb_bc"):
            d[nm] = din(nm, (128, D))
        d["bmatT"] = din("bmatT", (D, D))
        d["cmatT"] = din("cmatT", (D, D))
        d["dmatT"] = din("dmatT", (D, D))
        d["wg_sh"] = din("wg_sh", (384, D))
        d["wu_sh"] = din("wu_sh", (384, D))
        d["wd_sh"] = din("wd_sh", (128, DI))
        d["id_f32"] = din("id_f32", (128, 128))
        d["id_bf"] = din("id_bf", (128, 128), BF16)
        d["lts"] = din("lts", (128, 128))
        d["iota38"] = din("iota38", (128, TK))
        d["ones_f32"] = din("ones_f32", (1, 128))
        d["ones_bf"] = din("ones_bf", (1, 128), BF16)
        d["e2_bf"] = din("e2_bf", (2, 128), BF16)
        d["h16"] = din("h16", (D, H))
        d["meanw"] = din("meanw", (16, K_CL))
        d["zeros_bf"] = din("zeros_bf", (128, 2048), BF16)
        d["kfsel"] = din("kfsel", (1, len(self.meta["kf_cand"]) * 16))
        d["selmask"] = din("selmask", (128, 2 * K_CL * 2))
        d["cigflag"] = din("cigflag", (128, 8))
        d["wqk32"] = din("wqk32", (16, 1024, 128))

        d["y"] = nc.dram_tensor("y", [TPC, D], F32, kind="ExternalOutput")

        d["park_x"] = dint("park_x", (TPC, D))
        d["park_g0"] = dint("park_g0", (TPC, D), BF16)
        d["park_up"] = dint("park_up", (DI, TPC))
        d["park_gate"] = dint("park_gate", (DI, TPC))
        d["park_g2"] = dint("park_g2", (DI, TPC))
        d["park_op"] = dint("park_op", (DI, TPC))
        d["park_s"] = dint("park_s", (TPC, D))
        for nm in ("f0", "sx", "ug", "g2", "oo"):
            d["cc_%s_i" % nm] = dint("cc_%s_i" % nm, (1, 8))
            d["cc_%s_o" % nm] = dint("cc_%s_o" % nm, (1, 8), shared=True)
        d["cc_sc_i"] = dint("cc_sc_i", (16, 128))
        d["cc_sc_o"] = dint("cc_sc_o", (16, 128), shared=True)
        d["cc_kn_i"] = dint("cc_kn_i", (16, 1))
        d["cc_kn_o"] = dint("cc_kn_o", (32, 1))
        d["cc_kT_i"] = dint("cc_kT_i", (D, 16 * TK), BF16)
        d["cc_kT_o"] = dint("cc_kT_o", (2 * D, 16 * TK), BF16)
        d["cc_v_i"] = dint("cc_v_i", (16 * TK, 1288), BF16)
        d["cc_v_o"] = dint("cc_v_o", (2 * 16 * TK, 1288), BF16)
        d["cc_hm_i"] = dint("cc_hm_i", (128, 8))
        d["cc_hm_o"] = dint("cc_hm_o", (256, 8))
        d["cc_wg_i"] = dint("cc_wg_i", (384, D), BF16)
        d["cc_wg_o"] = dint("cc_wg_o", (DI, D), BF16, shared=True)
        d["cc_wu_i"] = dint("cc_wu_i", (384, D), BF16)
        d["cc_wu_o"] = dint("cc_wu_o", (DI, D), BF16, shared=True)
        d["cc_wd_i"] = dint("cc_wd_i", (128, DI), BF16)
        d["cc_wd_o"] = dint("cc_wd_o", (D, DI), BF16, shared=True)

    def _alloc_base(self):
        p = self.p
        self.idf = p.alloc("idf", (128, 128), F32)
        self.idb = p.alloc("idb", (128, 128), BF16)
        self.ltsb = p.alloc("ltsb", (128, 128), F32)
        self.iotab = p.alloc("iotab", (128, TK), F32)
        self.onesf = p.alloc("onesf", (1, 128), F32)
        self.onesb = p.alloc("onesb", (1, 128), BF16)
        self.e2b = p.alloc("e2b", (2, 128), BF16)
        self.h16b = p.alloc("h16b", (128, 8 * H), F32)
        self.meanwb = p.alloc("meanwb", (16, K_CL), F32)
        self.kfselb = p.alloc("kfselb", (1, len(self.meta["kf_cand"]) * 16), F32)
        self.selmb = p.alloc("selmb", (128, 2 * K_CL * 2), F32)
        self.cigfb = p.alloc("cigfb", (128, 8), F32)
        self.fm8 = p.alloc("fm8", (128, 5 * 8), F32)
        self.bqfm = p.alloc("bqfm", (128, 24), F32)
        self.stats = p.alloc("stats", (128, 64), F32)
        self.smallf = p.alloc("smallf", (128, 96), F32)
        self.scr = p.alloc("scr", (128, 2048), F32)
        self.scrb = p.alloc("scrb", (128, 3360), BF16)
        self.zbf = p.alloc("zbf", (128, 2048), BF16)
        self.fcb = p.alloc("fcb", (128, 8), F32)
        self._fvals = [LN_EPS, LN_EPS * LN_EPS, 1.0 + LN_EPS, 1.0, MAGIC,
                       65536.0, 0.125, -MAGIC]
        e_fc = None
        for k, v in enumerate(self._fvals):
            col = self.fcb[:, k:k + 1]
            e_fc = self.memset(col, v)
            self.nc.const_aps.register(F32, v, col)
        self.e_fc = self.memset(self.smallf[0:1, 56:64], 0.0)
        self.xsb = p.alloc("xsb", (128, NT * D), F32, side="right")
        self.psA = p.psum("psA", (128, 1024))
        self.psB = p.psum("psB", (128, 1024))
        self.psC = p.psum("psC", (128, 1024))
        self.psD = p.psum("psD", (128, 1024))

    def dma(self, e, out, in_, deps=()):
        return self.p.do(e, lambda eng, o=out, i=in_: eng.dma_start(out=o, in_=i),
                         deps=deps, dma=True)

    def dmat(self, e, out, in_, deps=()):
        return self.p.do(e, lambda eng, o=out, i=in_:
                         eng.dma_start_transpose(o, i), deps=deps, dma=True)

    def ts(self, out, in0, s1, s2, op0, op1=OP.bypass, deps=(), e="V"):
        return self.p.do(e, lambda eng, o=out, i=in0, a=s1, b=s2, p0=op0, p1=op1:
                         eng.tensor_scalar(o, i, a, b, p0, p1), deps=deps, mark=True)

    def tt(self, out, in0, in1, op, deps=(), e="V"):
        return self.p.do(e, lambda eng, o=out, a=in0, b=in1, pp=op:
                         eng.tensor_tensor(o, a, b, pp), deps=deps, mark=True)

    def stt(self, out, in0, scal, in1, op0, op1, deps=(), e="V"):
        return self.p.do(e, lambda eng, o=out, a=in0, s=scal, b=in1, p0=op0, p1=op1:
                         eng.scalar_tensor_tensor(o, a, s, b, p0, p1),
                         deps=deps, mark=True)

    def red(self, out, in_, op, deps=(), absv=None, e="V"):
        return self.p.do(e, lambda eng, o=out, i=in_, pp=op, av=absv:
                         eng.tensor_reduce(o, i, AX.X, pp, apply_absolute_value=av),
                         deps=deps, mark=True)

    def act(self, out, in_, func, bias=0.0, scale=1.0, deps=(), accum=None):
        def f(eng, o=out, i=in_, fu=func, bi=bias, sc=scale, ac=accum):
            if ac is not None:
                return eng.activation(o, i, fu, bias=bi, scale=sc, accum_out=ac)
            return eng.activation(o, i, fu, bias=bi, scale=sc)
        return self.p.do("S", f, deps=deps, mark=True)

    def mm(self, out, lhsT, rhs, start, stop, deps=(), mark=False, tp=None):
        return self.p.do("T", lambda eng, o=out, l=lhsT, r=rhs, st=start, sp=stop,
                         t=tp: eng.matmul(o, l, r, start=st, stop=sp,
                                          tile_position=t), deps=deps, mark=mark)

    def copy(self, out, in_, deps=(), e="V"):
        return self.p.do(e, lambda eng, o=out, i=in_: eng.tensor_copy(o, i),
                         deps=deps, mark=True)

    def transp(self, out, in_, ident, deps=(), mark=False):
        return self.p.do("T", lambda eng, o=out, i=in_, d_=ident:
                         eng.transpose(o, i, d_), deps=deps, mark=mark)

    def memset(self, ap_, val, deps=(), e="V"):
        return self.p.do(e, lambda eng, a=ap_, v=val: eng.memset(a, v),
                         deps=deps, mark=True)

    def recip(self, out, in_, deps=(), lp=False):
        def f(eng, o=out, i=in_, lp=lp):
            if lp:
                with self.nc.allow_low_precision(reason="bf16 softmax normalizer"):
                    return eng.reciprocal(o, i)
            return eng.reciprocal(o, i)
        return self.p.do("V", f, deps=deps, mark=True)

    def scan(self, out, d0, d1, init, deps=()):
        return self.p.do("V", lambda eng, o=out, a=d0, b=d1, i=init:
                         eng.tensor_tensor_scan(o, a, b, i, OP.mult, OP.add),
                         deps=deps, mark=True)

    def coll(self, kind, op, ins_, outs_, groups, deps=()):
        return self.p.do("G", lambda eng, k=kind, o=op, i=ins_, u=outs_, g=groups:
                         eng.collective_compute(k, o, replica_groups=g, ins=i, outs=u),
                         deps=deps, cc=True)

    _pr_last = None
    _bc_last = None

    def part_reduce(self, col_ap, dst, op, dep, absv=None):
        e1 = self.transp(self.psD[0:1, 640:768], col_ap, self.idf[:, :],
                         deps=[dep, Builder._pr_last], mark=True)
        Builder._pr_last = self.red(self.smallf[0:1, dst:dst + 1],
                                    self.psD[0:1, 640:768], op,
                                    deps=[e1], absv=absv)
        return Builder._pr_last

    def bcast_col(self, src, dst, dep, scale=None, bias_add=None):
        e1 = self.mm(self.psD[:, 800:801], self.onesf[0:1, :],
                     self.smallf[0:1, src:src + 1], True, True,
                     deps=[dep, Builder._bc_last], mark=True)
        if scale is None and bias_add is None:
            Builder._bc_last = self.copy(self.smallf[:, dst:dst + 1],
                                         self.psD[:, 800:801], deps=[e1])
        else:
            Builder._bc_last = self.ts(
                self.smallf[:, dst:dst + 1], self.psD[:, 800:801],
                1.0 if scale is None else scale, bias_add,
                OP.mult, OP.bypass if bias_add is None else OP.add, deps=[e1])
        return Builder._bc_last

    def ar_scalar(self, tag, src, dst, op, dep):
        d = self.d
        e0 = self.copy(self.smallf[0:1, 56:57], self.smallf[0:1, src:src + 1],
                       deps=[dep])
        e1 = self.dma("G", d["cc_%s_i" % tag][0:1, 0:8], self.smallf[0:1, 56:64],
                      deps=[e0])
        e2 = self.coll("AllReduce", op, [d["cc_%s_i" % tag].ap().opt()],
                       [d["cc_%s_o" % tag].ap().opt()], ALL8, deps=[e1])
        return self.dma("G", self.smallf[0:1, dst:dst + 1],
                        d["cc_%s_o" % tag][0:1, 0:1], deps=[e2])

    def ln_stats(self, dep):
        evs = []
        for t in range(NT):
            xt = self.xsb[:, D * t:D * (t + 1)]
            evs.append(self.red(self.stats[:, t:t + 1], xt, OP.add, deps=[dep]))
        e_m = self.ts(self.stats[:, 0:16], self.stats[:, 0:16], 1.0 / D, None,
                      OP.mult, deps=[evs[-1]])
        e_q = None
        for t in range(NT):
            xt = self.xsb[:, D * t:D * (t + 1)]
            buf = self.scr[:, 0:D] if t % 2 == 0 else self.scr[:, D:2 * D]
            e1 = self.act(buf, xt, AF.Square,
                          deps=[e_m if t < 2 else e_q, self.e_fc])
            e_q = self.red(self.stats[:, 48 + t:49 + t], buf, OP.add, deps=[e1])
        e_q2 = self.ts(self.stats[:, 48:64], self.stats[:, 48:64], 1.0 / D, None,
                       OP.mult, deps=[e_q])
        e_m2 = self.tt(self.smallf[:, 32:48], self.stats[:, 0:16],
                       self.stats[:, 0:16], OP.mult, deps=[e_q2])
        e_v = self.tt(self.smallf[:, 32:48], self.stats[:, 48:64],
                      self.smallf[:, 32:48], OP.subtract, deps=[e_m2])
        e_s1 = self.act(self.smallf[:, 48:64], self.smallf[:, 32:48], AF.Sqrt,
                        bias=LN_EPS, deps=[e_v])
        e_r1 = self.recip(self.stats[:, 16:32], self.smallf[:, 48:64], deps=[e_s1])
        e_s2 = self.act(self.smallf[:, 48:64], self.smallf[:, 32:48], AF.Sqrt,
                        bias=LN_EPS * LN_EPS, scale=1.0 + LN_EPS, deps=[e_r1])
        e_r2 = self.recip(self.stats[:, 32:48], self.smallf[:, 48:64], deps=[e_s2])
        return e_r1, e_r2

    def phase_loads(self):
        d = self.d
        x_t = d["x_in"].ap().rearrange("(n p) m -> n p m", p=128)
        for t in range(NT):
            self.e_x = self.dma("Y", self.xsb[:, D * t:D * (t + 1)], x_t[t])
        self.dma("Y", self.idf[:, :], d["id_f32"][:, :])
        self.dma("Y", self.idb[:, :], d["id_bf"][:, :])
        self.dma("Y", self.ltsb[:, :], d["lts"][:, :])
        self.dma("Y", self.iotab[:, :], d["iota38"][:, :])
        self.dma("Y", self.onesf[:, :], d["ones_f32"][:, :])
        self.dma("Y", self.onesb[:, :], d["ones_bf"][:, :])
        self.dma("Y", self.e2b[:, :], d["e2_bf"][:, :])
        h16t = d["h16"].ap().rearrange("(n p) m -> n p m", p=128)
        for i in range(8):
            self.dma("Y", self.h16b[:, H * i:H * (i + 1)], h16t[i])
        self.dma("Y", self.meanwb[:, :], d["meanw"][:, :])
        self.dma("Y", self.kfselb[:, :], d["kfsel"][:, :])
        self.dma("Y", self.selmb[:, :], d["selmask"][:, :])
        self.dma("Y", self.cigfb[:, :], d["cigflag"][:, :])
        for i, nm in enumerate(("f0g_fm", "f0b_fm", "adiag_fm", "h0_fm", "oddflag")):
            self.dma("Y", self.fm8[:, 8 * i:8 * i + 8], d[nm][:, :])
        self.dma("Y", self.bqfm[:, :], d["bqkv_fm"][:, :])
        self.e_consts = self.p.dma_all("Y")
        self.e_x = self.p.dma_all("Y")

    def phase_wdequant(self, dep):
        d, p = self.d, self.p
        wqsh = p.alloc("wqsh", (128, DI), F32)
        wqbm = p.alloc("wqbm", (128, 192), F32)
        parts = []
        e_prev = dep
        for (src, dst, ilen, ntile) in ((d["wg_sh"], d["cc_wg_i"], D, 3),
                                        (d["wu_sh"], d["cc_wu_i"], D, 3),
                                        (d["wd_sh"], d["cc_wd_i"], DI, 1)):
            nblk = ilen // 32
            for tt_ in range(ntile):
                e_l = self.dma("Y", wqsh[:, 0:ilen],
                               src[128 * tt_:128 * (tt_ + 1), :],
                               deps=[e_prev] if e_prev else ())
                in3 = bass.AP(wqsh, 0, [[DI, 128], [32, nblk], [1, 32]])
                e_bm = self.red(wqbm[:, 0:nblk], in3, OP.max, deps=[e_l], absv=True)
                e_s = self.ts(wqbm[:, 0:nblk], wqbm[:, 0:nblk], 1.0 / 7.0, 1e-12,
                              OP.mult, OP.max, deps=[e_bm])
                e_rs = self.recip(wqbm[:, 96:96 + nblk], wqbm[:, 0:nblk], deps=[e_s])
                sblk = bass.AP(wqbm, 0, [[192, 128], [1, nblk], [0, 32]])
                rsblk = bass.AP(wqbm, 96, [[192, 128], [1, nblk], [0, 32]])
                w3 = bass.AP(wqsh, 0, [[DI, 128], [32, nblk], [1, 32]])
                s3 = bass.AP(self.scr, 0, [[2048, 128], [32, nblk], [1, 32]])
                if ilen <= 2048:
                    e_t1 = self.tt(s3, w3, rsblk, OP.mult, deps=[e_rs])
                    e_t2 = self.ts(self.scr[:, 0:ilen], self.scr[:, 0:ilen],
                                   MAGIC, MAGIC, OP.add, OP.subtract, deps=[e_t1])
                    e_t3 = self.tt(s3, s3, sblk, OP.mult, deps=[e_t2])
                    e_c = self.copy(self.scrb[:, 0:ilen], self.scr[:, 0:ilen],
                                    deps=[e_t3])
                else:
                    e_c = None
                    for hh in range(2):
                        o0 = hh * (ilen // 2)
                        nb2 = nblk // 2
                        w3h = bass.AP(wqsh, o0, [[DI, 128], [32, nb2], [1, 32]])
                        s3h = bass.AP(self.scr, 0, [[2048, 128], [32, nb2], [1, 32]])
                        sblkh = bass.AP(wqbm, nb2 * hh, [[192, 128], [1, nb2], [0, 32]])
                        rsblkh = bass.AP(wqbm, 96 + nb2 * hh,
                                         [[192, 128], [1, nb2], [0, 32]])
                        e_t1 = self.tt(s3h, w3h, rsblkh, OP.mult, deps=[e_rs])
                        e_t2 = self.ts(self.scr[:, 0:ilen // 2],
                                       self.scr[:, 0:ilen // 2],
                                       MAGIC, MAGIC, OP.add, OP.subtract, deps=[e_t1])
                        e_t3 = self.tt(s3h, s3h, sblkh, OP.mult, deps=[e_t2])
                        e_c = self.copy(self.scrb[:, o0:o0 + ilen // 2],
                                        self.scr[:, 0:ilen // 2], deps=[e_t3])
                e_prev = self.dma("Y", dst[128 * tt_:128 * (tt_ + 1), 0:ilen],
                                  self.scrb[:, 0:ilen], deps=[e_c])
                parts.append(e_prev)
        self.e_wq_ag = {}
        for nm in ("wg", "wu", "wd"):
            self.e_wq_ag[nm] = self.coll(
                "AllGather", OP.bypass, [d["cc_%s_i" % nm].ap().opt()],
                [d["cc_%s_o" % nm].ap().opt()], ALL8, deps=parts)
        p.free("wqbm", "wqsh")

    def phase_f0_absmax(self, e_r2):
        p, d = self.p, self.d
        gb = p.alloc("f0bc", (128, 2 * D), F32)
        e_g = self.dma("Y", gb[:, 0:D], d["f0g_bc"][:, :])
        e_b = self.dma("Y", gb[:, D:2 * D], d["f0b_bc"][:, :])
        e_last = None
        for t in range(NT):
            xt = self.xsb[:, D * t:D * (t + 1)]
            buf = self.scr[:, 0:D] if t % 2 == 0 else self.scr[:, D:2 * D]
            e1 = self.ts(buf, xt, self.stats[:, t:t + 1],
                         self.stats[:, 32 + t:33 + t], OP.subtract, OP.mult,
                         deps=[e_r2, e_g, e_b])
            e2 = self.tt(buf, buf, gb[:, 0:D], OP.mult, deps=[e1])
            e3 = self.tt(buf, buf, gb[:, D:2 * D], OP.add, deps=[e2])
            e_last = self.red(self.smallf[:, 8 + t:9 + t], buf, OP.max,
                              deps=[e3], absv=True)
        e5 = self.red(self.smallf[:, 24:25], self.smallf[:, 8:24], OP.max,
                      deps=[e_last])
        e6 = self.part_reduce(self.smallf[:, 24:25], 0, OP.max, e5)
        e_rd = self.ar_scalar("f0", 0, 1, OP.max, e6)
        e_s = self.ts(self.smallf[0:1, 2:3], self.smallf[0:1, 1:2], 1e-8,
                      1.0 / QMAX0, OP.max, OP.mult, deps=[e_rd])
        e_r = self.recip(self.smallf[0:1, 3:4], self.smallf[0:1, 2:3], deps=[e_s])
        self.e_rf0 = self.bcast_col(3, 4, e_r)
        self.e_sf0 = self.bcast_col(2, 5, self.e_rf0)
        p.free("f0bc")

    def phase_xnT(self, e_r1):
        p = self.p
        self.xnT = p.alloc("xnT", (128, 8 * TPC), BF16)
        e_tp = {}
        for t in range(NT):
            xt = self.xsb[:, D * t:D * (t + 1)]
            buf = self.scr[:, 0:D] if (t % 2 == 0) else self.scr[:, D:2 * D]
            dep = [e_r1]
            if t >= 2:
                dep.append(e_tp[t - 2]["tp"])
            e1 = self.ts(buf, xt, self.stats[:, t:t + 1],
                         self.stats[:, 16 + t:17 + t], OP.subtract, OP.mult,
                         deps=dep)
            pt = self.psA if (t % 2 == 0) else self.psB
            e_tpl = None
            for i in range(8):
                dep2 = [e1]
                if t >= 2:
                    dep2.append(e_tp[t - 2]["ev"])
                e_tpl = self.transp(pt[:, 128 * i:128 * (i + 1)],
                                    buf[:, 128 * i:128 * (i + 1)],
                                    self.idf[:, :], deps=dep2 if i == 0 else (),
                                    mark=(i == 7))
            e_ev = None
            for i in range(8):
                e_ev = self.act(self.xnT[:, TPC * i + 128 * t:TPC * i + 128 * (t + 1)],
                                pt[:, 128 * i:128 * (i + 1)], AF.Identity, deps=[e_tpl])
            e_tp[t] = {"tp": e_tpl, "ev": e_ev}
        self.e_xnT = e_tp[NT - 1]["ev"]

    def phase_q(self):
        p, d = self.p, self.d
        wqw = p.alloc("wqw", (128, 8 * D), BF16)
        wqt = d["wqkv"].ap().rearrange("(n p) m -> n p m", p=128)
        for i in range(8):
            self.dma("G", wqw[:, D * i:D * (i + 1)], wqt[i][:, 0:D])
        e_wq = self.p.dma_all("G")
        nqc = TPC // 512
        e_ev_prev = {}
        for qi in range(8):
            for tc in range(nqc):
                k = (qi * nqc + tc) % 4
                pt = self.psA if k < 2 else self.psB
                bank = pt[:, 0:512] if (k % 2 == 0) else pt[:, 512:1024]
                deps = [self.e_xnT, e_wq]
                if k in e_ev_prev:
                    deps.append(e_ev_prev[k])
                ee = None
                for i in range(8):
                    ee = self.mm(bank, wqw[:, D * i + 128 * qi:D * i + 128 * (qi + 1)],
                                 self.xnT[:, TPC * i + 512 * tc:TPC * i + 512 * (tc + 1)],
                                 start=(i == 0), stop=(i == 7),
                                 deps=deps if i == 0 else (), mark=(i == 7))
                e_ev = self.act(self.qT[:, TPC * qi + 512 * tc:TPC * qi + 512 * (tc + 1)],
                                bank, AF.Identity, bias=self.bqfm[:, qi:qi + 1], deps=[ee])
                e_ev_prev[k] = e_ev
        self.e_qT = self.p.now("S")
        p.free("wqw")

    def _scores_impl(self):
        p, d = self.p, self.d
        kf_cand = self.meta["kf_cand"]
        nkf = len(kf_cand)
        kfx = p.alloc("kfx", (128, nkf * 8 * 128), F32)
        wqf = p.alloc("wqf32", (128, 3 * 1024), F32)
        wqt = d["wqkv"].ap().rearrange("(n p) m -> n p m", p=128)

        e_ev = None
        for ii, lt in enumerate(kf_cand):
            buf = self.scr[:, 0:D] if ii % 2 == 0 else self.scr[:, D:2 * D]
            e1 = self.ts(buf, self.xsb[:, D * lt:D * (lt + 1)],
                         self.stats[:, lt:lt + 1], self.stats[:, 16 + lt:17 + lt],
                         OP.subtract, OP.mult, deps=[self.e_xnT, e_ev])
            e_tpl = None
            for i in range(8):
                e_tpl = self.transp(self.psA[:, 128 * i:128 * (i + 1)],
                                    buf[:, 128 * i:128 * (i + 1)], self.idf[:, :],
                                    deps=[e1, e_ev] if i == 0 else (),
                                    mark=(i == 7))
            for i in range(8):
                e_ev = self.copy(kfx[:, 1024 * ii + 128 * i:1024 * ii + 128 * (i + 1)],
                                 self.psA[:, 128 * i:128 * (i + 1)], deps=[e_tpl])
        qkkf = p.alloc("qkkf", (128, 16 * (nkf * 128)), F32)
        e_prev = {}
        e_mmo = {}
        W = nkf * 128
        for oc in range(16):
            k2 = oc % 3
            e_wf = self.dma(
                "YGS"[oc % 3], wqf[:, 1024 * k2:1024 * (k2 + 1)],
                bass.AP(d["wqk32"], 1024 * 128 * oc,
                        [[128, 128], [128 * 128, 8], [1, 128]]),
                deps=[e_mmo.get(k2)])
            assert W <= 512
            bank = (self.psB[:, 0:W], self.psB[:, 512:512 + W],
                    self.psA[:, 0:W])[k2]
            ee = None
            for i in range(8):
                mv = bass.AP(kfx, 128 * i,
                             [[nkf * 8 * 128, 128], [1024, nkf], [1, 128]])
                ee = self.mm(bank,
                             wqf[:, 1024 * k2 + 128 * i:1024 * k2 + 128 * (i + 1)],
                             mv, start=(i == 0), stop=(i == 7),
                             deps=[e_wf, e_ev, e_prev.get(k2)] if i == 0 else (),
                             mark=(i == 7))
            e_mmo[k2] = ee
            e_pr = self.act(qkkf[:, W * oc:W * (oc + 1)], bank, AF.Identity,
                            bias=self.bqfm[:, oc:oc + 1], deps=[ee])
            e_prev[k2] = e_pr
        e_qk = self.p.now("S")

        e_ar_parts = []
        for ii in range(nkf):
            e_pr = None
            for i in range(8):
                e_pr = self.tt(self.scr[:, 128 * i:128 * (i + 1)],
                               qkkf[:, W * i + 128 * ii:W * i + 128 * (ii + 1)],
                               qkkf[:, W * (8 + i) + 128 * ii:W * (8 + i) + 128 * (ii + 1)],
                               OP.mult, deps=[e_qk] if i == 0 else ())
            ee = None
            for i in range(8):
                ee = self.mm(self.psB[0:16, 0:128], self.h16b[:, H * i:H * (i + 1)],
                             self.scr[:, 128 * i:128 * (i + 1)],
                             start=(i == 0), stop=(i == 7),
                             deps=[e_pr] if i == 0 else (), mark=(i == 7))
            e_hs = self.copy(self.scr[0:16, 1024:1152], self.psB[0:16, 0:128],
                             deps=[ee])
            e_tp = self.transp(self.psB[0:128, 512:528], self.scr[0:16, 1024:1152],
                               self.idf[0:16, 0:16], deps=[e_hs], mark=True)
            e_mx = self.red(self.scr[:, 1152:1153], self.psB[:, 512:528], OP.max,
                            deps=[e_tp])
            e_tp2 = self.transp(self.psB[0:1, 528:656], self.scr[:, 1152:1153],
                                self.idf[:, :], deps=[e_mx], mark=True)
            e_row = self.copy(self.scr[0:1, 1168:1296], self.psB[0:1, 528:656],
                              deps=[e_tp2])
            e_pl = self.mm(self.psC[0:16, 0:128],
                           self.kfselb[0:1, 16 * ii:16 * (ii + 1)],
                           self.scr[0:1, 1168:1296],
                           start=(ii == 0), stop=(ii == nkf - 1),
                           deps=[e_row], mark=True)
            e_ar_parts.append(e_pl)
        e_arb = self.copy(self.scr[0:16, 0:128], self.psC[0:16, 0:128],
                          deps=[e_ar_parts[-1], self.p.now("V")])
        e_w = self.dma("G", d["cc_sc_i"][:, :], self.scr[0:16, 0:128], deps=[e_arb])
        e_cc = self.coll("AllReduce", OP.add, [d["cc_sc_i"].ap().opt()],
                         [d["cc_sc_o"].ap().opt()], ALL8, deps=[e_w])
        self.e_scores_ar = self.dma("G", self.scr[0:16, 0:128], d["cc_sc_o"][:, :],
                                    deps=[e_cc])
        p.free("qkkf", "wqf32", "kfx")

    def phase_select(self):
        p = self.p
        e1 = self.mm(self.psB[0:4, 0:128], self.meanwb[:, :], self.scr[0:16, 0:128],
                     True, True, deps=[self.e_scores_ar], mark=True)
        e2 = self.copy(self.scr[0:4, 128:256], self.psB[0:4, 0:128], deps=[e1])
        e3 = self.transp(self.psB[0:128, 512:516], self.scr[0:4, 128:256],
                         self.idf[0:4, 0:4], deps=[e2], mark=True)
        e4 = self.copy(self.scr[:, 256:260], self.psB[:, 512:516], deps=[e3])
        self.onehot = p.alloc("onehot", (128, K_CL * TK), BF16)
        for ci in range(K_CL):
            scol = self.scr[:, 256 + ci:257 + ci]
            e4b = self.transp(self.psB[0:1, 512:640], scol, self.idf[:, :],
                              deps=[e4], mark=True)
            e4c = self.copy(self.scr[0:1, 128:256], self.psB[0:1, 512:640],
                            deps=[e4b])
            e5 = self.mm(self.psB[:, 0:128], self.onesf[0:1, :],
                         self.scr[0:1, 128:256], True, True,
                         deps=[e4c], mark=True)
            eA = self.ts(self.scr[:, 512:640], self.psB[:, 0:128], scol, None,
                         OP.is_gt, deps=[e5])
            eB = self.ts(self.scr[:, 640:768], self.psB[:, 0:128], scol, None,
                         OP.is_equal, deps=[eA])
            eC = self.tt(self.scr[:, 640:768], self.scr[:, 640:768], self.ltsb[:, :],
                         OP.mult, deps=[eB])
            eD = self.tt(self.scr[:, 512:640], self.scr[:, 512:640],
                         self.scr[:, 640:768], OP.add, deps=[eC])
            eR = self.red(self.scr[:, 260 + ci:261 + ci], self.scr[:, 512:640],
                          OP.add, deps=[eD])
            eM = self.ts(self.scr[:, 264 + ci:265 + ci], self.scr[:, 260 + ci:261 + ci],
                         TK - 0.5, None, OP.is_lt, deps=[eR])
            eT = self.transp(self.psB[0:1, 512:640], self.scr[:, 264 + ci:265 + ci],
                             self.idf[:, :], deps=[eM], mark=True)
            eRw = self.copy(self.scr[0:1, 384:512], self.psB[0:1, 512:640], deps=[eT])
            eMb = self.mm(self.psB[:, 0:128], self.onesf[0:1, :],
                          self.scr[0:1, 384:512], True, True, deps=[eRw], mark=True)
            eC2 = self.tt(self.scr[:, 512:640], self.psB[:, 0:128], self.ltsb[:, :],
                          OP.mult, deps=[eMb])
            eCs = self.red(self.scr[:, 268 + ci:269 + ci], self.scr[:, 512:640],
                           OP.add, deps=[eC2])
            eOh = self.ts(self.scr[:, 512:512 + TK], self.iotab[:, :],
                          self.scr[:, 268 + ci:269 + ci], None, OP.is_equal,
                          deps=[eCs])
            self.e_onehot = self.ts(self.onehot[:, TK * ci:TK * (ci + 1)],
                                    self.scr[:, 512:512 + TK],
                                    self.scr[:, 264 + ci:265 + ci], None, OP.mult,
                                    deps=[eOh])

    def phase_kv_kmm(self):
        p, d = self.p, self.d
        self.kst = p.alloc("kstore", (128, NT * D), BF16)
        e_kb, e_vb = self.e_kvb
        self.e_kstev = {}
        e_gevac = {}
        for t in range(NT):
            e_ev = None
            for half in range(2):
                bank = self.psA[:, 512 * half:512 * (half + 1)]
                ee = None
                for i in range(8):
                    ee = self.mm(bank,
                                 self.xnT[:, TPC * i + 128 * t:TPC * i + 128 * (t + 1)],
                                 self.wqw2[:, D * i + 512 * half:D * i + 512 * (half + 1)],
                                 start=(i == 0), stop=(i == 7),
                                 deps=[self.e_xnT, e_kb, self.e_wk,
                                       e_gevac.get(half)] if i == 0 else (),
                                 mark=(i == 7))
                e_ev = self.stt(self.kst[:, D * t + 512 * half:
                                         D * t + 512 * (half + 1)], bank,
                                1.0, self.kvb[:, 512 * half:512 * (half + 1)],
                                OP.mult, OP.add, deps=[ee])
                e_gevac[half] = e_ev
            self.e_kstev[t] = e_ev

    def phase_kv(self):
        p, d = self.p, self.d
        wqw = self.wqw2
        wqt = d["wqkv"].ap().rearrange("(n p) m -> n p m", p=128)

        self.ohl = p.alloc("ohlocal", (128, 2 * TK), BF16)
        self.e_ohl2 = {}
        self.mksel = p.alloc("mksel", (128, 2), F32)
        for g in range(2):
            e1 = self.ts(self.scr[:, 0:TK], self.onehot[:, 0:TK],
                         self.cigfb[:, 4 * g:4 * g + 1], None, OP.mult,
                         deps=[self.e_onehot])
            for ci in range(1, K_CL):
                e1 = self.stt(self.scr[:, 0:TK],
                              self.onehot[:, TK * ci:TK * (ci + 1)],
                              self.cigfb[:, 4 * g + ci:4 * g + ci + 1],
                              self.scr[:, 0:TK], OP.mult, OP.add, deps=[e1])
            e2 = self.copy(self.ohl[:, TK * g:TK * (g + 1)], self.scr[:, 0:TK],
                           deps=[e1])
            self.e_ohl2[g] = e2
            self.e_mksel = self.red(self.mksel[:, g:g + 1], self.scr[:, 0:TK],
                                    OP.add, deps=[e2])

        knmax = p.alloc("knmax", (128, H), F32)
        e_kn = self.memset(knmax[:, :], 0.0)
        e_kb, e_vb = self.e_kvb

        e_gdp = {0: None, 1: None}
        e_gc = None
        e_n1p = None
        for t in range(NT):
            g = 0 if t < 8 else 1
            off = 2 * D + 304 * (t % 2)
            ktile = self.kst[:, D * t:D * (t + 1)]
            e_ev_t = self.e_kstev[t]
            ee = None
            for qi in range(8):
                ee = self.mm(self.psB[:, TK * qi:TK * (qi + 1)],
                             ktile[:, 128 * qi:128 * (qi + 1)],
                             self.ohl[:, TK * g:TK * (g + 1)],
                             start=True, stop=True,
                             deps=[e_ev_t, self.e_ohl2[g], self.e_mksel,
                                   e_gc] if qi == 0 else (),
                             mark=(qi == 7))
            e_gc = self.copy(self.scrb[:, off:off + 8 * TK],
                             self.psB[:, 0:8 * TK], deps=[ee, e_gdp[t % 2]])
            src = bass.AP(self.scrb, off, [[3360, 128], [TK, 8], [1, TK]])
            dst = bass.AP(d["cc_kT_i"], TK * t,
                          [[16 * TK, 128], [16 * TK * 128, 8], [1, TK]])
            e_gdp[t % 2] = self.dma("Y", dst, src, deps=[e_gc])
            e_sq = self.act(self.scr[:, 0:D], ktile, AF.Square,
                            deps=[e_ev_t, self.e_mksel, e_n1p])
            nrm3 = bass.AP(self.scr, 0, [[2048, 128], [HD, H], [1, HD]])
            e_n1 = self.red(self.scr[:, 1024:1024 + H], nrm3, OP.add,
                            deps=[e_sq])
            e_n1p = e_n1
            e_n2 = self.ts(self.scr[:, 1024:1024 + H],
                           self.scr[:, 1024:1024 + H],
                           self.mksel[:, g:g + 1], None, OP.mult,
                           deps=[e_n1])
            e_kn = self.tt(knmax[:, :], knmax[:, :],
                           self.scr[:, 1024:1024 + H], OP.max, deps=[e_n2])
        e_wv = None
        e_gevac = {0: None, 1: None, "v": [e_gdp[0], e_gdp[1]]}
        for which in (1,):
            if which == 1:
                for i in range(8):
                    self.dma("G", wqw[:, D * i:D * (i + 1)],
                             wqt[i][:, 2 * D:3 * D],
                             deps=[self.p.now("T")])
                e_wv = self.p.dma_all("G")
            for t in range(NT):
                g = 0 if t < 8 else 1
                ktile = self.scrb[:, 0:D] if t % 2 == 0 else self.scrb[:, D:2 * D]
                e_ev_t = None
                for half in range(2):
                    bank = self.psA[:, 512 * half:512 * (half + 1)]
                    ee = None
                    dep0 = [self.e_xnT, e_kb, e_vb, self.e_mksel, e_kn]
                    if which == 1:
                        dep0.append(e_wv)
                    dep0.append(e_gevac.get(half))
                    for i in range(8):
                        ee = self.mm(bank,
                                     self.xnT[:, TPC * i + 128 * t:TPC * i + 128 * (t + 1)],
                                     wqw[:, D * i + 512 * half:D * i + 512 * (half + 1)],
                                     start=(i == 0), stop=(i == 7),
                                     deps=dep0 if i == 0 else (), mark=(i == 7))
                    bias = self.kvb[:, D * which + 512 * half:
                                    D * which + 512 * (half + 1)]
                    e_ev_t = self.stt(ktile[:, 512 * half:512 * (half + 1)], bank,
                                      1.0, bias, OP.mult, OP.add, deps=[ee])
                    e_gevac[half] = e_ev_t
                if True:
                    ee = None
                    for half in range(2):
                        ee = self.mm(self.psB[0:TK, 512 * half:512 * (half + 1)],
                                     self.ohl[:, TK * g:TK * (g + 1)],
                                     ktile[:, 512 * half:512 * (half + 1)],
                                     start=True, stop=True,
                                     deps=[e_ev_t] if half == 0 else (),
                                     mark=(half == 1))
                    e_z0 = self.memset(self.scrb[0:TK, 2 * D:2 * D + 1288], 0.0,
                                       deps=[e_gevac.get("v")])
                    e_on = self.memset(
                        bass.AP(self.scrb, 2 * D + 64, [[3360, TK], [161, 8]]),
                        1.0, deps=[e_z0])
                    e_on = self.memset(
                        bass.AP(self.scrb, 2 * D + 65, [[3360, TK], [161, 8]]),
                        1.0, deps=[e_on])
                    srcE = bass.AP(self.psB, 0, [[1024, TK], [128, 8], [1, HD]])
                    dstE = bass.AP(self.scrb, 2 * D, [[3360, TK], [161, 8], [1, HD]])
                    e_vc = self.copy(dstE, srcE, deps=[ee, e_on])
                    srcO = bass.AP(self.psB, HD, [[1024, TK], [128, 8], [1, HD]])
                    dstO = bass.AP(self.scrb, 2 * D + 97,
                                   [[3360, TK], [161, 8], [1, HD]])
                    e_vc = self.copy(dstO, srcO, deps=[e_vc])
                    e_vd = self.dma("Y", d["cc_v_i"][TK * t:TK * (t + 1), :],
                                    self.scrb[0:TK, 2 * D:2 * D + 1288],
                                    deps=[e_vc])
                    e_gevac["v"] = e_vd
        e_t1 = self.transp(self.psB[0:H, 0:128], knmax[:, :], self.idf[:, :],
                           deps=[e_kn, self.p.now("V"), self.p.now("Y")],
                           mark=True)
        e_t2 = self.copy(self.scr[0:H, 0:128], self.psB[0:H, 0:128], deps=[e_t1])
        e_t3 = self.red(self.scr[0:H, 128:129], self.scr[0:H, 0:128], OP.max,
                        deps=[e_t2])
        e_t4 = self.dma("G", d["cc_kn_i"][:, :], self.scr[0:H, 128:129], deps=[e_t3])
        self.e_kn_ag = self.coll("AllGather", OP.bypass, [d["cc_kn_i"].ap().opt()],
                                 [d["cc_kn_o"].ap().opt()], PAIRS, deps=[e_t4])
        self.e_kv_ag = []
        dmadone = self.p.dma_all("Y")
        self.e_kv_ag.append(self.coll("AllGather", OP.bypass,
                                      [d["cc_kT_i"].ap().opt()],
                                      [d["cc_kT_o"].ap().opt()], PAIRS,
                                      deps=dmadone))
        self.e_kv_ag.append(self.coll("AllGather", OP.bypass,
                                      [d["cc_v_i"].ap().opt()],
                                      [d["cc_v_o"].ap().opt()], PAIRS,
                                      deps=dmadone))
        p.free("knmax")

    def phase_ffn0(self):
        p, d = self.p, self.d
        tT = p.alloc("tT", (128, 8 * D), BF16)
        wft = d["wf0T"].ap().rearrange("(n p) m -> n p m", p=128)
        e_red = []
        for i in range(8):
            buf = self.scr[:, 0:D] if i % 2 == 0 else self.scr[:, D:2 * D]
            dep = [e_red[i - 2]] if i >= 2 else ()
            e_l = self.dma("YG"[i % 2], buf, wft[i], deps=dep)
            e_red.append(self.red(self.smallf[:, 8 + i:9 + i], buf, OP.add,
                                  deps=[e_l], absv=True))
        e_s = self.red(self.smallf[:, 16:17], self.smallf[:, 8:16], OP.add,
                       deps=[e_red[-1]])
        e_s1 = self.part_reduce(self.smallf[:, 16:17], 6, OP.add, e_s)
        e_d = self.ts(self.smallf[0:1, 7:8], self.smallf[0:1, 6:7],
                      0.7 / (D * D), None, OP.mult, deps=[e_s1])
        e_dc = self.bcast_col(7, 8, e_d)
        awb = p.alloc("awb", (128, D), F32)
        e_cnt = []
        e_s2 = []
        e_tt = None
        e_prev = e_dc
        for i in range(8):
            buf = self.scr[:, 0:D] if i % 2 == 0 else self.scr[:, D:2 * D]
            e_l = self.dma("YG"[i % 2], buf, wft[i], deps=[e_prev])
            e_aw = self.stt(awb[:, :], buf, -1.0, buf, OP.mult, OP.max,
                            deps=[e_l])
            e_m = self.ts(self.scrb[:, 0:D], awb[:, :], self.smallf[:, 8:9], None,
                          OP.is_gt, deps=[e_aw])
            e_c = self.red(self.smallf[:, 64 + i:65 + i], self.scrb[:, 0:D],
                           OP.add, deps=[e_m])
            e_p = self.tt(awb[:, :], awb[:, :], self.scrb[:, 0:D], OP.mult,
                          deps=[e_c])
            e_s2i = self.red(self.smallf[:, 72 + i:73 + i], awb[:, :], OP.add,
                             deps=[e_p])
            e_sg = self.act(buf, buf, AF.Sign, deps=[e_s2i])
            e_tt = self.tt(tT[:, D * i:D * (i + 1)], buf, self.scrb[:, 0:D],
                           OP.mult, deps=[e_sg])
            e_cnt.append(e_c)
            e_s2.append(e_s2i)
            e_prev = e_tt
        e_c2 = self.red(self.smallf[:, 17:18], self.smallf[:, 64:72], OP.add,
                        deps=[e_tt])
        e_cS = self.part_reduce(self.smallf[:, 17:18], 9, OP.add, e_c2)
        e_s3 = self.red(self.smallf[:, 18:19], self.smallf[:, 72:80], OP.add,
                        deps=[e_cS])
        e_sS = self.part_reduce(self.smallf[:, 18:19], 10, OP.add, e_s3)
        e_mx = self.ts(self.smallf[0:1, 11:12], self.smallf[0:1, 9:10], 1.0, None,
                       OP.max, deps=[e_sS])
        e_rc = self.recip(self.smallf[0:1, 12:13], self.smallf[0:1, 11:12],
                          deps=[e_mx])
        e_al = self.tt(self.smallf[0:1, 13:14], self.smallf[0:1, 10:11],
                       self.smallf[0:1, 12:13], OP.mult, deps=[e_rc])
        e_sa = self.tt(self.smallf[0:1, 14:15], self.smallf[0:1, 13:14],
                       self.smallf[0:1, 2:3], OP.mult, deps=[e_al, self.e_sf0])
        e_sac = self.bcast_col(14, 15, e_sa)
        p.free("awb")

        gcol = p.alloc("gcol", (128, 16), F32)
        e_g1 = self.ts(gcol[:, 0:8], self.fm8[:, 0:8], self.smallf[:, 4:5], None,
                       OP.mult, deps=[self.e_rf0])
        e_g2 = self.ts(gcol[:, 8:16], self.fm8[:, 8:16], self.smallf[:, 4:5],
                       None, OP.mult, deps=[e_g1])

        e_ev_prev = {0: None, 1: None}
        e_tp_prev = {0: None, 1: None}
        e_mm2 = {0: None, 1: None}
        e_rn_t = {}
        e_pk_prev = None
        e_sl = None

        def ints_of(t):
            return (self.scrb[:, 0:D] if t % 2 == 0
                    else self.scrb[:, 2048:2048 + D])

        for t in range(NT + 1):
            if t < NT:
                xt = self.xsb[:, D * t:D * (t + 1)]
                buf = self.scr[:, 0:D] if t % 2 == 0 else self.scr[:, D:2 * D]
                e1 = self.ts(buf, xt, self.stats[:, t:t + 1],
                             self.stats[:, 32 + t:33 + t], OP.subtract, OP.mult,
                             deps=[e_g2, e_tp_prev[t % 2]])
                pt = self.psA if t % 2 == 0 else self.psB
                e_tpl = None
                for i in range(8):
                    e_tpl = self.transp(pt[:, 128 * i:128 * (i + 1)],
                                        buf[:, 128 * i:128 * (i + 1)],
                                        self.idf[:, :],
                                        deps=[e1, e_ev_prev[t % 2]]
                                        if i == 0 else (), mark=(i == 7))
                e_tp_prev[t % 2] = e_tpl
                e_ev = None
                for i in range(8):
                    e_ev = self.act(buf[:, 128 * i:128 * (i + 1)],
                                    pt[:, 128 * i:128 * (i + 1)], AF.Identity,
                                    bias=gcol[:, 8 + i:9 + i],
                                    scale=gcol[:, i:i + 1],
                                    deps=[e_tpl])
                e_ev_prev[t % 2] = e_ev
                e_rn_t[t] = self.ts(ints_of(t), buf, MAGIC, MAGIC,
                                    OP.add, OP.subtract,
                                    deps=[e_ev, e_mm2[t % 2]])
            if t >= 1:
                u = t - 1
                ints = ints_of(u)
                ees = []
                for half in range(2):
                    bank = self.psC[:, 512 * half:512 * (half + 1)]
                    ee = None
                    for i in range(8):
                        ee = self.mm(bank, ints[:, 128 * i:128 * (i + 1)],
                                     tT[:, D * i + 512 * half:
                                        D * i + 512 * (half + 1)],
                                     start=(i == 0), stop=(i == 7),
                                     deps=[e_rn_t[u], e_pk_prev]
                                     if i == 0 else (), mark=(i == 7))
                    ees.append(ee)
                    e_sl = self.act(self.scrb[:, D + 512 * half:
                                              D + 512 * (half + 1)],
                                    bank, AF.Silu, scale=self.smallf[:, 15:16],
                                    deps=[ee, e_sac, e_pk_prev])
                e_mm2[u % 2] = ees
                e_pk_prev = self.dma("YG"[u % 2],
                                     d["park_g0"][128 * u:128 * (u + 1), :],
                                     self.scrb[:, D:2 * D], deps=[e_sl])
        self.e_g0 = self.p.dma_all("Y") + self.p.dma_all("G")
        p.free("gcol", "tT")

    def phase_parkx(self):
        d = self.d
        for t in range(NT):
            e = self.dma("YG"[t % 2], d["park_x"][128 * t:128 * (t + 1), :],
                         self.xsb[:, D * t:D * (t + 1)],
                         deps=[self.p.now("S"), self.p.now("V")])
        self.e_parkx = self.p.dma_all("Y") + self.p.dma_all("G")
        self.p.free("xsb")

    def phase_assemble(self):
        p, d = self.p, self.d
        self.kselT = p.alloc("kselT", (128, 8 * NSEL), BF16)
        self.vsel = p.alloc("vsel", (128, 10 * 1288), BF16)
        asw = p.alloc("asw", (128, 300), F32)
        asb = p.alloc("asb", (128, TPC), BF16)
        evs = []
        for r in range(2):
            for q in range(2):
                ncl = 2 * r + q
                for i in range(8):
                    evs.append(self.dma(
                        "Y",
                        self.kselT[:, NSEL * i + NBLK * ncl:
                                   NSEL * i + NBLK * ncl + 8 * TK],
                        d["cc_kT_o"][D * r + 128 * i:D * r + 128 * (i + 1),
                                     8 * TK * q:8 * TK * (q + 1)],
                        deps=[self.e_kv_ag[0]]))
        e_kvd = self.p.dma_all("Y")
        for i in range(8):
            for ncl in range(4):
                evs.append(self.copy(
                    self.kselT[:, NSEL * i + 1536 + 48 * ncl:
                               NSEL * i + 1536 + 48 * (ncl + 1)],
                    self.kselT[:, NSEL * i + NBLK * ncl + 256:
                               NSEL * i + NBLK * ncl + 304],
                    deps=[e_kvd] if (i == 0 and ncl == 0) else ()))
        for r in range(2):
            for q in range(2):
                ncl = 2 * r + q
                for pi, (p0, n) in enumerate(((0, 128), (128, 128))):
                    evs.append(self.dma(
                        "Y",
                        self.vsel[0:n, 1288 * (2 * ncl + pi):
                                  1288 * (2 * ncl + pi) + 1288],
                        d["cc_v_o"][608 * r + 304 * q + p0:
                                    608 * r + 304 * q + p0 + n, :],
                        deps=[self.e_kv_ag[1]]))
                evs.append(self.dma(
                    "Y",
                    self.vsel[48 * (ncl % 2):48 * (ncl % 2) + 48,
                              1288 * (8 + ncl // 2):1288 * (8 + ncl // 2) + 1288],
                    d["cc_v_o"][608 * r + 304 * q + 256:
                                608 * r + 304 * q + 304, :],
                    deps=[self.e_kv_ag[1]]))
        self.e_ksel = evs

        e_qm = []
        for i in range(8):
            e_sq = self.act(asb[:, 0:TPC], self.qT[:, TPC * i:TPC * (i + 1)],
                            AF.Square,
                            deps=[self.e_qT, e_qm[i - 1] if i else None])
            e_qm.append(self.red(self.smallf[:, 24 + i:25 + i],
                                 asb[:, 0:TPC], OP.max, deps=[e_sq]))
        e_t = self.transp(self.psD[0:8, 640:768], self.smallf[:, 24:32],
                          self.idf[:, :], deps=[e_qm[-1]], mark=True)
        e_c = self.copy(asw[0:8, 0:128], self.psD[0:8, 640:768], deps=[e_t])
        q3 = bass.AP(asw, 0, [[300, 8], [64, 2], [1, 64]])
        e_z1 = self.memset(asw[:, 258:259], 0.0, deps=[e_c])
        e_h1 = self.red(asw[0:8, 256:258], q3, OP.add, deps=[e_z1])
        e_h2 = self.red(asw[0:8, 258:259], asw[0:8, 256:258], OP.max,
                        deps=[e_h1])
        e_qg = self.part_reduce(asw[:, 258:259], 20, OP.max, e_h2)
        e_k1 = self.dma("Y", asw[0:H, 129:130], d["cc_kn_o"][0:H, :],
                        deps=[self.e_kn_ag])
        e_k2 = self.dma("Y", asw[0:H, 130:131], d["cc_kn_o"][H:2 * H, :],
                        deps=[self.e_kn_ag])
        e_z3 = self.memset(asw[:, 131:132], 0.0, deps=[self.e_kn_ag])
        e_k3 = self.tt(asw[0:H, 131:132], asw[0:H, 129:130],
                       asw[0:H, 130:131], OP.max, deps=[e_z3, e_k1, e_k2])
        e_kg = self.part_reduce(asw[:, 131:132], 21, OP.max, e_k3)
        e_q6 = self.tt(self.smallf[0:1, 22:23], self.smallf[0:1, 20:21],
                       self.smallf[0:1, 21:22], OP.mult, deps=[e_qg, e_kg])
        e_q7 = self.act(self.smallf[0:1, 20:21], self.smallf[0:1, 22:23], AF.Sqrt,
                        deps=[e_q6])
        self.e_negc = self.bcast_col(20, 21, e_q7, scale=-0.125)
        e_bias = None
        for idx in range(16):
            e_bias = self.stt(self.smallf[:, 40 + idx:41 + idx],
                              self.selmb[:, idx:idx + 1], -1e9,
                              self.smallf[:, 21:22], OP.mult, OP.add,
                              deps=[self.e_negc])
        ebl = [e_bias]
        for g in range(2):
            for P in range(2):
                c = 84 + 2 * g + P
                ebl.append(self.dma("Y", self.smallf[0:48, c:c + 1],
                                    self.smallf[0:48, 41 + 8 * g + 4 * P:
                                                42 + 8 * g + 4 * P],
                                    deps=[e_bias]))
                ebl.append(self.dma("Y", self.smallf[48:96, c:c + 1],
                                    self.smallf[0:48, 43 + 8 * g + 4 * P:
                                                44 + 8 * g + 4 * P],
                                    deps=[e_bias]))
        self.e_bias = ebl

    def phase_attention(self):
        p, d = self.p, self.d
        p.barrier()
        self.attnT = p.alloc("attnT", (128, 8 * TPC), BF16)
        pbuf = p.alloc("pbuf", (128, 6 * 1024), BF16)
        onesall = p.alloc("onesall", (128, 128), BF16)
        e_oa = self.memset(onesall[:, :], 1.0)
        rb = p.alloc("rbuf", (128, 1024), BF16)

        chunks = []
        for ncl in range(K_CL):
            chunks.append((NBLK * ncl, 2 * ncl, 128, None))
            chunks.append((NBLK * ncl + 128, 2 * ncl + 1, 128, ("m", 2 * ncl)))
        for P in range(2):
            chunks.append((1536 + 96 * P, 8 + P, 96, ("p", P)))
        NCH = len(chunks)

        deps0 = self.e_ksel + [self.e_bias, e_oa, self.e_qT]
        e_exp = {}
        e_avh = {}
        e_attn_ev = None
        e_pt0_prev = None
        e_rbm0_p = None
        e_rbm1_p = None
        e_rb = {0: None, 1: None}
        first = True
        for i in range(8):
            for g in range(2):
                e_av = {}
                avst = {"S": None, "prev": e_attn_ev}

                def av_emit(cc, i=i):
                    col0c, strip, nc_, _m = chunks[cc]
                    base0 = 1288 * strip + 161 * i
                    for sub in range(2):
                        pbi = (2 * cc + sub) % 6
                        pb = pbuf[:, 1024 * pbi:1024 * pbi + 1024]
                        e_ex = e_exp[(sub, cc)]
                        for qc in range(2):
                            dep = [e_ex]
                            if cc == 0:
                                dep.append(avst["prev"])
                            if sub == 0:
                                e_av[0] = self.mm(
                                    self.psC[0:65, 512 * qc:512 * (qc + 1)],
                                    self.vsel[0:nc_, base0:base0 + 65],
                                    pb[0:nc_, 512 * qc:512 * (qc + 1)],
                                    start=(cc == 0), stop=(cc == NCH - 1),
                                    deps=dep if qc == 0 else (), mark=(qc == 1))
                            else:
                                e_av[1] = self.mm(
                                    self.psD[64:128, 512 * qc:512 * (qc + 1)],
                                    self.vsel[0:nc_, base0 + 97:base0 + 161],
                                    pb[0:nc_, 512 * qc:512 * (qc + 1)],
                                    start=(cc == 0), stop=(cc == NCH - 1),
                                    deps=dep if qc == 0 else (), tp=(0, 64),
                                    mark=(qc == 1))
                                avst["S"] = self.mm(
                                    self.psD[0:1, 512 * qc:512 * (qc + 1)],
                                    self.vsel[0:nc_, base0 + 65:base0 + 66],
                                    pb[0:nc_, 512 * qc:512 * (qc + 1)],
                                    start=(cc == 0), stop=(cc == NCH - 1),
                                    deps=(), tp=(0, 0), mark=(qc == 1))
                        e_avh[pbi] = ([e_av[sub], avst["S"]] if sub
                                      else e_av[0])

                for c, (col0, strip, n, bspec) in enumerate(chunks):
                    pts = (self.psA, self.psB)
                    e_log = {}
                    for sub in range(2):
                        p0 = 64 * sub
                        ee = None
                        for qc in range(2):
                            dep = []
                            if first:
                                dep = list(deps0)
                            if (sub, (c - 1)) in e_exp:
                                dep.append(e_exp[(sub, c - 1)])
                            if c == 0:
                                dep.append(e_rb[sub])
                            ee = self.mm(
                                pts[sub][0:n, 512 * qc:512 * (qc + 1)],
                                self.kselT[p0:p0 + 64,
                                           NSEL * i + col0:NSEL * i + col0 + n],
                                self.qT[p0:p0 + 64,
                                        TPC * i + 1024 * g + 512 * qc:
                                        TPC * i + 1024 * g + 512 * (qc + 1)],
                                start=True, stop=True, deps=dep, mark=True)
                        first = False
                        e_log[sub] = ee
                    if c >= 1:
                        av_emit(c - 1)
                    if bspec is None:
                        bcol = self.smallf[:, 21:22]
                    elif bspec[0] == "m":
                        bcol = self.smallf[:, 40 + 8 * g + bspec[1]:
                                           41 + 8 * g + bspec[1]]
                    else:
                        bcol = self.smallf[:, 84 + 2 * g + bspec[1]:
                                           85 + 2 * g + bspec[1]]
                    for sub in range(2):
                        pbi = (2 * c + sub) % 6
                        pb = pbuf[:, 1024 * pbi:1024 * pbi + 1024]
                        e_ex = self.act(pb[0:n, :], pts[sub][0:n, 0:1024], AF.Exp,
                                        bias=bcol[0:n, :], scale=0.125,
                                        deps=[e_log[sub], e_avh.get(pbi)])
                        e_exp[(sub, c)] = e_ex
                av_emit(NCH - 1)
                e_avS = avst["S"]
                e_r0 = self.recip(rb[64:65, 0:1024], self.psC[64:65, 0:1024],
                                  deps=[e_av[0], e_rbm0_p], lp=True)
                e_r1 = self.recip(rb[0:1, 0:1024], self.psD[0:1, 0:1024],
                                  deps=[e_avS, e_av[1], e_rbm1_p], lp=True)
                e_rbm0 = None
                e_rbm1 = None
                for qc in range(2):
                    e_rbm0 = self.mm(self.psD[0:64, 512 * qc:512 * (qc + 1)],
                                     onesall[64:65, 0:64],
                                     rb[64:65, 512 * qc:512 * (qc + 1)],
                                     start=True, stop=True,
                                     deps=[e_r0, e_r1, e_pt0_prev]
                                     if qc == 0 else (),
                                     mark=(qc == 1))
                    e_rbm1 = self.mm(self.psC[64:128, 512 * qc:512 * (qc + 1)],
                                     onesall[0:1, 0:64],
                                     rb[0:1, 512 * qc:512 * (qc + 1)],
                                     start=True, stop=True,
                                     deps=[e_r0, e_attn_ev] if qc == 0 else (),
                                     tp=(0, 64), mark=(qc == 1))
                e_rbm0_p = e_rbm0
                e_rbm1_p = e_rbm1
                e_rb[0] = e_exp[(0, NCH - 1)]
                e_rb[1] = e_exp[(1, NCH - 1)]
                e_c0 = self.copy(rb[0:64, 0:1024], self.psD[0:64, 0:1024],
                                 deps=[e_rbm0, e_rbm1])
                e_c1 = self.copy(rb[64:128, 0:1024], self.psC[64:128, 0:1024],
                                 deps=[e_rbm1, e_rbm0])
                e_t0 = self.tt(self.attnT[0:64,
                                          TPC * i + 1024 * g:TPC * i + 1024 * (g + 1)],
                               self.psC[0:64, 0:1024], rb[0:64, 0:1024],
                               OP.mult, deps=[e_c0])
                e_pt0_prev = e_t0
                e_attn_ev = self.tt(
                    self.attnT[64:128,
                               TPC * i + 1024 * g:TPC * i + 1024 * (g + 1)],
                    self.psD[64:128, 0:1024], rb[64:128, 0:1024],
                    OP.mult, deps=[e_c1, e_t0])
                e_exp = {}
        self.e_attn = e_attn_ev
        self.e_attn = e_attn_ev
        p.free("rbuf", "onesall", "pbuf")

    def phase_proj(self):
        p, d = self.p, self.d
        p.barrier()
        wps = p.alloc("wps", (128, 8 * D), BF16)
        bpb = p.alloc("bpb", (128, D), F32)
        e_bp = self.dma("Y", bpb[:, :], d["bproj_bc"][:, :])
        wpt = d["wproj"].ap().rearrange("(n p) m -> n p m", p=128)
        for i in range(8):
            self.dma("G", wps[:, D * i:D * (i + 1)], wpt[i])
        e_wp = self.p.dma_all("G")
        prx = p.alloc("prx", (128, 2048), F32)
        e_bk = {0: None, 1: None}
        e_tr = {}
        e_gr = {}
        e_pw = {}
        e_xd = {}
        for t in range(NT):
            o = 1024 * (t % 2)
            e_xl = self.dma("Y", prx[:, o:o + D],
                            d["park_x"][128 * t:128 * (t + 1), :],
                            deps=[self.e_parkx, e_pw.get(t % 2)])
            e_gl = self.dma("G", self.scrb[:, o:o + D],
                            d["park_g0"][128 * t:128 * (t + 1), :],
                            deps=[self.e_g0, e_gr.get((t % 2, 1))])
            e_x1 = {}
            for half in range(2):
                bank = self.psA[:, 0:512] if half == 0 else self.psA[:, 512:1024]
                ee = None
                for j in range(8):
                    ee = self.mm(bank,
                                 self.attnT[:, TPC * j + 128 * t:TPC * j + 128 * (t + 1)],
                                 wps[:, D * j + 512 * half:D * j + 512 * (half + 1)],
                                 start=(j == 0), stop=(j == 7),
                                 deps=[self.e_attn, e_wp, e_bp, e_bk[half]]
                                 if j == 0 else (), mark=(j == 7))
                tmp = self.scr[:, o + 512 * half:o + 512 * (half + 1)]
                e_s0 = self.stt(tmp, bank, 1.0,
                                bpb[:, 512 * half:512 * (half + 1)],
                                OP.mult, OP.add,
                                deps=[ee, e_tr.get((t % 2, half))])
                e_bk[half] = e_s0
                e_t2 = self.tt(tmp, tmp,
                               self.scrb[:, o + 512 * half:o + 512 * (half + 1)],
                               OP.mult, deps=[e_s0, e_gl])
                e_gr[(t % 2, half)] = e_t2
                e_x1[half] = self.tt(prx[:, o + 512 * half:o + 512 * (half + 1)],
                                     prx[:, o + 512 * half:o + 512 * (half + 1)],
                                     tmp, OP.add, deps=[e_t2, e_xl])
                e_tr[(t % 2, half)] = e_x1[half]
            e_pw[t % 2] = self.dma("YS"[t % 2],
                                   d["park_x"][128 * t:128 * (t + 1), :],
                                   prx[:, o:o + D], deps=[e_x1[0], e_x1[1]])
        self.e_x1w = self.p.dma_all("Y") + self.p.dma_all("S")
        p.free("prx", "bpb", "wps")

    def phase_l1_mixer(self, e_r1):
        p, d = self.p, self.d
        e_a1 = self.act(self.smallf[:, 24:32], self.fm8[:, 16:24], AF.Exp,
                        deps=[e_r1])
        e_a2 = self.act(self.smallf[:, 24:32], self.smallf[:, 24:32], AF.Ln,
                        bias=1.0, deps=[e_a1])
        e_a = self.ts(self.smallf[:, 24:32], self.smallf[:, 24:32], -1.0, None,
                      OP.mult, deps=[e_a2])

        self.phase_xnT(e_r1)
        p.barrier()
        self.xsF = p.alloc("xsF", (128, 8 * TPC), BF16)
        zf32 = self.zbf
        e_z = self.memset(zf32[:, :], 0.0)
        abc = p.alloc("abc", (128, 2048), F32)
        hmid = p.alloc("hmid", (128, 8), F32)
        bmsb = p.alloc("bmsb", (128, 8 * D), BF16)
        bmt = d["bmatT"].ap().rearrange("(n p) m -> n p m", p=128)
        for i in range(8):
            self.dma("G", bmsb[:, D * i:D * (i + 1)], bmt[i])
        e_bm = self.p.dma_all("G")

        e_am = None
        for t in range(NT):
            buf = self.scr[:, 0:D] if t % 2 == 0 else self.scr[:, D:2 * D]
            e1 = self.ts(buf, self.xsb[:, D * t:D * (t + 1)],
                         self.stats[:, t:t + 1], self.stats[:, 16 + t:17 + t],
                         OP.subtract, OP.mult, deps=[e_r1, e_am])
            e_am = self.red(self.smallf[:, 40 + t:41 + t], buf, OP.max,
                            deps=[e1], absv=True)
        e_am2 = self.red(self.smallf[:, 16:17], self.smallf[:, 40:56], OP.max,
                         deps=[e_am])
        e_am3 = self.part_reduce(self.smallf[:, 16:17], 0, OP.max, e_am2)
        e_ar = self.ar_scalar("sx", 0, 1, OP.max, e_am3)

        e_scan = {}
        e_hl = None
        for nch in range(8):
            e_bu = None
            for tc in range(4):
                pt = self.psA if tc % 2 == 0 else self.psB
                bank = pt[:, 0:512] if tc < 2 else pt[:, 512:1024]
                ee = None
                for i in range(8):
                    ee = self.mm(bank,
                                 bmsb[:, D * i + 128 * nch:D * i + 128 * (nch + 1)],
                                 self.xnT[:, TPC * i + 512 * tc:TPC * i + 512 * (tc + 1)],
                                 start=(i == 0), stop=(i == 7),
                                 deps=[self.e_xnT, e_bm, e_scan.get(nch - 1)]
                                 if i == 0 else (), mark=(i == 7))
                e_bu = self.copy(self.scr[:, 512 * tc:512 * (tc + 1)], bank,
                                 deps=[ee])
            e_ab = self.ts(abc[:, :], zf32[:, :], self.smallf[:, 24 + nch:25 + nch],
                           None, OP.add, deps=[e_z, e_a, e_bu])
            e_sc1 = self.scan(self.xsF[:, TPC * nch:TPC * (nch + 1)], abc[:, :],
                              self.scr[:, 0:2048], self.fm8[:, 24 + nch:25 + nch],
                              deps=[e_ab])
            e_scan[nch] = e_sc1
            e_hl = self.copy(hmid[:, nch:nch + 1],
                             self.xsF[:, TPC * nch + 2047:TPC * nch + 2048],
                             deps=[e_sc1])
        e_hd = self.dma("G", d["cc_hm_i"][:, :], hmid[:, :], deps=[e_hl])
        e_hc = self.coll("AllGather", OP.bypass, [d["cc_hm_i"].ap().opt()],
                         [d["cc_hm_o"].ap().opt()], PAIRS, deps=[e_hd])
        e_hr = self.dma("G", hmid[:, :], d["cc_hm_o"][0:128, :], deps=[e_hc])
        e_hm = self.tt(hmid[:, :], hmid[:, :], self.fm8[:, 32:40], OP.mult,
                       deps=[e_hr])
        for nch in range(8):
            e_ab = self.ts(abc[:, :], zf32[:, :], self.smallf[:, 24 + nch:25 + nch],
                           None, OP.add, deps=[e_hm])
            e_c1 = self.scan(self.scr[:, 0:2048], abc[:, :], zf32[:, :],
                             hmid[:, nch:nch + 1], deps=[e_ab])
            e_scan[nch] = self.tt(self.xsF[:, TPC * nch:TPC * (nch + 1)],
                                  self.xsF[:, TPC * nch:TPC * (nch + 1)],
                                  self.scr[:, 0:2048], OP.add, deps=[e_c1])
        self.e_xs = e_scan[7]
        e_s = self.ts(self.smallf[0:1, 2:3], self.smallf[0:1, 1:2], 1.0 / QMAX1,
                      1e-12, OP.mult, OP.max, deps=[e_ar])
        e_r = self.recip(self.smallf[0:1, 3:4], self.smallf[0:1, 2:3], deps=[e_s])
        e_rc = self.bcast_col(3, 4, e_r)
        self.e_sxcols = self.bcast_col(2, 5, e_rc)
        p.free("bmsb", "hmid", "abc")

    def phase_l1_cd_setup(self):
        p, d = self.p, self.d
        cmsb = p.alloc("cmsb", (128, 8 * D), BF16)
        dmsb = p.alloc("dmsb", (128, 8 * D), BF16)
        lnb = p.alloc("lnb", (128, 2 * D), F32)
        cmt = d["cmatT"].ap().rearrange("(n p) m -> n p m", p=128)
        dmt = d["dmatT"].ap().rearrange("(n p) m -> n p m", p=128)
        for i in range(8):
            self.dma("G", cmsb[:, D * i:D * (i + 1)], cmt[i])
            self.dma("G", dmsb[:, D * i:D * (i + 1)], dmt[i])
        self.cd = dict(cmsb=cmsb, dmsb=dmsb, lnb=lnb,
                       e_wc=self.p.dma_all("G"),
                       e_lg=self.dma("Y", lnb[:, 0:D], d["lng_bc"][:, :]),
                       e_lb=self.dma("Y", lnb[:, D:2 * D], d["lnb_bc"][:, :]),
                       e_prev={0: None, 1: None})

    def phase_l1_cd(self, t0, t1):
        p, d = self.p, self.d
        cmsb, dmsb, lnb = self.cd["cmsb"], self.cd["dmsb"], self.cd["lnb"]
        e_wc, e_lg, e_lb = self.cd["e_wc"], self.cd["e_lg"], self.cd["e_lb"]
        e_prev = self.cd["e_prev"]
        e_xs = self.e_xs
        for t in range(t0, t1):
            for half in range(2):
                bank = self.psC[:, 0:512] if half == 0 else self.psC[:, 512:1024]
                ee = None
                for i in range(8):
                    ee = self.mm(bank,
                                 self.xsF[:, TPC * i + 128 * t:TPC * i + 128 * (t + 1)],
                                 cmsb[:, D * i + 512 * half:D * i + 512 * (half + 1)],
                                 start=(i == 0), stop=False,
                                 deps=[e_xs, e_wc, e_prev[half]] if i == 0 else (),
                                 mark=False)
                for i in range(8):
                    ee = self.mm(bank,
                                 self.xnT[:, TPC * i + 128 * t:TPC * i + 128 * (t + 1)],
                                 dmsb[:, D * i + 512 * half:D * i + 512 * (half + 1)],
                                 start=False, stop=(i == 7),
                                 deps=(), mark=(i == 7))
                e_prev[half] = self.act(self.scr[:, 512 * half:512 * (half + 1)],
                                        bank, AF.Gelu, deps=[ee])
            e_m = self.red(self.smallf[:, 40:41], self.scr[:, 0:D], OP.add,
                           deps=[e_prev[0], e_prev[1]])
            e_m2 = self.ts(self.smallf[:, 40:41], self.smallf[:, 40:41], 1.0 / D,
                           None, OP.mult, deps=[e_m])
            e_sq = self.act(self.scrb[:, 0:D], self.scr[:, 0:D], AF.Square,
                            deps=[e_m2])
            e_q = self.red(self.smallf[:, 41:42], self.scrb[:, 0:D], OP.add,
                           deps=[e_sq])
            e_q2 = self.ts(self.smallf[:, 41:42], self.smallf[:, 41:42], 1.0 / D,
                           None, OP.mult, deps=[e_q])
            e_v1 = self.tt(self.smallf[:, 42:43], self.smallf[:, 40:41],
                           self.smallf[:, 40:41], OP.mult, deps=[e_q2])
            e_v2 = self.tt(self.smallf[:, 42:43], self.smallf[:, 41:42],
                           self.smallf[:, 42:43], OP.subtract, deps=[e_v1])
            e_sd = self.act(self.smallf[:, 43:44], self.smallf[:, 42:43], AF.Sqrt,
                            bias=LN_EPS, deps=[e_v2])
            e_rr = self.recip(self.smallf[:, 44:45], self.smallf[:, 43:44],
                              deps=[e_sd])
            e_n1 = self.ts(self.scr[:, 0:D], self.scr[:, 0:D],
                           self.smallf[:, 40:41], self.smallf[:, 44:45],
                           OP.subtract, OP.mult, deps=[e_rr])
            e_n2 = self.tt(self.scr[:, 0:D], self.scr[:, 0:D], lnb[:, 0:D],
                           OP.mult, deps=[e_n1, e_lg])
            e_n3 = self.tt(self.scr[:, 0:D], self.scr[:, 0:D], lnb[:, D:2 * D],
                           OP.add, deps=[e_n2, e_lb])
            e_pk = self.dma("YG"[t % 2], d["park_s"][128 * t:128 * (t + 1), :],
                            self.scr[:, 0:D], deps=[e_n3])
            e_prev[0] = e_pk
            e_prev[1] = e_pk
        if t1 == NT:
            self.e_s_parked = self.p.dma_all("Y") + self.p.dma_all("G")

    def phase_xq(self):
        p = self.p
        self.xqT = p.alloc("xqT", (128, 8 * TPC), BF16)
        e_tp = {}
        for t in range(NT):
            xt = self.xsb[:, D * t:D * (t + 1)]
            buf = self.scr[:, 0:D] if (t % 2 == 0) else self.scr[:, D:2 * D]
            dep = [self.e_r1b]
            if t >= 2:
                dep.append(e_tp[t - 2]["tp"])
            e1 = self.ts(buf, xt, self.stats[:, t:t + 1],
                         self.stats[:, 16 + t:17 + t], OP.subtract, OP.mult,
                         deps=dep)
            pt = self.psA if (t % 2 == 0) else self.psB
            e_tpl = None
            for i in range(8):
                dep2 = [e1]
                if t >= 2:
                    dep2.append(e_tp[t - 2]["ev"])
                e_tpl = self.transp(pt[:, 128 * i:128 * (i + 1)],
                                    buf[:, 128 * i:128 * (i + 1)],
                                    self.idf[:, :], deps=dep2 if i == 0 else (),
                                    mark=(i == 7))
            e_ev = None
            for i in range(8):
                e_ev = self.act(buf[:, 128 * i:128 * (i + 1)],
                                pt[:, 128 * i:128 * (i + 1)], AF.Identity,
                                bias=MAGIC, scale=self.smallf[:, 4:5],
                                deps=[e_tpl, self.e_sxcols])
            e_rn = None
            for i in range(8):
                e_rn = self.ts(self.xqT[:, TPC * i + 128 * t:TPC * i + 128 * (t + 1)],
                               buf[:, 128 * i:128 * (i + 1)], MAGIC, None,
                               OP.subtract, deps=[e_ev])
            e_tp[t] = {"tp": e_tpl, "ev": e_rn}
        self.e_xqT = self.p.now("V")
        self.p.free("xsb2")

    def phase_l1_p1(self):
        p, d = self.p, self.d
        self.l1w = p.alloc("l1w", (128, 8192), F32)
        l1w = self.l1w
        wq = p.alloc("wqT", (128, 8 * DI), BF16)
        self.wqT = wq
        e_amax = {}
        e_parked = {0: None, 1: None}
        e_pv = {}
        ccw, park = d["cc_wu_o"], d["park_up"]
        for i in range(8):
            self.dmat("Y", wq[:, DI * i:DI * (i + 1)],
                      ccw[:, 128 * i:128 * (i + 1)],
                      deps=[self.e_wq_ag["wu"]])
        e_w = self.p.dma_all("Y")
        e_am = self.memset(self.smallf[:, 46:47], 0.0)
        for R in range(24):
            buf = self.scr[:, 0:2048] if R % 2 == 0 else l1w[:, 0:2048]
            ee4 = [None] * 4
            for i in range(8):
                for tc in range(4):
                    pt = (self.psA, self.psB, self.psC, self.psD)[tc]
                    bank = pt[:, 0:512]
                    ee4[tc] = self.mm(
                        bank,
                        wq[:, DI * i + 128 * R:DI * i + 128 * (R + 1)],
                        self.xqT[:, TPC * i + 512 * tc:TPC * i + 512 * (tc + 1)],
                        start=(i == 0), stop=(i == 7),
                        deps=[self.e_xqT, e_w, e_pv.get(tc)]
                        if i == 0 else (), mark=(i == 7))
            for tc in range(4):
                pt = (self.psA, self.psB, self.psC, self.psD)[tc]
                e_c = self.act(buf[:, 512 * tc:512 * (tc + 1)],
                               pt[:, 0:512], AF.Identity,
                               deps=[ee4[tc], e_parked[R % 2]])
                e_pv[tc] = e_c
            e_r = self.red(self.smallf[:, 48:49], buf, OP.max,
                           deps=[e_pv[0], e_pv[1], e_pv[2], e_pv[3]], absv=True)
            e_mx = self.tt(self.smallf[:, 46:47],
                           self.smallf[:, 46:47],
                           self.smallf[:, 48:49], OP.max, deps=[e_r, e_am])
            e_pk = self.dma("YG"[R % 2], park[128 * R:128 * (R + 1), :],
                            buf, deps=[e_mx])
            e_parked[R % 2] = e_pk
            e_am = e_mx
        e_amax[0] = e_am
        e_amax["park0"] = e_pk
        e_up_last = list(ee4)
        for i in range(8):
            self.dmat("Y", wq[:, DI * i:DI * (i + 1)],
                      d["cc_wg_o"][:, 128 * i:128 * (i + 1)],
                      deps=[self.e_wq_ag["wg"], e_up_last])
        e_w = self.p.dma_all("Y")
        self.ff = dict(e_amax=e_amax, e_wg=e_w,
                       e_up_park=[e_parked[0], e_parked[1]])
        e_am = self.memset(self.smallf[:, 47:48], 0.0)
        e_acc = {}
        for R in range(24):
            ee4 = [None] * 4
            for i in range(8):
                for tc in range(4):
                    pt = (self.psA, self.psB, self.psC, self.psD)[tc]
                    bank = pt[:, 0:512]
                    ee4[tc] = self.mm(
                        bank,
                        wq[:, DI * i + 128 * R:DI * i + 128 * (R + 1)],
                        self.xqT[:, TPC * i + 512 * tc:TPC * i + 512 * (tc + 1)],
                        start=(i == 0), stop=(i == 7),
                        deps=[e_w, e_pv.get(tc)]
                        if i == 0 else (), mark=(i == 7))
            for tc in range(4):
                pt = (self.psA, self.psB, self.psC, self.psD)[tc]
                e_rr = self.red(self.smallf[:, 41 + tc:42 + tc], pt[:, 0:512],
                                OP.max, deps=[ee4[tc], e_acc.get(tc)], absv=True)
                e_pv[tc] = e_rr
                e_acc[tc] = self.tt(self.smallf[:, 47:48], self.smallf[:, 47:48],
                                    self.smallf[:, 41 + tc:42 + tc], OP.max,
                                    deps=[e_rr, e_am])
                e_am = e_acc[tc]
        e_amax[1] = e_am
        self.ff["e_pv_p1"] = dict(e_pv)
        self.ff["e_amax"] = e_amax

    def phase_l1_wqfree(self):
        self.p.free("wqT")

    def phase_l1_ugar(self):
        d, e_amax = self.d, self.ff["e_amax"]
        e_u1 = self.part_reduce(self.smallf[:, 46:47], 24, OP.max, e_amax[0])
        e_g1 = self.part_reduce(self.smallf[:, 47:48], 25, OP.max, e_amax[1])
        e_p0 = self.copy(self.smallf[0:1, 56:58], self.smallf[0:1, 24:26],
                         deps=[e_u1, e_g1])
        e_p1 = self.dma("G", d["cc_ug_i"][0:1, 0:8], self.smallf[0:1, 56:64],
                        deps=[e_p0])
        e_c1 = self.coll("AllReduce", OP.max, [d["cc_ug_i"].ap().opt()],
                         [d["cc_ug_o"].ap().opt()], ALL8, deps=[e_p1])
        self.ff["e_ug_r"] = self.dma("G", self.smallf[0:1, 26:28],
                                     d["cc_ug_o"][0:1, 0:2], deps=[e_c1])

    def phase_l1_p2(self):
        p, d = self.p, self.d
        l1w, e_amax = self.l1w, self.ff["e_amax"]
        e_f1 = self.ts(self.smallf[0:1, 26:27], self.smallf[0:1, 26:27],
                       self.smallf[0:1, 2:3], None, OP.mult,
                       deps=[self.ff["e_ug_r"]])
        e_f2 = self.ts(self.smallf[0:1, 27:28], self.smallf[0:1, 27:28],
                       self.smallf[0:1, 2:3], None, OP.mult, deps=[e_f1])
        e_su = self.ts(self.smallf[0:1, 28:29], self.smallf[0:1, 26:27],
                       1.0 / QMAX1, 1e-12, OP.mult, OP.max, deps=[e_f2])
        e_sg = self.ts(self.smallf[0:1, 30:31], self.smallf[0:1, 27:28],
                       1.0 / QMAX1, 1e-12, OP.mult, OP.max, deps=[e_su])
        e_ru = self.recip(self.smallf[0:1, 29:30], self.smallf[0:1, 28:29],
                          deps=[e_sg])
        e_rg = self.recip(self.smallf[0:1, 31:32], self.smallf[0:1, 30:31],
                          deps=[e_ru])
        e_ru2 = self.ts(self.smallf[0:1, 29:30], self.smallf[0:1, 29:30],
                        self.smallf[0:1, 2:3], None, OP.mult, deps=[e_rg])
        e_rg2 = self.ts(self.smallf[0:1, 31:32], self.smallf[0:1, 31:32],
                        self.smallf[0:1, 2:3], None, OP.mult, deps=[e_ru2])
        self.ff["e_ruc"] = self.bcast_col(29, 33, e_rg2)
        e_rgc = self.bcast_col(31, 34, self.ff["e_ruc"])
        e_sgc = self.bcast_col(30, 35, e_rgc)

        e_am0 = self.memset(self.smallf[:, 49:50], 0.0, deps=[e_rgc])
        st = {"am": e_am0, "pv": {0: self.ff["e_up_park"],
                                  1: self.ff["e_up_park"]},
              "t2": {0: None, 1: None}, "mm": {}, "ev": {}, "h": {}}
        st["ev"][-1] = self.ff["e_pv_p1"]
        wq = self.wqT
        banks = (self.psA, self.psB, self.psC, self.psD)

        def p2_slots(R):
            return (l1w[:, 2048 * (R % 2):2048 * (R % 2) + 2048],
                    l1w[:, 4096 + 2048 * (R % 2):4096 + 2048 * (R % 2) + 2048])

        def p2_mm(R):
            ee4 = [None] * 4
            for i in range(8):
                for tc in range(4):
                    ee4[tc] = self.mm(
                        banks[tc][:, 0:512],
                        wq[:, DI * i + 128 * R:DI * i + 128 * (R + 1)],
                        self.xqT[:, TPC * i + 512 * tc:TPC * i + 512 * (tc + 1)],
                        start=(i == 0), stop=(i == 7),
                        deps=[st["ev"][R - 1][tc]] if i == 0 else (),
                        mark=(i == 7))
            st["mm"][R] = ee4

        def p2_head(R):
            gi, t2 = p2_slots(R)
            evs = []
            for tc in range(4):
                evs.append(self.act(gi[:, 512 * tc:512 * (tc + 1)],
                                    banks[tc][:, 0:512],
                                    AF.Identity, bias=MAGIC,
                                    scale=self.smallf[:, 34:35],
                                    deps=[st["mm"][R][tc], e_sgc,
                                          st["pv"][R % 2]]))
            st["ev"][R] = evs
            e_2 = self.ts(gi, gi, MAGIC, None, OP.subtract, deps=[evs[3]])
            e_3 = self.act(t2, gi, AF.Sigmoid,
                           scale=self.smallf[:, 35:36],
                           deps=[e_2, st["t2"][R % 2]])
            st["h"][R] = self.act(t2, t2,
                                  AF.Identity, bias=MAGIC, scale=65536.0,
                                  deps=[e_3])

        def p2_tail(R):
            gi, t2 = p2_slots(R)
            e_5 = self.ts(t2, t2, MAGIC,
                          65535.0, OP.subtract, OP.min, deps=[st["h"][R]])
            e_6 = self.tt(gi, gi, t2, OP.mult, deps=[e_5])
            st["t2"][R % 2] = e_6
            e_7 = self.red(self.smallf[:, 48:49], gi, OP.max,
                           deps=[e_6], absv=True)
            e_8 = self.tt(self.smallf[:, 49:50], self.smallf[:, 49:50],
                          self.smallf[:, 48:49], OP.max, deps=[e_7, st["am"]])
            st["am"] = e_8
            st["pv"][R % 2] = self.dma("GY"[R % 2],
                                       d["park_g2"][128 * R:128 * (R + 1), :],
                                       gi, deps=[e_8])

        p2_mm(0)
        for R in range(24):
            if R >= 1:
                p2_tail(R - 1)
            p2_head(R)
            if R + 1 < 24:
                p2_mm(R + 1)
        p2_tail(23)
        self.ff["e_g2park"] = [st["pv"][0], st["pv"][1]]
        self.ff["e_t2f"] = [st["t2"][0], st["t2"][1]]
        self.ff["e_g2am"] = st["am"]

    def phase_l1_g2ar(self):
        e_g2m = self.part_reduce(self.smallf[:, 49:50], 26, OP.max,
                                 self.ff["e_g2am"])
        self.ff["e_g2r"] = self.ar_scalar("g2", 26, 27, OP.max, e_g2m)

    def phase_l1_p3(self):
        p, d = self.p, self.d
        l1w, e_amax = self.l1w, self.ff["e_amax"]
        e_ruc = self.ff["e_ruc"]
        e_c2 = self.tt(self.smallf[0:1, 36:37], self.smallf[0:1, 27:28],
                       self.smallf[0:1, 30:31], OP.mult, deps=[self.ff["e_g2r"]])
        e_c3 = self.ts(self.smallf[0:1, 36:37], self.smallf[0:1, 36:37],
                       2.0 ** -16 / QMAX1, 1e-12, OP.mult, OP.max, deps=[e_c2])
        e_c4 = self.recip(self.smallf[0:1, 37:38], self.smallf[0:1, 36:37],
                          deps=[e_c3])
        e_c5 = self.tt(self.smallf[0:1, 38:39], self.smallf[0:1, 37:38],
                       self.smallf[0:1, 30:31], OP.mult, deps=[e_c4])
        e_c6 = self.ts(self.smallf[0:1, 38:39], self.smallf[0:1, 38:39],
                       2.0 ** -16, None, OP.mult, deps=[e_c5])
        e_cc = self.bcast_col(38, 39, e_c6)

        e_am0 = self.memset(self.smallf[:, 50:51], 0.0, deps=[e_cc])
        e_pv2 = e_amax["park0"]
        st = {"am": e_am0, "pv": {0: self.ff["e_g2park"], 1: self.ff["e_g2park"]},
              "ui": {0: self.ff["e_t2f"][0], 1: self.ff["e_t2f"][1]},
              "l1": {}, "l2": {}, "h": {}}

        def p3_slots(R):
            return (l1w[:, 2048 * (R % 2):2048 * (R % 2) + 2048],
                    l1w[:, 4096 + 2048 * (R % 2):4096 + 2048 * (R % 2) + 2048])

        def p3_load(R):
            gi, ui = p3_slots(R)
            st["l1"][R] = self.dma("YG"[R % 2], gi,
                                   d["park_g2"][128 * R:128 * (R + 1), :],
                                   deps=[st["pv"][R % 2]])
            st["l2"][R] = self.dma("S", ui,
                                   d["park_up"][128 * R:128 * (R + 1), :],
                                   deps=[st["ui"][R % 2], e_pv2])

        def p3_head(R):
            gi, ui = p3_slots(R)
            e_1 = self.act(gi, gi,
                           AF.Identity, bias=MAGIC, scale=self.smallf[:, 39:40],
                           deps=[st["l1"][R], e_cc])
            e_3 = self.act(ui, ui,
                           AF.Identity, bias=MAGIC, scale=self.smallf[:, 33:34],
                           deps=[st["l2"][R], e_ruc])
            e_2 = self.ts(gi, gi, MAGIC, None, OP.subtract, deps=[e_1])
            e_4 = self.ts(ui, ui, MAGIC, None, OP.subtract, deps=[e_3])
            st["h"][R] = (e_2, e_4)

        def p3_tail(R):
            gi, ui = p3_slots(R)
            e_5 = self.tt(gi, gi, ui, OP.mult, deps=list(st["h"][R]))
            st["ui"][R % 2] = e_5
            e_6 = self.red(self.smallf[:, 48:49], gi, OP.max,
                           deps=[e_5], absv=True)
            e_7 = self.tt(self.smallf[:, 50:51], self.smallf[:, 50:51],
                          self.smallf[:, 48:49], OP.max, deps=[e_6, st["am"]])
            st["am"] = e_7
            st["pv"][R % 2] = self.dma("GY"[R % 2],
                                       d["park_op"][128 * R:128 * (R + 1), :],
                                       gi, deps=[e_7])

        p3_load(0)
        p3_load(1)
        for R in range(24):
            if R >= 1:
                p3_tail(R - 1)
                if R + 1 < 24:
                    p3_load(R + 1)
            p3_head(R)
        p3_tail(23)
        self.ff["e_oppark"] = [st["pv"][0], st["pv"][1]]
        e_om = self.part_reduce(self.smallf[:, 50:51], 80, OP.max, st["am"])
        self.ff["e_oor"] = self.ar_scalar("oo", 80, 81, OP.max, e_om)

    def phase_l1_p4(self):
        p, d = self.p, self.d
        e_oppark = self.ff["e_oppark"]
        p.free("lnb", "dmsb", "cmsb", "l1w", "xqT", "xsF", "xnT")
        wdq = p.alloc("wdT", (128, 24 * D), BF16)
        for j in range(24):
            self.dmat("S", wdq[:, D * j:D * (j + 1)],
                      d["cc_wd_o"][:, 128 * j:128 * (j + 1)],
                      deps=[self.e_wq_ag["wd"]])
        e_wd = self.p.dma_all("S")
        oiF = p.alloc("oiF", (128, 2 * 24 * 1024), BF16)
        ois = p.alloc("oistg", (128, 2048), F32)
        mrg = p.alloc("mrg", (128, 2048), F32)
        e_o1 = self.tt(self.smallf[0:1, 42:43], self.smallf[0:1, 81:82],
                       self.smallf[0:1, 36:37], OP.mult, deps=[self.ff["e_oor"]])
        e_o2 = self.tt(self.smallf[0:1, 42:43], self.smallf[0:1, 42:43],
                       self.smallf[0:1, 28:29], OP.mult, deps=[e_o1])
        e_o3 = self.ts(self.smallf[0:1, 42:43], self.smallf[0:1, 42:43],
                       1.0 / QMAX1, 1e-12, OP.mult, OP.max, deps=[e_o2])
        e_o4 = self.recip(self.smallf[0:1, 43:44], self.smallf[0:1, 42:43],
                          deps=[e_o3])
        e_o5 = self.tt(self.smallf[0:1, 44:45], self.smallf[0:1, 43:44],
                       self.smallf[0:1, 36:37], OP.mult, deps=[e_o4])
        e_o6 = self.tt(self.smallf[0:1, 44:45], self.smallf[0:1, 44:45],
                       self.smallf[0:1, 28:29], OP.mult, deps=[e_o5])
        e_roc = self.bcast_col(44, 51, e_o6)
        e_soc = self.bcast_col(42, 52, e_roc)

        e_bk = {0: None, 1: None}
        e_ymrg = {0: None, 1: None}
        e_sb = {}
        e_pr = {}
        for hf in range(2):
            of = 24576 * hf
            e_oi = None
            e_oid = {0: None, 1: None}
            e_ld = {}
            e_h = {}

            def oi_load(R, hf=hf, of=of):
                e_ld[R] = self.dma("YG"[R % 2],
                                   ois[:, 1024 * (R % 2):1024 * (R % 2) + 1024],
                                   d["park_op"][128 * R:128 * (R + 1),
                                                1024 * hf:1024 * (hf + 1)],
                                   deps=[e_oid[R % 2], e_oppark])

            oi_load(0)
            oi_load(1)
            for R in range(24):
                o0 = 1024 * (R % 2)
                if R >= 1:
                    op = 1024 * ((R - 1) % 2)
                    e_oi = self.ts(oiF[:, of + 1024 * (R - 1):of + 1024 * R],
                                   ois[:, op:op + 1024], MAGIC, None,
                                   OP.subtract, deps=[e_h[R - 1]])
                    e_oid[(R - 1) % 2] = e_oi
                    if R + 1 < 24:
                        oi_load(R + 1)
                e_h[R] = self.act(ois[:, o0:o0 + 1024],
                                  ois[:, o0:o0 + 1024],
                                  AF.Identity, bias=MAGIC,
                                  scale=self.smallf[:, 51:52],
                                  deps=[e_ld[R], e_roc])
            e_oi = self.ts(oiF[:, of + 1024 * 23:of + 1024 * 24],
                           ois[:, 1024:2048], MAGIC, None,
                           OP.subtract, deps=[e_h[23]])
            e_oid[1] = e_oi
            for tt_ in range(8):
                t = 8 * hf + tt_
                xb = mrg[:, 1024 * (t % 2):1024 * (t % 2) + 1024]
                e_xl = self.dma("S", xb, d["park_x"][128 * t:128 * (t + 1), :],
                                deps=[e_ymrg[t % 2], self.e_x1w])
                for half in range(2):
                    sb = self.scrb[:, 1024 * (t % 2) + 512 * half:
                                   1024 * (t % 2) + 512 * (half + 1)]
                    pr = self.scr[:, 512 * (2 * (t % 2) + half):
                                  512 * (2 * (t % 2) + half) + 512]
                    bank = (self.psA if half == 0 else self.psB)[:, 0:512]
                    ee = None
                    for j in range(24):
                        ee = self.mm(bank,
                                     oiF[:, of + 1024 * j + 128 * tt_:
                                         of + 1024 * j + 128 * (tt_ + 1)],
                                     wdq[:, D * j + 512 * half:
                                         D * j + 512 * (half + 1)],
                                     start=(j == 0), stop=(j == 23),
                                     deps=[e_oi, e_wd, e_bk[half]]
                                     if j == 0 else (), mark=(j == 23))
                    e_g = self.act(sb, bank, AF.Silu, scale=self.smallf[:, 52:53],
                                   deps=[ee, e_soc, e_sb.get((t % 2, half))])
                    e_bk[half] = e_g
                    e_sl = self.dma("YG"[half], pr,
                                    d["park_s"][128 * t:128 * (t + 1),
                                                512 * half:512 * (half + 1)],
                                    deps=[self.e_s_parked, e_oi,
                                          e_pr.get((t % 2, half))])
                    e_t2 = self.tt(pr, pr, sb, OP.mult, deps=[e_g, e_sl])
                    e_sb[(t % 2, half)] = e_t2
                    e_x2 = self.tt(xb[:, 512 * half:512 * (half + 1)],
                                   xb[:, 512 * half:512 * (half + 1)],
                                   pr, OP.add, deps=[e_t2, e_xl])
                    e_pr[(t % 2, half)] = e_x2
                e_ymrg[t % 2] = self.dma("YG"[t % 2],
                                         d["y"][128 * t:128 * (t + 1), :],
                                         xb, deps=[e_pr[(t % 2, 0)],
                                                   e_pr[(t % 2, 1)]])
        p.free("mrg", "oistg", "oiF", "wdT")

    def build_all(self, phase_limit=99):
        self.phase_loads()
        e_r1, e_r2 = self.ln_stats(self.e_x)
        if phase_limit >= 2:
            self.phase_f0_absmax(e_r2)
        if phase_limit >= 3:
            self.qT = self.p.alloc("qT", (128, 8 * TPC), BF16)
            self.phase_xnT(e_r1)
        if phase_limit >= 4:
            self.phase_q()
        if phase_limit >= 1:
            self.phase_wdequant(None)
        if phase_limit >= 5:
            self._scores_impl_setup()
            self._scores_impl()
        if phase_limit >= 6:
            self.phase_kv_kmm()
            self.phase_select()
        if phase_limit >= 7:
            self.phase_kv()
            self.p.free("mksel", "ohlocal", "onehot", "kstore", "wqw2",
                        "kvb", "xnT")
        if phase_limit >= 9:
            self.phase_assemble()
        if phase_limit >= 8:
            self.phase_ffn0()
        if phase_limit >= 9:
            self.phase_parkx()
        if phase_limit >= 10:
            self.phase_attention()
        if phase_limit >= 11:
            self.phase_proj()
            self.p.free("attnT", "asb", "asw", "vsel", "kselT", "qT")
        if phase_limit >= 12:
            self.xsb = self.p.alloc("xsb2", (128, NT * D), F32, side="right")
            for t in range(NT):
                self.dma("YG"[t % 2], self.xsb[:, D * t:D * (t + 1)],
                         self.d["park_x"][128 * t:128 * (t + 1), :],
                         deps=[self.e_x1w])
            e_r1b, _ = self.ln_stats(self.p.dma_all("Y") + self.p.dma_all("G"))
            self.e_r1b = e_r1b
            self.phase_l1_mixer(e_r1b)
        if phase_limit >= 13:
            self.phase_xq()
            self.phase_l1_p1()
            self.phase_l1_ugar()
            self.phase_l1_p2()
            self.phase_l1_wqfree()
            self.phase_l1_cd_setup()
            self.phase_l1_g2ar()
            self.phase_l1_cd(0, 12)
            self.phase_l1_p3()
            self.phase_l1_cd(12, 16)
            self.phase_l1_p4()
        else:
            for t in range(NT):
                self.dma("YG"[t % 2], self.d["y"][128 * t:128 * (t + 1), :],
                         self.xsb[:, D * t:D * (t + 1)],
                         deps=[self.p.now("V"), self.p.now("S"),
                               self.p.now("T"), self.p.dma_all("G")])
        self.p.do("G", lambda eng: eng.engine_nop(),
                  deps=self.p.dma_all("Y") + self.p.dma_all("G") +
                       self.p.dma_all("S"))
        self.p.finalize()
        self.p.stack.close()
        return self.nc

    def _scores_impl_setup(self):
        p, d = self.p, self.d
        self.kvb = p.alloc("kvb", (128, 2 * D), BF16)
        e_kb = self.dma("Y", self.kvb[:, 0:D], d["bk_bc"][:, :])
        e_vb = self.dma("Y", self.kvb[:, D:2 * D], d["bv_bc"][:, :])
        self.e_kvb = (e_kb, e_vb)
        wqw = p.alloc("wqw2", (128, 8 * D), BF16)
        wqt = d["wqkv"].ap().rearrange("(n p) m -> n p m", p=128)
        for i in range(8):
            self.dma("G", wqw[:, D * i:D * (i + 1)], wqt[i][:, D:2 * D])
        self.wqw2 = wqw
        self.e_wk = self.p.dma_all("G")


def build(meta, taps=(), phase_limit=99):
    return Builder(meta, taps).build_all(phase_limit)


def kernel(**inputs):
    in_maps, meta = _host_prep(inputs)
    nc = build(meta)
    res = run_bass_kernel_spmd(nc, in_maps, core_ids=list(range(NCORE)))
    y = np.zeros((B, S, D), dtype=np.float32)
    for c in range(NCORE):
        b_, h_ = c // 2, c % 2
        y[b_, h_ * TPC:(h_ + 1) * TPC] = res.results[c]["y"]
    return y


def locate_instr(p, idx):
    n = 0
    for e in ENGS:
        for site, cnt in p.sitemap[e]:
            if idx < n + cnt:
                return e, site
            n += cnt
    return None, "beyond program (%d instrs)" % n


# version 32
# speedup vs baseline: 1.0648x; 1.0078x over previous
import math
import sys
from contextlib import ExitStack

import numpy as np

import concourse.bass as bass
import concourse.mybir as mybir
from concourse.bass_utils import run_bass_kernel_spmd

dt = mybir.dt
AF = mybir.ActivationFunctionType
OP = mybir.AluOpType
AX = mybir.AxisListType
F32 = dt.float32
BF16 = dt.bfloat16

MAGIC = 12582912.0

B, S, D = 4, 4096, 1024
NCORE = 8
TPC = S // 2
NT = TPC // 128
H, HD = 16, 64
K_CL, FC = 4, 8
TK = 38
NSUB = 5
NBLK = 384
NSEL = 1728
DI = 3072
LN_EPS = 1e-5
ALL8 = [list(range(8))]
PAIRS = [[0, 1], [2, 3], [4, 5], [6, 7]]
QMAX0 = 127.0
QMAX1 = 32767.0



def _host_prep(inp):
    import ml_dtypes
    bf = ml_dtypes.bfloat16

    clusters = np.asarray(inp["L0_attn_clusters"]).astype(np.int64)
    keyframes = np.asarray(inp["L0_attn_keyframes"]).astype(np.int64)
    assert clusters.shape == (K_CL, FC)
    assert np.array_equal(clusters.ravel(), np.arange(K_CL * FC)), clusters
    assert np.array_equal(keyframes, clusters[:, 0]), keyframes

    x = np.ascontiguousarray(np.asarray(inp["L0_attn_x"], dtype=np.float32))

    def t(a):
        return np.ascontiguousarray(np.asarray(a, dtype=np.float32).T)

    def c(a):
        return np.ascontiguousarray(np.asarray(a, dtype=np.float32))

    def fmaj(a, nt=8):
        return np.ascontiguousarray(
            np.asarray(a, dtype=np.float32).reshape(nt, 128).T)

    def bcast(a, dtype=np.float32):
        a = np.asarray(a, dtype=np.float32)
        return np.ascontiguousarray(
            np.broadcast_to(a[None, :], (128, a.shape[0])).astype(dtype))

    id128 = np.eye(128, dtype=np.float32)
    lts = np.tril(np.ones((128, 128), dtype=np.float32), -1)
    iota38 = np.ascontiguousarray(
        np.broadcast_to(np.arange(TK, dtype=np.float32)[None, :], (128, TK)))
    ones_row = np.ones((1, 128), dtype=np.float32)
    h16 = np.zeros((D, H), dtype=np.float32)
    for hh in range(H):
        h16[hh * HD:(hh + 1) * HD, hh] = 1.0
    meanw = np.zeros((16, K_CL), dtype=np.float32)
    for kf in range(K_CL):
        for bb in range(B):
            meanw[kf * 4 + bb, kf] = 0.25
    e2 = np.zeros((2, 128), dtype=np.float32)
    e2[0, 0:64] = 1.0
    e2[1, 64:128] = 1.0

    wq = c(inp["L0_attn_W_qkv"])
    bqkv = np.asarray(inp["L0_attn_b_qkv"], dtype=np.float32)
    wg = c(inp["L1_ffn_w_gate"])
    wu = c(inp["L1_ffn_w_up"])
    wd = c(inp["L1_ffn_w_down"])

    kf_cand = sorted({int(kf) % 16 for kf in keyframes})

    consts = {
        "id_f32": id128,
        "id_bf": id128.astype(bf),
        "lts": lts,
        "iota38": iota38,
        "ones_f32": ones_row,
        "ones_bf": ones_row.astype(bf),
        "e2_bf": e2.astype(bf),
        "h16": h16,
        "meanw": meanw,
        "zeros_bf": np.zeros((128, 2048), dtype=bf),
    }

    in_maps = []
    for cidx in range(NCORE):
        b_, h_ = cidx // 2, cidx % 2
        kfsel = np.zeros((1, len(kf_cand) * 16), dtype=np.float32)
        for ii, lt in enumerate(kf_cand):
            gf = 16 * h_ + lt
            w = np.where(keyframes == gf)[0]
            if len(w):
                kfsel[0, 16 * ii + int(w[0]) * 4 + b_] = 1.0
        selmask = np.zeros((128, 2 * K_CL * 2), dtype=np.float32)
        for cig in range(2):
            ci = 2 * h_ + cig
            for ncl in range(K_CL):
                if not (ncl == 0 or ncl == ci):
                    col1 = (cig * K_CL + ncl) * 2
                    selmask[190 - 128:, col1] = 1.0
                    selmask[0:48, col1 + 1] = 1.0
        m = {
            "x_in": np.ascontiguousarray(x[b_, h_ * TPC:(h_ + 1) * TPC]),
            "wqkv": wq,
            "bqkv_fm": fmaj(bqkv, 24),
            "bk_bc": bcast(bqkv[D:2 * D], bf),
            "bv_bc": bcast(bqkv[2 * D:3 * D], bf),
            "wproj": c(inp["L0_attn_W_proj"]),
            "bproj_bc": bcast(inp["L0_attn_b_proj"]),
            "wf0T": t(inp["L0_ffn_weight"]),
            "f0g_fm": fmaj(inp["L0_ffn_gamma"]),
            "f0b_fm": fmaj(inp["L0_ffn_beta"]),
            "f0g_bc": bcast(inp["L0_ffn_gamma"]),
            "f0b_bc": bcast(inp["L0_ffn_beta"]),
            "adiag_fm": fmaj(inp["L1_attn_A_diag"]),
            "bmatT": t(inp["L1_attn_Bmat"]),
            "cmatT": t(inp["L1_attn_Cmat"]),
            "dmatT": t(inp["L1_attn_Dmat"]),
            "h0_fm": fmaj(inp["L1_attn_h0"]) if h_ == 0
                     else np.zeros((128, 8), dtype=np.float32),
            "oddflag": np.full((128, 8), float(h_), dtype=np.float32),
            "lng_bc": bcast(inp["L1_attn_ln_gamma"]),
            "lnb_bc": bcast(inp["L1_attn_ln_beta"]),
            "wg_sh": np.ascontiguousarray(wg[cidx * 384:(cidx + 1) * 384]),
            "wu_sh": np.ascontiguousarray(wu[cidx * 384:(cidx + 1) * 384]),
            "wd_sh": np.ascontiguousarray(wd[cidx * 128:(cidx + 1) * 128]),
            "kfsel": kfsel,
            "selmask": selmask,
            "wqk32": np.ascontiguousarray(
                wq[:, 0:2048].reshape(8, 128, 16, 128).transpose(2, 0, 1, 3)
                .reshape(16, 1024, 128)),
            "cigflag": np.ascontiguousarray(np.broadcast_to(
                np.array([[1.0 if ci == 2 * h_ + g else 0.0
                           for g in range(2) for ci in range(K_CL)]],
                         dtype=np.float32), (128, 8))),
        }
        m.update(consts)
        in_maps.append(m)
    return in_maps, {"kf_cand": kf_cand}



ENGS = ("T", "V", "S", "G", "Y")


class Prog:
    def __init__(self, nc):
        self.nc = nc
        self.stack = ExitStack()
        self.items = {e: [] for e in ENGS}
        self.sems = {}
        self.cnt = {}
        self._ccn = 0
        self._bar = []
        self._bar_epoch = 0
        self._eng_epoch = {e: 0 for e in ENGS}
        self.NSLOT = 8
        for e in ENGS:
            self._mksem("s" + e)
            for k in range(self.NSLOT):
                self._mksem("d%s%d" % (e, k))
        self.didx = {e: 0 for e in ENGS}
        self.sb = {}
        self.ps = {}

    def _mksem(self, name):
        self.sems[name] = self.stack.enter_context(self.nc.semaphore(name))
        self.cnt[name] = 0

    _uniq = 0

    def alloc(self, name, shape, dtype, side="left"):
        Prog._uniq += 1
        cm = self.nc.sbuf_tensor("%s_u%d" % (name, Prog._uniq), list(shape),
                                 dtype, side=side)
        h = cm.__enter__()
        self.sb[name] = (cm, h)
        return h

    def barrier(self):
        bar = [("s" + e, self.cnt["s" + e]) for e in ENGS]
        for e in ENGS:
            bar.extend(self.dma_all(e))
        self._bar = [b for b in bar if b[1] > 0]
        self._bar_epoch += 1

    def free(self, *names):
        for name in names:
            cm, h = self.sb.pop(name)
            cm.__exit__(None, None, None)
        self.barrier()

    def psum(self, name, shape, dtype=F32):
        cm = self.nc.psum_tensor(name, list(shape), dtype)
        h = cm.__enter__()
        self.ps[name] = (cm, h)
        return h

    def do(self, e, fn, deps=(), mark=False, dma=False, cc=False):
        incs = []
        ev = None
        if e == "T":
            dd = [("sT", max(0, self.cnt["sT"] - 3))]
        else:
            dd = [("s" + e, self.cnt["s" + e])]
        if self._eng_epoch[e] < self._bar_epoch:
            dd.extend(self._bar)
            self._eng_epoch[e] = self._bar_epoch
        for d in deps:
            if d is None:
                continue
            if isinstance(d, list):
                dd.extend(x for x in d if x is not None)
            else:
                dd.append(d)
        if mark:
            self.cnt["s" + e] += 1
            incs.append(("s" + e, 1))
            ev = ("s" + e, self.cnt["s" + e])
        if dma:
            nm = "d%s%d" % (e, self.didx[e] % self.NSLOT)
            self.didx[e] += 1
            dd.append((nm, self.cnt[nm]))
            self.cnt[nm] += 16
            incs.append((nm, 16))
            ev = (nm, self.cnt[nm])
        if cc:
            nm = "cc%d" % self._ccn
            self._ccn += 1
            self._mksem(nm)
            self.cnt[nm] = 1
            incs.append((nm, 1))
            ev = (nm, 1)
        fr = sys._getframe(1)
        site = "%s:%d" % (fr.f_code.co_name, fr.f_lineno)
        if fr.f_code.co_name in ("dma", "ts", "tt", "stt", "red", "act", "mm",
                                 "copy", "transp", "memset", "recip", "scan",
                                 "coll", "dmat"):
            fr2 = fr.f_back
            site = "%s:%d>%s" % (fr2.f_code.co_name, fr2.f_lineno, fr.f_code.co_name)
        self.items[e].append((tuple(dd), fn, tuple(incs), site))
        return ev

    def dma_all(self, e):
        return [("d%s%d" % (e, k), self.cnt["d%s%d" % (e, k)])
                for k in range(self.NSLOT)]

    def now(self, e):
        return ("s" + e, self.cnt["s" + e])

    def finalize(self):
        with self.nc.Block() as block:
            self.sitemap = {e: [] for e in ENGS}
            def runner(e):
                def go(eng):
                    seen = {}
                    for deps, fn, incs, site in self.items[e]:
                        nw = 0
                        for s_, v_ in deps:
                            if seen.get(s_, -1) < v_:
                                eng.wait_ge(self.sems[s_], v_)
                                seen[s_] = v_
                                nw += 1
                        inst = fn(eng)
                        for s_, n_ in incs:
                            inst.then_inc(self.sems[s_], n_)
                        self.sitemap[e].append((site, nw + 1))
                return go
            block.tensor(runner("T"))
            block.vector(runner("V"))
            block.scalar(runner("S"))
            block.gpsimd(runner("G"))
            block.sync(runner("Y"))



class Builder:
    def __init__(self, meta, taps=()):
        self.meta = meta
        self.taps = set(taps)
        nc = bass.Bass("TRN2", target_bir_lowering=False)
        self.nc = nc
        self.p = Prog(nc)
        self._declare_io()
        self._alloc_base()

    def _declare_io(self):
        nc = self.nc

        def din(name, shape, dtype=F32):
            return nc.dram_tensor(name, list(shape), dtype, kind="ExternalInput")

        def dint(name, shape, dtype=F32, shared=False):
            if shared:
                return nc.dram_tensor(name, list(shape), dtype,
                                      addr_space="Shared")
            return nc.dram_tensor(name, list(shape), dtype)

        self.d = d = {}
        d["x_in"] = din("x_in", (TPC, D))
        d["wqkv"] = din("wqkv", (D, 3 * D))
        d["bqkv_fm"] = din("bqkv_fm", (128, 24))
        d["bk_bc"] = din("bk_bc", (128, D), BF16)
        d["bv_bc"] = din("bv_bc", (128, D), BF16)
        d["wproj"] = din("wproj", (D, D))
        d["bproj_bc"] = din("bproj_bc", (128, D))
        d["wf0T"] = din("wf0T", (D, D))
        for nm in ("f0g_fm", "f0b_fm", "adiag_fm", "h0_fm", "oddflag"):
            d[nm] = din(nm, (128, 8))
        for nm in ("f0g_bc", "f0b_bc", "lng_bc", "lnb_bc"):
            d[nm] = din(nm, (128, D))
        d["bmatT"] = din("bmatT", (D, D))
        d["cmatT"] = din("cmatT", (D, D))
        d["dmatT"] = din("dmatT", (D, D))
        d["wg_sh"] = din("wg_sh", (384, D))
        d["wu_sh"] = din("wu_sh", (384, D))
        d["wd_sh"] = din("wd_sh", (128, DI))
        d["id_f32"] = din("id_f32", (128, 128))
        d["id_bf"] = din("id_bf", (128, 128), BF16)
        d["lts"] = din("lts", (128, 128))
        d["iota38"] = din("iota38", (128, TK))
        d["ones_f32"] = din("ones_f32", (1, 128))
        d["ones_bf"] = din("ones_bf", (1, 128), BF16)
        d["e2_bf"] = din("e2_bf", (2, 128), BF16)
        d["h16"] = din("h16", (D, H))
        d["meanw"] = din("meanw", (16, K_CL))
        d["zeros_bf"] = din("zeros_bf", (128, 2048), BF16)
        d["kfsel"] = din("kfsel", (1, len(self.meta["kf_cand"]) * 16))
        d["selmask"] = din("selmask", (128, 2 * K_CL * 2))
        d["cigflag"] = din("cigflag", (128, 8))
        d["wqk32"] = din("wqk32", (16, 1024, 128))

        d["y"] = nc.dram_tensor("y", [TPC, D], F32, kind="ExternalOutput")

        d["park_x"] = dint("park_x", (TPC, D))
        d["park_g0"] = dint("park_g0", (TPC, D), BF16)
        d["park_up"] = dint("park_up", (DI, TPC))
        d["park_gate"] = dint("park_gate", (DI, TPC))
        d["park_g2"] = dint("park_g2", (DI, TPC))
        d["park_op"] = dint("park_op", (DI, TPC))
        d["park_s"] = dint("park_s", (TPC, D))
        for nm in ("f0", "sx", "ug", "g2", "oo"):
            d["cc_%s_i" % nm] = dint("cc_%s_i" % nm, (1, 8))
            d["cc_%s_o" % nm] = dint("cc_%s_o" % nm, (1, 8), shared=True)
        d["cc_sc_i"] = dint("cc_sc_i", (16, 128))
        d["cc_sc_o"] = dint("cc_sc_o", (16, 128), shared=True)
        d["cc_kn_i"] = dint("cc_kn_i", (16, 1))
        d["cc_kn_o"] = dint("cc_kn_o", (32, 1))
        d["cc_kT_i"] = dint("cc_kT_i", (D, 16 * TK), BF16)
        d["cc_kT_o"] = dint("cc_kT_o", (2 * D, 16 * TK), BF16)
        d["cc_v_i"] = dint("cc_v_i", (16 * TK, 1288), BF16)
        d["cc_v_o"] = dint("cc_v_o", (2 * 16 * TK, 1288), BF16)
        d["cc_hm_i"] = dint("cc_hm_i", (128, 8))
        d["cc_hm_o"] = dint("cc_hm_o", (256, 8))
        d["cc_wg_i"] = dint("cc_wg_i", (384, D), BF16)
        d["cc_wg_o"] = dint("cc_wg_o", (DI, D), BF16, shared=True)
        d["cc_wu_i"] = dint("cc_wu_i", (384, D), BF16)
        d["cc_wu_o"] = dint("cc_wu_o", (DI, D), BF16, shared=True)
        d["cc_wd_i"] = dint("cc_wd_i", (128, DI), BF16)
        d["cc_wd_o"] = dint("cc_wd_o", (D, DI), BF16, shared=True)

    def _alloc_base(self):
        p = self.p
        self.idf = p.alloc("idf", (128, 128), F32)
        self.idb = p.alloc("idb", (128, 128), BF16)
        self.ltsb = p.alloc("ltsb", (128, 128), F32)
        self.iotab = p.alloc("iotab", (128, TK), F32)
        self.onesf = p.alloc("onesf", (1, 128), F32)
        self.onesb = p.alloc("onesb", (1, 128), BF16)
        self.e2b = p.alloc("e2b", (2, 128), BF16)
        self.h16b = p.alloc("h16b", (128, 8 * H), F32)
        self.meanwb = p.alloc("meanwb", (16, K_CL), F32)
        self.kfselb = p.alloc("kfselb", (1, len(self.meta["kf_cand"]) * 16), F32)
        self.selmb = p.alloc("selmb", (128, 2 * K_CL * 2), F32)
        self.cigfb = p.alloc("cigfb", (128, 8), F32)
        self.fm8 = p.alloc("fm8", (128, 5 * 8), F32)
        self.bqfm = p.alloc("bqfm", (128, 24), F32)
        self.stats = p.alloc("stats", (128, 64), F32)
        self.smallf = p.alloc("smallf", (128, 96), F32)
        self.scr = p.alloc("scr", (128, 2048), F32)
        self.scrb = p.alloc("scrb", (128, 3360), BF16)
        self.zbf = p.alloc("zbf", (128, 2048), BF16)
        self.fcb = p.alloc("fcb", (128, 8), F32)
        self._fvals = [LN_EPS, LN_EPS * LN_EPS, 1.0 + LN_EPS, 1.0, MAGIC,
                       65536.0, 0.125, -MAGIC]
        e_fc = None
        for k, v in enumerate(self._fvals):
            col = self.fcb[:, k:k + 1]
            e_fc = self.memset(col, v)
            self.nc.const_aps.register(F32, v, col)
        self.e_fc = self.memset(self.smallf[0:1, 56:64], 0.0)
        self.xsb = p.alloc("xsb", (128, NT * D), F32, side="right")
        self.psA = p.psum("psA", (128, 1024))
        self.psB = p.psum("psB", (128, 1024))
        self.psC = p.psum("psC", (128, 1024))
        self.psD = p.psum("psD", (128, 1024))

    def dma(self, e, out, in_, deps=()):
        return self.p.do(e, lambda eng, o=out, i=in_: eng.dma_start(out=o, in_=i),
                         deps=deps, dma=True)

    def dmat(self, e, out, in_, deps=()):
        return self.p.do(e, lambda eng, o=out, i=in_:
                         eng.dma_start_transpose(o, i), deps=deps, dma=True)

    def ts(self, out, in0, s1, s2, op0, op1=OP.bypass, deps=(), e="V"):
        return self.p.do(e, lambda eng, o=out, i=in0, a=s1, b=s2, p0=op0, p1=op1:
                         eng.tensor_scalar(o, i, a, b, p0, p1), deps=deps, mark=True)

    def tt(self, out, in0, in1, op, deps=(), e="V"):
        return self.p.do(e, lambda eng, o=out, a=in0, b=in1, pp=op:
                         eng.tensor_tensor(o, a, b, pp), deps=deps, mark=True)

    def stt(self, out, in0, scal, in1, op0, op1, deps=(), e="V"):
        return self.p.do(e, lambda eng, o=out, a=in0, s=scal, b=in1, p0=op0, p1=op1:
                         eng.scalar_tensor_tensor(o, a, s, b, p0, p1),
                         deps=deps, mark=True)

    def red(self, out, in_, op, deps=(), absv=None, e="V"):
        return self.p.do(e, lambda eng, o=out, i=in_, pp=op, av=absv:
                         eng.tensor_reduce(o, i, AX.X, pp, apply_absolute_value=av),
                         deps=deps, mark=True)

    def act(self, out, in_, func, bias=0.0, scale=1.0, deps=(), accum=None):
        def f(eng, o=out, i=in_, fu=func, bi=bias, sc=scale, ac=accum):
            if ac is not None:
                return eng.activation(o, i, fu, bias=bi, scale=sc, accum_out=ac)
            return eng.activation(o, i, fu, bias=bi, scale=sc)
        return self.p.do("S", f, deps=deps, mark=True)

    def mm(self, out, lhsT, rhs, start, stop, deps=(), mark=False, tp=None):
        return self.p.do("T", lambda eng, o=out, l=lhsT, r=rhs, st=start, sp=stop,
                         t=tp: eng.matmul(o, l, r, start=st, stop=sp,
                                          tile_position=t), deps=deps, mark=mark)

    def copy(self, out, in_, deps=(), e="V"):
        return self.p.do(e, lambda eng, o=out, i=in_: eng.tensor_copy(o, i),
                         deps=deps, mark=True)

    def transp(self, out, in_, ident, deps=(), mark=False):
        return self.p.do("T", lambda eng, o=out, i=in_, d_=ident:
                         eng.transpose(o, i, d_), deps=deps, mark=mark)

    def memset(self, ap_, val, deps=(), e="V"):
        return self.p.do(e, lambda eng, a=ap_, v=val: eng.memset(a, v),
                         deps=deps, mark=True)

    def recip(self, out, in_, deps=(), lp=False):
        def f(eng, o=out, i=in_, lp=lp):
            if lp:
                with self.nc.allow_low_precision(reason="bf16 softmax normalizer"):
                    return eng.reciprocal(o, i)
            return eng.reciprocal(o, i)
        return self.p.do("V", f, deps=deps, mark=True)

    def scan(self, out, d0, d1, init, deps=()):
        return self.p.do("V", lambda eng, o=out, a=d0, b=d1, i=init:
                         eng.tensor_tensor_scan(o, a, b, i, OP.mult, OP.add),
                         deps=deps, mark=True)

    def coll(self, kind, op, ins_, outs_, groups, deps=()):
        return self.p.do("G", lambda eng, k=kind, o=op, i=ins_, u=outs_, g=groups:
                         eng.collective_compute(k, o, replica_groups=g, ins=i, outs=u),
                         deps=deps, cc=True)

    _pr_last = None
    _bc_last = None

    def part_reduce(self, col_ap, dst, op, dep, absv=None):
        e1 = self.transp(self.psD[0:1, 640:768], col_ap, self.idf[:, :],
                         deps=[dep, Builder._pr_last], mark=True)
        Builder._pr_last = self.red(self.smallf[0:1, dst:dst + 1],
                                    self.psD[0:1, 640:768], op,
                                    deps=[e1], absv=absv)
        return Builder._pr_last

    def bcast_col(self, src, dst, dep, scale=None, bias_add=None):
        e1 = self.mm(self.psD[:, 800:801], self.onesf[0:1, :],
                     self.smallf[0:1, src:src + 1], True, True,
                     deps=[dep, Builder._bc_last], mark=True)
        if scale is None and bias_add is None:
            Builder._bc_last = self.copy(self.smallf[:, dst:dst + 1],
                                         self.psD[:, 800:801], deps=[e1])
        else:
            Builder._bc_last = self.ts(
                self.smallf[:, dst:dst + 1], self.psD[:, 800:801],
                1.0 if scale is None else scale, bias_add,
                OP.mult, OP.bypass if bias_add is None else OP.add, deps=[e1])
        return Builder._bc_last

    def ar_scalar(self, tag, src, dst, op, dep):
        d = self.d
        e0 = self.copy(self.smallf[0:1, 56:57], self.smallf[0:1, src:src + 1],
                       deps=[dep])
        e1 = self.dma("G", d["cc_%s_i" % tag][0:1, 0:8], self.smallf[0:1, 56:64],
                      deps=[e0])
        e2 = self.coll("AllReduce", op, [d["cc_%s_i" % tag].ap().opt()],
                       [d["cc_%s_o" % tag].ap().opt()], ALL8, deps=[e1])
        return self.dma("G", self.smallf[0:1, dst:dst + 1],
                        d["cc_%s_o" % tag][0:1, 0:1], deps=[e2])

    def ln_stats(self, dep):
        evs = []
        for t in range(NT):
            xt = self.xsb[:, D * t:D * (t + 1)]
            evs.append(self.red(self.stats[:, t:t + 1], xt, OP.add, deps=[dep]))
        e_m = self.ts(self.stats[:, 0:16], self.stats[:, 0:16], 1.0 / D, None,
                      OP.mult, deps=[evs[-1]])
        e_q = None
        for t in range(NT):
            xt = self.xsb[:, D * t:D * (t + 1)]
            buf = self.scr[:, 0:D] if t % 2 == 0 else self.scr[:, D:2 * D]
            e1 = self.act(buf, xt, AF.Square,
                          deps=[e_m if t < 2 else e_q, self.e_fc])
            e_q = self.red(self.stats[:, 48 + t:49 + t], buf, OP.add, deps=[e1])
        e_q2 = self.ts(self.stats[:, 48:64], self.stats[:, 48:64], 1.0 / D, None,
                       OP.mult, deps=[e_q])
        e_m2 = self.tt(self.smallf[:, 32:48], self.stats[:, 0:16],
                       self.stats[:, 0:16], OP.mult, deps=[e_q2])
        e_v = self.tt(self.smallf[:, 32:48], self.stats[:, 48:64],
                      self.smallf[:, 32:48], OP.subtract, deps=[e_m2])
        e_s1 = self.act(self.smallf[:, 48:64], self.smallf[:, 32:48], AF.Sqrt,
                        bias=LN_EPS, deps=[e_v])
        e_r1 = self.recip(self.stats[:, 16:32], self.smallf[:, 48:64], deps=[e_s1])
        e_s2 = self.act(self.smallf[:, 48:64], self.smallf[:, 32:48], AF.Sqrt,
                        bias=LN_EPS * LN_EPS, scale=1.0 + LN_EPS, deps=[e_r1])
        e_r2 = self.recip(self.stats[:, 32:48], self.smallf[:, 48:64], deps=[e_s2])
        return e_r1, e_r2

    def phase_loads(self):
        d = self.d
        x_t = d["x_in"].ap().rearrange("(n p) m -> n p m", p=128)
        for t in range(NT):
            self.e_x = self.dma("Y", self.xsb[:, D * t:D * (t + 1)], x_t[t])
        self.dma("Y", self.idf[:, :], d["id_f32"][:, :])
        self.dma("Y", self.idb[:, :], d["id_bf"][:, :])
        self.dma("Y", self.ltsb[:, :], d["lts"][:, :])
        self.dma("Y", self.iotab[:, :], d["iota38"][:, :])
        self.dma("Y", self.onesf[:, :], d["ones_f32"][:, :])
        self.dma("Y", self.onesb[:, :], d["ones_bf"][:, :])
        self.dma("Y", self.e2b[:, :], d["e2_bf"][:, :])
        h16t = d["h16"].ap().rearrange("(n p) m -> n p m", p=128)
        for i in range(8):
            self.dma("Y", self.h16b[:, H * i:H * (i + 1)], h16t[i])
        self.dma("Y", self.meanwb[:, :], d["meanw"][:, :])
        self.dma("Y", self.kfselb[:, :], d["kfsel"][:, :])
        self.dma("Y", self.selmb[:, :], d["selmask"][:, :])
        self.dma("Y", self.cigfb[:, :], d["cigflag"][:, :])
        for i, nm in enumerate(("f0g_fm", "f0b_fm", "adiag_fm", "h0_fm", "oddflag")):
            self.dma("Y", self.fm8[:, 8 * i:8 * i + 8], d[nm][:, :])
        self.dma("Y", self.bqfm[:, :], d["bqkv_fm"][:, :])
        self.e_consts = self.p.dma_all("Y")
        self.e_x = self.p.dma_all("Y")

    def phase_wdequant(self, dep):
        d, p = self.d, self.p
        wqsh = p.alloc("wqsh", (128, DI), F32)
        wqbm = p.alloc("wqbm", (128, 192), F32)
        parts = []
        e_prev = dep
        for (src, dst, ilen, ntile) in ((d["wg_sh"], d["cc_wg_i"], D, 3),
                                        (d["wu_sh"], d["cc_wu_i"], D, 3),
                                        (d["wd_sh"], d["cc_wd_i"], DI, 1)):
            nblk = ilen // 32
            for tt_ in range(ntile):
                e_l = self.dma("Y", wqsh[:, 0:ilen],
                               src[128 * tt_:128 * (tt_ + 1), :],
                               deps=[e_prev] if e_prev else ())
                in3 = bass.AP(wqsh, 0, [[DI, 128], [32, nblk], [1, 32]])
                e_bm = self.red(wqbm[:, 0:nblk], in3, OP.max, deps=[e_l], absv=True)
                e_s = self.ts(wqbm[:, 0:nblk], wqbm[:, 0:nblk], 1.0 / 7.0, 1e-12,
                              OP.mult, OP.max, deps=[e_bm])
                e_rs = self.recip(wqbm[:, 96:96 + nblk], wqbm[:, 0:nblk], deps=[e_s])
                sblk = bass.AP(wqbm, 0, [[192, 128], [1, nblk], [0, 32]])
                rsblk = bass.AP(wqbm, 96, [[192, 128], [1, nblk], [0, 32]])
                w3 = bass.AP(wqsh, 0, [[DI, 128], [32, nblk], [1, 32]])
                s3 = bass.AP(self.scr, 0, [[2048, 128], [32, nblk], [1, 32]])
                if ilen <= 2048:
                    e_t1 = self.tt(s3, w3, rsblk, OP.mult, deps=[e_rs])
                    e_t2 = self.ts(self.scr[:, 0:ilen], self.scr[:, 0:ilen],
                                   MAGIC, MAGIC, OP.add, OP.subtract, deps=[e_t1])
                    e_t3 = self.tt(s3, s3, sblk, OP.mult, deps=[e_t2])
                    e_c = self.copy(self.scrb[:, 0:ilen], self.scr[:, 0:ilen],
                                    deps=[e_t3])
                else:
                    e_c = None
                    for hh in range(2):
                        o0 = hh * (ilen // 2)
                        nb2 = nblk // 2
                        w3h = bass.AP(wqsh, o0, [[DI, 128], [32, nb2], [1, 32]])
                        s3h = bass.AP(self.scr, 0, [[2048, 128], [32, nb2], [1, 32]])
                        sblkh = bass.AP(wqbm, nb2 * hh, [[192, 128], [1, nb2], [0, 32]])
                        rsblkh = bass.AP(wqbm, 96 + nb2 * hh,
                                         [[192, 128], [1, nb2], [0, 32]])
                        e_t1 = self.tt(s3h, w3h, rsblkh, OP.mult, deps=[e_rs])
                        e_t2 = self.ts(self.scr[:, 0:ilen // 2],
                                       self.scr[:, 0:ilen // 2],
                                       MAGIC, MAGIC, OP.add, OP.subtract, deps=[e_t1])
                        e_t3 = self.tt(s3h, s3h, sblkh, OP.mult, deps=[e_t2])
                        e_c = self.copy(self.scrb[:, o0:o0 + ilen // 2],
                                        self.scr[:, 0:ilen // 2], deps=[e_t3])
                e_prev = self.dma("Y", dst[128 * tt_:128 * (tt_ + 1), 0:ilen],
                                  self.scrb[:, 0:ilen], deps=[e_c])
                parts.append(e_prev)
        self.e_wq_ag = {}
        for nm in ("wg", "wu", "wd"):
            self.e_wq_ag[nm] = self.coll(
                "AllGather", OP.bypass, [d["cc_%s_i" % nm].ap().opt()],
                [d["cc_%s_o" % nm].ap().opt()], ALL8, deps=parts)
        p.free("wqbm", "wqsh")

    def phase_f0_absmax(self, e_r2):
        p, d = self.p, self.d
        gb = p.alloc("f0bc", (128, 2 * D), F32)
        e_g = self.dma("Y", gb[:, 0:D], d["f0g_bc"][:, :])
        e_b = self.dma("Y", gb[:, D:2 * D], d["f0b_bc"][:, :])
        e_last = None
        for t in range(NT):
            xt = self.xsb[:, D * t:D * (t + 1)]
            buf = self.scr[:, 0:D] if t % 2 == 0 else self.scr[:, D:2 * D]
            e1 = self.ts(buf, xt, self.stats[:, t:t + 1],
                         self.stats[:, 32 + t:33 + t], OP.subtract, OP.mult,
                         deps=[e_r2, e_g, e_b])
            e2 = self.tt(buf, buf, gb[:, 0:D], OP.mult, deps=[e1])
            e3 = self.tt(buf, buf, gb[:, D:2 * D], OP.add, deps=[e2])
            e_last = self.red(self.smallf[:, 8 + t:9 + t], buf, OP.max,
                              deps=[e3], absv=True)
        e5 = self.red(self.smallf[:, 24:25], self.smallf[:, 8:24], OP.max,
                      deps=[e_last])
        e6 = self.part_reduce(self.smallf[:, 24:25], 0, OP.max, e5)
        e_rd = self.ar_scalar("f0", 0, 1, OP.max, e6)
        e_s = self.ts(self.smallf[0:1, 2:3], self.smallf[0:1, 1:2], 1e-8,
                      1.0 / QMAX0, OP.max, OP.mult, deps=[e_rd])
        e_r = self.recip(self.smallf[0:1, 3:4], self.smallf[0:1, 2:3], deps=[e_s])
        self.e_rf0 = self.bcast_col(3, 4, e_r)
        self.e_sf0 = self.bcast_col(2, 5, self.e_rf0)
        p.free("f0bc")

    def phase_xnT(self, e_r1):
        p = self.p
        self.xnT = p.alloc("xnT", (128, 8 * TPC), BF16)
        e_tp = {}
        for t in range(NT):
            xt = self.xsb[:, D * t:D * (t + 1)]
            buf = self.scr[:, 0:D] if (t % 2 == 0) else self.scr[:, D:2 * D]
            dep = [e_r1]
            if t >= 2:
                dep.append(e_tp[t - 2]["tp"])
            e1 = self.ts(buf, xt, self.stats[:, t:t + 1],
                         self.stats[:, 16 + t:17 + t], OP.subtract, OP.mult,
                         deps=dep)
            pt = self.psA if (t % 2 == 0) else self.psB
            e_tpl = None
            for i in range(8):
                dep2 = [e1]
                if t >= 2:
                    dep2.append(e_tp[t - 2]["ev"])
                e_tpl = self.transp(pt[:, 128 * i:128 * (i + 1)],
                                    buf[:, 128 * i:128 * (i + 1)],
                                    self.idf[:, :], deps=dep2 if i == 0 else (),
                                    mark=(i == 7))
            e_ev = None
            for i in range(8):
                e_ev = self.act(self.xnT[:, TPC * i + 128 * t:TPC * i + 128 * (t + 1)],
                                pt[:, 128 * i:128 * (i + 1)], AF.Identity, deps=[e_tpl])
            e_tp[t] = {"tp": e_tpl, "ev": e_ev}
        self.e_xnT = e_tp[NT - 1]["ev"]

    def phase_q(self):
        p, d = self.p, self.d
        wqw = p.alloc("wqw", (128, 8 * D), BF16)
        wqt = d["wqkv"].ap().rearrange("(n p) m -> n p m", p=128)
        for i in range(8):
            self.dma("G", wqw[:, D * i:D * (i + 1)], wqt[i][:, 0:D])
        e_wq = self.p.dma_all("G")
        nqc = TPC // 512
        e_ev_prev = {}
        for qi in range(8):
            for tc in range(nqc):
                k = (qi * nqc + tc) % 4
                pt = self.psA if k < 2 else self.psB
                bank = pt[:, 0:512] if (k % 2 == 0) else pt[:, 512:1024]
                deps = [self.e_xnT, e_wq]
                if k in e_ev_prev:
                    deps.append(e_ev_prev[k])
                ee = None
                for i in range(8):
                    ee = self.mm(bank, wqw[:, D * i + 128 * qi:D * i + 128 * (qi + 1)],
                                 self.xnT[:, TPC * i + 512 * tc:TPC * i + 512 * (tc + 1)],
                                 start=(i == 0), stop=(i == 7),
                                 deps=deps if i == 0 else (), mark=(i == 7))
                e_ev = self.act(self.qT[:, TPC * qi + 512 * tc:TPC * qi + 512 * (tc + 1)],
                                bank, AF.Identity, bias=self.bqfm[:, qi:qi + 1], deps=[ee])
                e_ev_prev[k] = e_ev
        self.e_qT = self.p.now("S")
        p.free("wqw")

    def _scores_impl(self):
        p, d = self.p, self.d
        kf_cand = self.meta["kf_cand"]
        nkf = len(kf_cand)
        kfx = p.alloc("kfx", (128, nkf * 8 * 128), F32)
        wqf = p.alloc("wqf32", (128, 3 * 1024), F32)
        wqt = d["wqkv"].ap().rearrange("(n p) m -> n p m", p=128)

        e_ev = None
        for ii, lt in enumerate(kf_cand):
            buf = self.scr[:, 0:D] if ii % 2 == 0 else self.scr[:, D:2 * D]
            e1 = self.ts(buf, self.xsb[:, D * lt:D * (lt + 1)],
                         self.stats[:, lt:lt + 1], self.stats[:, 16 + lt:17 + lt],
                         OP.subtract, OP.mult, deps=[self.e_xnT, e_ev])
            e_tpl = None
            for i in range(8):
                e_tpl = self.transp(self.psA[:, 128 * i:128 * (i + 1)],
                                    buf[:, 128 * i:128 * (i + 1)], self.idf[:, :],
                                    deps=[e1, e_ev] if i == 0 else (),
                                    mark=(i == 7))
            for i in range(8):
                e_ev = self.copy(kfx[:, 1024 * ii + 128 * i:1024 * ii + 128 * (i + 1)],
                                 self.psA[:, 128 * i:128 * (i + 1)], deps=[e_tpl])
        qkkf = p.alloc("qkkf", (128, 16 * (nkf * 128)), F32)
        e_prev = {}
        e_mmo = {}
        W = nkf * 128
        for oc in range(16):
            k2 = oc % 3
            e_wf = self.dma(
                "YGS"[oc % 3], wqf[:, 1024 * k2:1024 * (k2 + 1)],
                bass.AP(d["wqk32"], 1024 * 128 * oc,
                        [[128, 128], [128 * 128, 8], [1, 128]]),
                deps=[e_mmo.get(k2)])
            assert W <= 512
            bank = (self.psB[:, 0:W], self.psB[:, 512:512 + W],
                    self.psA[:, 0:W])[k2]
            ee = None
            for i in range(8):
                mv = bass.AP(kfx, 128 * i,
                             [[nkf * 8 * 128, 128], [1024, nkf], [1, 128]])
                ee = self.mm(bank,
                             wqf[:, 1024 * k2 + 128 * i:1024 * k2 + 128 * (i + 1)],
                             mv, start=(i == 0), stop=(i == 7),
                             deps=[e_wf, e_ev, e_prev.get(k2)] if i == 0 else (),
                             mark=(i == 7))
            e_mmo[k2] = ee
            e_pr = self.act(qkkf[:, W * oc:W * (oc + 1)], bank, AF.Identity,
                            bias=self.bqfm[:, oc:oc + 1], deps=[ee])
            e_prev[k2] = e_pr
        e_qk = self.p.now("S")

        e_ar_parts = []
        for ii in range(nkf):
            e_pr = None
            for i in range(8):
                e_pr = self.tt(self.scr[:, 128 * i:128 * (i + 1)],
                               qkkf[:, W * i + 128 * ii:W * i + 128 * (ii + 1)],
                               qkkf[:, W * (8 + i) + 128 * ii:W * (8 + i) + 128 * (ii + 1)],
                               OP.mult, deps=[e_qk] if i == 0 else ())
            ee = None
            for i in range(8):
                ee = self.mm(self.psB[0:16, 0:128], self.h16b[:, H * i:H * (i + 1)],
                             self.scr[:, 128 * i:128 * (i + 1)],
                             start=(i == 0), stop=(i == 7),
                             deps=[e_pr] if i == 0 else (), mark=(i == 7))
            e_hs = self.copy(self.scr[0:16, 1024:1152], self.psB[0:16, 0:128],
                             deps=[ee])
            e_tp = self.transp(self.psB[0:128, 512:528], self.scr[0:16, 1024:1152],
                               self.idf[0:16, 0:16], deps=[e_hs], mark=True)
            e_mx = self.red(self.scr[:, 1152:1153], self.psB[:, 512:528], OP.max,
                            deps=[e_tp])
            e_tp2 = self.transp(self.psB[0:1, 528:656], self.scr[:, 1152:1153],
                                self.idf[:, :], deps=[e_mx], mark=True)
            e_row = self.copy(self.scr[0:1, 1168:1296], self.psB[0:1, 528:656],
                              deps=[e_tp2])
            e_pl = self.mm(self.psC[0:16, 0:128],
                           self.kfselb[0:1, 16 * ii:16 * (ii + 1)],
                           self.scr[0:1, 1168:1296],
                           start=(ii == 0), stop=(ii == nkf - 1),
                           deps=[e_row], mark=True)
            e_ar_parts.append(e_pl)
        e_arb = self.copy(self.scr[0:16, 0:128], self.psC[0:16, 0:128],
                          deps=[e_ar_parts[-1], self.p.now("V")])
        e_w = self.dma("G", d["cc_sc_i"][:, :], self.scr[0:16, 0:128], deps=[e_arb])
        e_cc = self.coll("AllReduce", OP.add, [d["cc_sc_i"].ap().opt()],
                         [d["cc_sc_o"].ap().opt()], ALL8, deps=[e_w])
        self.e_scores_ar = self.dma("G", self.scr[0:16, 0:128], d["cc_sc_o"][:, :],
                                    deps=[e_cc])
        p.free("qkkf", "wqf32", "kfx")

    def phase_select(self):
        p = self.p
        e1 = self.mm(self.psB[0:4, 0:128], self.meanwb[:, :], self.scr[0:16, 0:128],
                     True, True, deps=[self.e_scores_ar], mark=True)
        e2 = self.copy(self.scr[0:4, 128:256], self.psB[0:4, 0:128], deps=[e1])
        e3 = self.transp(self.psB[0:128, 512:516], self.scr[0:4, 128:256],
                         self.idf[0:4, 0:4], deps=[e2], mark=True)
        e4 = self.copy(self.scr[:, 256:260], self.psB[:, 512:516], deps=[e3])
        self.onehot = p.alloc("onehot", (128, K_CL * TK), BF16)
        for ci in range(K_CL):
            scol = self.scr[:, 256 + ci:257 + ci]
            e4b = self.transp(self.psB[0:1, 512:640], scol, self.idf[:, :],
                              deps=[e4], mark=True)
            e4c = self.copy(self.scr[0:1, 128:256], self.psB[0:1, 512:640],
                            deps=[e4b])
            e5 = self.mm(self.psB[:, 0:128], self.onesf[0:1, :],
                         self.scr[0:1, 128:256], True, True,
                         deps=[e4c], mark=True)
            eA = self.ts(self.scr[:, 512:640], self.psB[:, 0:128], scol, None,
                         OP.is_gt, deps=[e5])
            eB = self.ts(self.scr[:, 640:768], self.psB[:, 0:128], scol, None,
                         OP.is_equal, deps=[eA])
            eC = self.tt(self.scr[:, 640:768], self.scr[:, 640:768], self.ltsb[:, :],
                         OP.mult, deps=[eB])
            eD = self.tt(self.scr[:, 512:640], self.scr[:, 512:640],
                         self.scr[:, 640:768], OP.add, deps=[eC])
            eR = self.red(self.scr[:, 260 + ci:261 + ci], self.scr[:, 512:640],
                          OP.add, deps=[eD])
            eM = self.ts(self.scr[:, 264 + ci:265 + ci], self.scr[:, 260 + ci:261 + ci],
                         TK - 0.5, None, OP.is_lt, deps=[eR])
            eT = self.transp(self.psB[0:1, 512:640], self.scr[:, 264 + ci:265 + ci],
                             self.idf[:, :], deps=[eM], mark=True)
            eRw = self.copy(self.scr[0:1, 384:512], self.psB[0:1, 512:640], deps=[eT])
            eMb = self.mm(self.psB[:, 0:128], self.onesf[0:1, :],
                          self.scr[0:1, 384:512], True, True, deps=[eRw], mark=True)
            eC2 = self.tt(self.scr[:, 512:640], self.psB[:, 0:128], self.ltsb[:, :],
                          OP.mult, deps=[eMb])
            eCs = self.red(self.scr[:, 268 + ci:269 + ci], self.scr[:, 512:640],
                           OP.add, deps=[eC2])
            eOh = self.ts(self.scr[:, 512:512 + TK], self.iotab[:, :],
                          self.scr[:, 268 + ci:269 + ci], None, OP.is_equal,
                          deps=[eCs])
            self.e_onehot = self.ts(self.onehot[:, TK * ci:TK * (ci + 1)],
                                    self.scr[:, 512:512 + TK],
                                    self.scr[:, 264 + ci:265 + ci], None, OP.mult,
                                    deps=[eOh])

    def phase_kv_kmm(self):
        p, d = self.p, self.d
        self.kst = p.alloc("kstore", (128, NT * D), BF16)
        e_kb, e_vb = self.e_kvb
        self.e_kstev = {}
        e_gevac = {}
        for t in range(NT):
            e_ev = None
            for half in range(2):
                bank = self.psA[:, 512 * half:512 * (half + 1)]
                ee = None
                for i in range(8):
                    ee = self.mm(bank,
                                 self.xnT[:, TPC * i + 128 * t:TPC * i + 128 * (t + 1)],
                                 self.wqw2[:, D * i + 512 * half:D * i + 512 * (half + 1)],
                                 start=(i == 0), stop=(i == 7),
                                 deps=[self.e_xnT, e_kb, self.e_wk,
                                       e_gevac.get(half)] if i == 0 else (),
                                 mark=(i == 7))
                e_ev = self.stt(self.kst[:, D * t + 512 * half:
                                         D * t + 512 * (half + 1)], bank,
                                1.0, self.kvb[:, 512 * half:512 * (half + 1)],
                                OP.mult, OP.add, deps=[ee])
                e_gevac[half] = e_ev
            self.e_kstev[t] = e_ev

    def phase_kv(self):
        p, d = self.p, self.d
        wqw = self.wqw2
        wqt = d["wqkv"].ap().rearrange("(n p) m -> n p m", p=128)

        self.ohl = p.alloc("ohlocal", (128, 2 * TK), BF16)
        self.e_ohl2 = {}
        self.mksel = p.alloc("mksel", (128, 2), F32)
        for g in range(2):
            e1 = self.ts(self.scr[:, 0:TK], self.onehot[:, 0:TK],
                         self.cigfb[:, 4 * g:4 * g + 1], None, OP.mult,
                         deps=[self.e_onehot])
            for ci in range(1, K_CL):
                e1 = self.stt(self.scr[:, 0:TK],
                              self.onehot[:, TK * ci:TK * (ci + 1)],
                              self.cigfb[:, 4 * g + ci:4 * g + ci + 1],
                              self.scr[:, 0:TK], OP.mult, OP.add, deps=[e1])
            e2 = self.copy(self.ohl[:, TK * g:TK * (g + 1)], self.scr[:, 0:TK],
                           deps=[e1])
            self.e_ohl2[g] = e2
            self.e_mksel = self.red(self.mksel[:, g:g + 1], self.scr[:, 0:TK],
                                    OP.add, deps=[e2])

        knmax = p.alloc("knmax", (128, H), F32)
        e_kn = self.memset(knmax[:, :], 0.0)
        e_kb, e_vb = self.e_kvb

        e_gdp = {0: None, 1: None}
        e_gc = None
        e_n1p = None
        for t in range(NT):
            g = 0 if t < 8 else 1
            off = 2 * D + 304 * (t % 2)
            ktile = self.kst[:, D * t:D * (t + 1)]
            e_ev_t = self.e_kstev[t]
            ee = None
            for qi in range(8):
                ee = self.mm(self.psB[:, TK * qi:TK * (qi + 1)],
                             ktile[:, 128 * qi:128 * (qi + 1)],
                             self.ohl[:, TK * g:TK * (g + 1)],
                             start=True, stop=True,
                             deps=[e_ev_t, self.e_ohl2[g], self.e_mksel,
                                   e_gc] if qi == 0 else (),
                             mark=(qi == 7))
            e_gc = self.copy(self.scrb[:, off:off + 8 * TK],
                             self.psB[:, 0:8 * TK], deps=[ee, e_gdp[t % 2]])
            src = bass.AP(self.scrb, off, [[3360, 128], [TK, 8], [1, TK]])
            dst = bass.AP(d["cc_kT_i"], TK * t,
                          [[16 * TK, 128], [16 * TK * 128, 8], [1, TK]])
            e_gdp[t % 2] = self.dma("Y", dst, src, deps=[e_gc])
            e_sq = self.act(self.scr[:, 0:D], ktile, AF.Square,
                            deps=[e_ev_t, self.e_mksel, e_n1p])
            nrm3 = bass.AP(self.scr, 0, [[2048, 128], [HD, H], [1, HD]])
            e_n1 = self.red(self.scr[:, 1024:1024 + H], nrm3, OP.add,
                            deps=[e_sq])
            e_n1p = e_n1
            e_n2 = self.ts(self.scr[:, 1024:1024 + H],
                           self.scr[:, 1024:1024 + H],
                           self.mksel[:, g:g + 1], None, OP.mult,
                           deps=[e_n1])
            e_kn = self.tt(knmax[:, :], knmax[:, :],
                           self.scr[:, 1024:1024 + H], OP.max, deps=[e_n2])
        e_wv = None
        e_gevac = {0: None, 1: None, "v": [e_gdp[0], e_gdp[1]]}
        for which in (1,):
            if which == 1:
                for i in range(8):
                    self.dma("G", wqw[:, D * i:D * (i + 1)],
                             wqt[i][:, 2 * D:3 * D],
                             deps=[self.p.now("T")])
                e_wv = self.p.dma_all("G")
            for t in range(NT):
                g = 0 if t < 8 else 1
                ktile = self.scrb[:, 0:D] if t % 2 == 0 else self.scrb[:, D:2 * D]
                e_ev_t = None
                for half in range(2):
                    bank = self.psA[:, 512 * half:512 * (half + 1)]
                    ee = None
                    dep0 = [self.e_xnT, e_kb, e_vb, self.e_mksel, e_kn]
                    if which == 1:
                        dep0.append(e_wv)
                    dep0.append(e_gevac.get(half))
                    for i in range(8):
                        ee = self.mm(bank,
                                     self.xnT[:, TPC * i + 128 * t:TPC * i + 128 * (t + 1)],
                                     wqw[:, D * i + 512 * half:D * i + 512 * (half + 1)],
                                     start=(i == 0), stop=(i == 7),
                                     deps=dep0 if i == 0 else (), mark=(i == 7))
                    bias = self.kvb[:, D * which + 512 * half:
                                    D * which + 512 * (half + 1)]
                    e_ev_t = self.stt(ktile[:, 512 * half:512 * (half + 1)], bank,
                                      1.0, bias, OP.mult, OP.add, deps=[ee])
                    e_gevac[half] = e_ev_t
                if True:
                    ee = None
                    for half in range(2):
                        ee = self.mm(self.psB[0:TK, 512 * half:512 * (half + 1)],
                                     self.ohl[:, TK * g:TK * (g + 1)],
                                     ktile[:, 512 * half:512 * (half + 1)],
                                     start=True, stop=True,
                                     deps=[e_ev_t] if half == 0 else (),
                                     mark=(half == 1))
                    e_z0 = self.memset(self.scrb[0:TK, 2 * D:2 * D + 1288], 0.0,
                                       deps=[e_gevac.get("v")])
                    e_on = self.memset(
                        bass.AP(self.scrb, 2 * D + 64, [[3360, TK], [161, 8]]),
                        1.0, deps=[e_z0])
                    e_on = self.memset(
                        bass.AP(self.scrb, 2 * D + 65, [[3360, TK], [161, 8]]),
                        1.0, deps=[e_on])
                    srcE = bass.AP(self.psB, 0, [[1024, TK], [128, 8], [1, HD]])
                    dstE = bass.AP(self.scrb, 2 * D, [[3360, TK], [161, 8], [1, HD]])
                    e_vc = self.copy(dstE, srcE, deps=[ee, e_on])
                    srcO = bass.AP(self.psB, HD, [[1024, TK], [128, 8], [1, HD]])
                    dstO = bass.AP(self.scrb, 2 * D + 97,
                                   [[3360, TK], [161, 8], [1, HD]])
                    e_vc = self.copy(dstO, srcO, deps=[e_vc])
                    e_vd = self.dma("Y", d["cc_v_i"][TK * t:TK * (t + 1), :],
                                    self.scrb[0:TK, 2 * D:2 * D + 1288],
                                    deps=[e_vc])
                    e_gevac["v"] = e_vd
        e_t1 = self.transp(self.psB[0:H, 0:128], knmax[:, :], self.idf[:, :],
                           deps=[e_kn, self.p.now("V"), self.p.now("Y")],
                           mark=True)
        e_t2 = self.copy(self.scr[0:H, 0:128], self.psB[0:H, 0:128], deps=[e_t1])
        e_t3 = self.red(self.scr[0:H, 128:129], self.scr[0:H, 0:128], OP.max,
                        deps=[e_t2])
        e_t4 = self.dma("G", d["cc_kn_i"][:, :], self.scr[0:H, 128:129], deps=[e_t3])
        self.e_kn_ag = self.coll("AllGather", OP.bypass, [d["cc_kn_i"].ap().opt()],
                                 [d["cc_kn_o"].ap().opt()], PAIRS, deps=[e_t4])
        self.e_kv_ag = []
        dmadone = self.p.dma_all("Y")
        self.e_kv_ag.append(self.coll("AllGather", OP.bypass,
                                      [d["cc_kT_i"].ap().opt()],
                                      [d["cc_kT_o"].ap().opt()], PAIRS,
                                      deps=dmadone))
        self.e_kv_ag.append(self.coll("AllGather", OP.bypass,
                                      [d["cc_v_i"].ap().opt()],
                                      [d["cc_v_o"].ap().opt()], PAIRS,
                                      deps=dmadone))
        p.free("knmax")

    def phase_ffn0(self):
        p, d = self.p, self.d
        tT = p.alloc("tT", (128, 8 * D), BF16)
        wft = d["wf0T"].ap().rearrange("(n p) m -> n p m", p=128)
        e_red = []
        for i in range(8):
            buf = self.scr[:, 0:D] if i % 2 == 0 else self.scr[:, D:2 * D]
            dep = [e_red[i - 2]] if i >= 2 else ()
            e_l = self.dma("YG"[i % 2], buf, wft[i], deps=dep)
            e_red.append(self.red(self.smallf[:, 8 + i:9 + i], buf, OP.add,
                                  deps=[e_l], absv=True))
        e_s = self.red(self.smallf[:, 16:17], self.smallf[:, 8:16], OP.add,
                       deps=[e_red[-1]])
        e_s1 = self.part_reduce(self.smallf[:, 16:17], 6, OP.add, e_s)
        e_d = self.ts(self.smallf[0:1, 7:8], self.smallf[0:1, 6:7],
                      0.7 / (D * D), None, OP.mult, deps=[e_s1])
        e_dc = self.bcast_col(7, 8, e_d)
        awb = p.alloc("awb", (128, D), F32)
        e_cnt = []
        e_s2 = []
        e_tt = None
        e_prev = e_dc
        for i in range(8):
            buf = self.scr[:, 0:D] if i % 2 == 0 else self.scr[:, D:2 * D]
            e_l = self.dma("YG"[i % 2], buf, wft[i], deps=[e_prev])
            e_aw = self.stt(awb[:, :], buf, -1.0, buf, OP.mult, OP.max,
                            deps=[e_l])
            e_m = self.ts(self.scrb[:, 0:D], awb[:, :], self.smallf[:, 8:9], None,
                          OP.is_gt, deps=[e_aw])
            e_c = self.red(self.smallf[:, 64 + i:65 + i], self.scrb[:, 0:D],
                           OP.add, deps=[e_m])
            e_p = self.tt(awb[:, :], awb[:, :], self.scrb[:, 0:D], OP.mult,
                          deps=[e_c])
            e_s2i = self.red(self.smallf[:, 72 + i:73 + i], awb[:, :], OP.add,
                             deps=[e_p])
            e_sg = self.act(buf, buf, AF.Sign, deps=[e_s2i])
            e_tt = self.tt(tT[:, D * i:D * (i + 1)], buf, self.scrb[:, 0:D],
                           OP.mult, deps=[e_sg])
            e_cnt.append(e_c)
            e_s2.append(e_s2i)
            e_prev = e_tt
        e_c2 = self.red(self.smallf[:, 17:18], self.smallf[:, 64:72], OP.add,
                        deps=[e_tt])
        e_cS = self.part_reduce(self.smallf[:, 17:18], 9, OP.add, e_c2)
        e_s3 = self.red(self.smallf[:, 18:19], self.smallf[:, 72:80], OP.add,
                        deps=[e_cS])
        e_sS = self.part_reduce(self.smallf[:, 18:19], 10, OP.add, e_s3)
        e_mx = self.ts(self.smallf[0:1, 11:12], self.smallf[0:1, 9:10], 1.0, None,
                       OP.max, deps=[e_sS])
        e_rc = self.recip(self.smallf[0:1, 12:13], self.smallf[0:1, 11:12],
                          deps=[e_mx])
        e_al = self.tt(self.smallf[0:1, 13:14], self.smallf[0:1, 10:11],
                       self.smallf[0:1, 12:13], OP.mult, deps=[e_rc])
        e_sa = self.tt(self.smallf[0:1, 14:15], self.smallf[0:1, 13:14],
                       self.smallf[0:1, 2:3], OP.mult, deps=[e_al, self.e_sf0])
        e_sac = self.bcast_col(14, 15, e_sa)
        p.free("awb")

        gcol = p.alloc("gcol", (128, 16), F32)
        e_g1 = self.ts(gcol[:, 0:8], self.fm8[:, 0:8], self.smallf[:, 4:5], None,
                       OP.mult, deps=[self.e_rf0])
        e_g2 = self.ts(gcol[:, 8:16], self.fm8[:, 8:16], self.smallf[:, 4:5],
                       None, OP.mult, deps=[e_g1])

        e_ev_prev = {0: None, 1: None}
        e_tp_prev = {0: None, 1: None}
        e_mm2 = {0: None, 1: None}
        e_rn_t = {}
        e_pk_prev = None
        e_sl = None

        def ints_of(t):
            return (self.scrb[:, 0:D] if t % 2 == 0
                    else self.scrb[:, 2048:2048 + D])

        for t in range(NT + 1):
            if t < NT:
                xt = self.xsb[:, D * t:D * (t + 1)]
                buf = self.scr[:, 0:D] if t % 2 == 0 else self.scr[:, D:2 * D]
                e1 = self.ts(buf, xt, self.stats[:, t:t + 1],
                             self.stats[:, 32 + t:33 + t], OP.subtract, OP.mult,
                             deps=[e_g2, e_tp_prev[t % 2]])
                pt = self.psA if t % 2 == 0 else self.psB
                e_tpl = None
                for i in range(8):
                    e_tpl = self.transp(pt[:, 128 * i:128 * (i + 1)],
                                        buf[:, 128 * i:128 * (i + 1)],
                                        self.idf[:, :],
                                        deps=[e1, e_ev_prev[t % 2]]
                                        if i == 0 else (), mark=(i == 7))
                e_tp_prev[t % 2] = e_tpl
                e_ev = None
                for i in range(8):
                    e_ev = self.act(buf[:, 128 * i:128 * (i + 1)],
                                    pt[:, 128 * i:128 * (i + 1)], AF.Identity,
                                    bias=gcol[:, 8 + i:9 + i],
                                    scale=gcol[:, i:i + 1],
                                    deps=[e_tpl])
                e_ev_prev[t % 2] = e_ev
                e_rn_t[t] = self.ts(ints_of(t), buf, MAGIC, MAGIC,
                                    OP.add, OP.subtract,
                                    deps=[e_ev, e_mm2[t % 2]])
            if t >= 1:
                u = t - 1
                ints = ints_of(u)
                ees = []
                for half in range(2):
                    bank = self.psC[:, 512 * half:512 * (half + 1)]
                    ee = None
                    for i in range(8):
                        ee = self.mm(bank, ints[:, 128 * i:128 * (i + 1)],
                                     tT[:, D * i + 512 * half:
                                        D * i + 512 * (half + 1)],
                                     start=(i == 0), stop=(i == 7),
                                     deps=[e_rn_t[u], e_pk_prev]
                                     if i == 0 else (), mark=(i == 7))
                    ees.append(ee)
                    e_sl = self.act(self.scrb[:, D + 512 * half:
                                              D + 512 * (half + 1)],
                                    bank, AF.Silu, scale=self.smallf[:, 15:16],
                                    deps=[ee, e_sac, e_pk_prev])
                e_mm2[u % 2] = ees
                e_pk_prev = self.dma("YG"[u % 2],
                                     d["park_g0"][128 * u:128 * (u + 1), :],
                                     self.scrb[:, D:2 * D], deps=[e_sl])
        self.e_g0 = self.p.dma_all("Y") + self.p.dma_all("G")
        p.free("gcol", "tT")

    def phase_parkx(self):
        d = self.d
        for t in range(NT):
            e = self.dma("YG"[t % 2], d["park_x"][128 * t:128 * (t + 1), :],
                         self.xsb[:, D * t:D * (t + 1)],
                         deps=[self.p.now("S"), self.p.now("V")])
        self.e_parkx = self.p.dma_all("Y") + self.p.dma_all("G")
        self.p.free("xsb")

    def phase_assemble(self):
        p, d = self.p, self.d
        self.kselT = p.alloc("kselT", (128, 8 * NSEL), BF16)
        self.vsel = p.alloc("vsel", (128, 10 * 1288), BF16)
        asw = p.alloc("asw", (128, 300), F32)
        asb = p.alloc("asb", (128, TPC), BF16)
        evs = []
        for r in range(2):
            for q in range(2):
                ncl = 2 * r + q
                for i in range(8):
                    evs.append(self.dma(
                        "Y",
                        self.kselT[:, NSEL * i + NBLK * ncl:
                                   NSEL * i + NBLK * ncl + 8 * TK],
                        d["cc_kT_o"][D * r + 128 * i:D * r + 128 * (i + 1),
                                     8 * TK * q:8 * TK * (q + 1)],
                        deps=[self.e_kv_ag[0]]))
        e_kvd = self.p.dma_all("Y")
        for i in range(8):
            for ncl in range(4):
                evs.append(self.copy(
                    self.kselT[:, NSEL * i + 1536 + 48 * ncl:
                               NSEL * i + 1536 + 48 * (ncl + 1)],
                    self.kselT[:, NSEL * i + NBLK * ncl + 256:
                               NSEL * i + NBLK * ncl + 304],
                    deps=[e_kvd] if (i == 0 and ncl == 0) else ()))
        for r in range(2):
            for q in range(2):
                ncl = 2 * r + q
                for pi, (p0, n) in enumerate(((0, 128), (128, 128))):
                    evs.append(self.dma(
                        "Y",
                        self.vsel[0:n, 1288 * (2 * ncl + pi):
                                  1288 * (2 * ncl + pi) + 1288],
                        d["cc_v_o"][608 * r + 304 * q + p0:
                                    608 * r + 304 * q + p0 + n, :],
                        deps=[self.e_kv_ag[1]]))
                evs.append(self.dma(
                    "Y",
                    self.vsel[48 * (ncl % 2):48 * (ncl % 2) + 48,
                              1288 * (8 + ncl // 2):1288 * (8 + ncl // 2) + 1288],
                    d["cc_v_o"][608 * r + 304 * q + 256:
                                608 * r + 304 * q + 304, :],
                    deps=[self.e_kv_ag[1]]))
        self.e_ksel = evs

        e_qm = []
        for i in range(8):
            e_sq = self.act(asb[:, 0:TPC], self.qT[:, TPC * i:TPC * (i + 1)],
                            AF.Square,
                            deps=[self.e_qT, e_qm[i - 1] if i else None])
            e_qm.append(self.red(self.smallf[:, 24 + i:25 + i],
                                 asb[:, 0:TPC], OP.max, deps=[e_sq]))
        e_t = self.transp(self.psD[0:8, 640:768], self.smallf[:, 24:32],
                          self.idf[:, :], deps=[e_qm[-1]], mark=True)
        e_c = self.copy(asw[0:8, 0:128], self.psD[0:8, 640:768], deps=[e_t])
        q3 = bass.AP(asw, 0, [[300, 8], [64, 2], [1, 64]])
        e_z1 = self.memset(asw[:, 258:259], 0.0, deps=[e_c])
        e_h1 = self.red(asw[0:8, 256:258], q3, OP.add, deps=[e_z1])
        e_h2 = self.red(asw[0:8, 258:259], asw[0:8, 256:258], OP.max,
                        deps=[e_h1])
        e_qg = self.part_reduce(asw[:, 258:259], 20, OP.max, e_h2)
        e_k1 = self.dma("Y", asw[0:H, 129:130], d["cc_kn_o"][0:H, :],
                        deps=[self.e_kn_ag])
        e_k2 = self.dma("Y", asw[0:H, 130:131], d["cc_kn_o"][H:2 * H, :],
                        deps=[self.e_kn_ag])
        e_z3 = self.memset(asw[:, 131:132], 0.0, deps=[self.e_kn_ag])
        e_k3 = self.tt(asw[0:H, 131:132], asw[0:H, 129:130],
                       asw[0:H, 130:131], OP.max, deps=[e_z3, e_k1, e_k2])
        e_kg = self.part_reduce(asw[:, 131:132], 21, OP.max, e_k3)
        e_q6 = self.tt(self.smallf[0:1, 22:23], self.smallf[0:1, 20:21],
                       self.smallf[0:1, 21:22], OP.mult, deps=[e_qg, e_kg])
        e_q7 = self.act(self.smallf[0:1, 20:21], self.smallf[0:1, 22:23], AF.Sqrt,
                        deps=[e_q6])
        self.e_negc = self.bcast_col(20, 21, e_q7, scale=-0.125)
        e_bias = None
        for idx in range(16):
            e_bias = self.stt(self.smallf[:, 40 + idx:41 + idx],
                              self.selmb[:, idx:idx + 1], -1e9,
                              self.smallf[:, 21:22], OP.mult, OP.add,
                              deps=[self.e_negc])
        ebl = [e_bias]
        for g in range(2):
            for P in range(2):
                c = 84 + 2 * g + P
                ebl.append(self.dma("Y", self.smallf[0:48, c:c + 1],
                                    self.smallf[0:48, 41 + 8 * g + 4 * P:
                                                42 + 8 * g + 4 * P],
                                    deps=[e_bias]))
                ebl.append(self.dma("Y", self.smallf[48:96, c:c + 1],
                                    self.smallf[0:48, 43 + 8 * g + 4 * P:
                                                44 + 8 * g + 4 * P],
                                    deps=[e_bias]))
        self.e_bias = ebl

    def phase_attention(self):
        p, d = self.p, self.d
        p.barrier()
        self.attnT = p.alloc("attnT", (128, 8 * TPC), BF16)
        pbuf = p.alloc("pbuf", (128, 6 * 1024), BF16)
        onesall = p.alloc("onesall", (128, 128), BF16)
        e_oa = self.memset(onesall[:, :], 1.0)
        rb = p.alloc("rbuf", (128, 1024), BF16)

        chunks = []
        for ncl in range(K_CL):
            chunks.append((NBLK * ncl, 2 * ncl, 128, None))
            chunks.append((NBLK * ncl + 128, 2 * ncl + 1, 128, ("m", 2 * ncl)))
        for P in range(2):
            chunks.append((1536 + 96 * P, 8 + P, 96, ("p", P)))
        NCH = len(chunks)

        deps0 = self.e_ksel + [self.e_bias, e_oa, self.e_qT]
        e_exp = {}
        e_avh = {}
        e_attn_ev = None
        e_pt0_prev = None
        e_rbm0_p = None
        e_rbm1_p = None
        e_rb = {0: None, 1: None}
        first = True
        for i in range(8):
            for g in range(2):
                e_av = {}
                avst = {"S": None, "prev": e_attn_ev}

                def av_emit(cc, i=i):
                    col0c, strip, nc_, _m = chunks[cc]
                    base0 = 1288 * strip + 161 * i
                    for sub in range(2):
                        pbi = (2 * cc + sub) % 6
                        pb = pbuf[:, 1024 * pbi:1024 * pbi + 1024]
                        e_ex = e_exp[(sub, cc)]
                        for qc in range(2):
                            dep = [e_ex]
                            if cc == 0:
                                dep.append(avst["prev"])
                            if sub == 0:
                                e_av[0] = self.mm(
                                    self.psC[0:65, 512 * qc:512 * (qc + 1)],
                                    self.vsel[0:nc_, base0:base0 + 65],
                                    pb[0:nc_, 512 * qc:512 * (qc + 1)],
                                    start=(cc == 0), stop=(cc == NCH - 1),
                                    deps=dep if qc == 0 else (), mark=(qc == 1))
                            else:
                                e_av[1] = self.mm(
                                    self.psD[64:128, 512 * qc:512 * (qc + 1)],
                                    self.vsel[0:nc_, base0 + 97:base0 + 161],
                                    pb[0:nc_, 512 * qc:512 * (qc + 1)],
                                    start=(cc == 0), stop=(cc == NCH - 1),
                                    deps=dep if qc == 0 else (), tp=(0, 64),
                                    mark=(qc == 1))
                                avst["S"] = self.mm(
                                    self.psD[0:1, 512 * qc:512 * (qc + 1)],
                                    self.vsel[0:nc_, base0 + 65:base0 + 66],
                                    pb[0:nc_, 512 * qc:512 * (qc + 1)],
                                    start=(cc == 0), stop=(cc == NCH - 1),
                                    deps=(), tp=(0, 0), mark=(qc == 1))
                        e_avh[pbi] = ([e_av[sub], avst["S"]] if sub
                                      else e_av[0])

                for c, (col0, strip, n, bspec) in enumerate(chunks):
                    pts = (self.psA, self.psB)
                    e_log = {}
                    for sub in range(2):
                        p0 = 64 * sub
                        ee = None
                        for qc in range(2):
                            dep = []
                            if first:
                                dep = list(deps0)
                            if (sub, (c - 1)) in e_exp:
                                dep.append(e_exp[(sub, c - 1)])
                            if c == 0:
                                dep.append(e_rb[sub])
                            ee = self.mm(
                                pts[sub][0:n, 512 * qc:512 * (qc + 1)],
                                self.kselT[p0:p0 + 64,
                                           NSEL * i + col0:NSEL * i + col0 + n],
                                self.qT[p0:p0 + 64,
                                        TPC * i + 1024 * g + 512 * qc:
                                        TPC * i + 1024 * g + 512 * (qc + 1)],
                                start=True, stop=True, deps=dep, mark=True)
                        first = False
                        e_log[sub] = ee
                    if c >= 1:
                        av_emit(c - 1)
                    if bspec is None:
                        bcol = self.smallf[:, 21:22]
                    elif bspec[0] == "m":
                        bcol = self.smallf[:, 40 + 8 * g + bspec[1]:
                                           41 + 8 * g + bspec[1]]
                    else:
                        bcol = self.smallf[:, 84 + 2 * g + bspec[1]:
                                           85 + 2 * g + bspec[1]]
                    for sub in range(2):
                        pbi = (2 * c + sub) % 6
                        pb = pbuf[:, 1024 * pbi:1024 * pbi + 1024]
                        e_ex = self.act(pb[0:n, :], pts[sub][0:n, 0:1024], AF.Exp,
                                        bias=bcol[0:n, :], scale=0.125,
                                        deps=[e_log[sub], e_avh.get(pbi)])
                        e_exp[(sub, c)] = e_ex
                av_emit(NCH - 1)
                e_avS = avst["S"]
                e_r0 = self.recip(rb[64:65, 0:1024], self.psC[64:65, 0:1024],
                                  deps=[e_av[0], e_rbm0_p], lp=True)
                e_r1 = self.recip(rb[0:1, 0:1024], self.psD[0:1, 0:1024],
                                  deps=[e_avS, e_av[1], e_rbm1_p], lp=True)
                e_rbm0 = None
                e_rbm1 = None
                for qc in range(2):
                    e_rbm0 = self.mm(self.psD[0:64, 512 * qc:512 * (qc + 1)],
                                     onesall[64:65, 0:64],
                                     rb[64:65, 512 * qc:512 * (qc + 1)],
                                     start=True, stop=True,
                                     deps=[e_r0, e_r1, e_pt0_prev]
                                     if qc == 0 else (),
                                     mark=(qc == 1))
                    e_rbm1 = self.mm(self.psC[64:128, 512 * qc:512 * (qc + 1)],
                                     onesall[0:1, 0:64],
                                     rb[0:1, 512 * qc:512 * (qc + 1)],
                                     start=True, stop=True,
                                     deps=[e_r0, e_attn_ev] if qc == 0 else (),
                                     tp=(0, 64), mark=(qc == 1))
                e_rbm0_p = e_rbm0
                e_rbm1_p = e_rbm1
                e_rb[0] = e_exp[(0, NCH - 1)]
                e_rb[1] = e_exp[(1, NCH - 1)]
                e_c0 = self.copy(rb[0:64, 0:1024], self.psD[0:64, 0:1024],
                                 deps=[e_rbm0, e_rbm1])
                e_c1 = self.copy(rb[64:128, 0:1024], self.psC[64:128, 0:1024],
                                 deps=[e_rbm1, e_rbm0])
                e_t0 = self.tt(self.attnT[0:64,
                                          TPC * i + 1024 * g:TPC * i + 1024 * (g + 1)],
                               self.psC[0:64, 0:1024], rb[0:64, 0:1024],
                               OP.mult, deps=[e_c0])
                e_pt0_prev = e_t0
                e_attn_ev = self.tt(
                    self.attnT[64:128,
                               TPC * i + 1024 * g:TPC * i + 1024 * (g + 1)],
                    self.psD[64:128, 0:1024], rb[64:128, 0:1024],
                    OP.mult, deps=[e_c1, e_t0])
                e_exp = {}
        self.e_attn = e_attn_ev
        self.e_attn = e_attn_ev
        p.free("rbuf", "onesall", "pbuf")

    def phase_proj(self):
        p, d = self.p, self.d
        p.barrier()
        wps = p.alloc("wps", (128, 8 * D), BF16)
        bpb = p.alloc("bpb", (128, D), F32)
        e_bp = self.dma("Y", bpb[:, :], d["bproj_bc"][:, :])
        wpt = d["wproj"].ap().rearrange("(n p) m -> n p m", p=128)
        for i in range(8):
            self.dma("G", wps[:, D * i:D * (i + 1)], wpt[i])
        e_wp = self.p.dma_all("G")
        prx = p.alloc("prx", (128, 2048), F32)
        e_bk = {0: None, 1: None}
        e_tr = {}
        e_gr = {}
        e_pw = {}
        e_xd = {}
        for t in range(NT):
            o = 1024 * (t % 2)
            e_xl = self.dma("Y", prx[:, o:o + D],
                            d["park_x"][128 * t:128 * (t + 1), :],
                            deps=[self.e_parkx, e_pw.get(t % 2)])
            e_gl = self.dma("G", self.scrb[:, o:o + D],
                            d["park_g0"][128 * t:128 * (t + 1), :],
                            deps=[self.e_g0, e_gr.get((t % 2, 1))])
            e_x1 = {}
            for half in range(2):
                bank = self.psA[:, 0:512] if half == 0 else self.psA[:, 512:1024]
                ee = None
                for j in range(8):
                    ee = self.mm(bank,
                                 self.attnT[:, TPC * j + 128 * t:TPC * j + 128 * (t + 1)],
                                 wps[:, D * j + 512 * half:D * j + 512 * (half + 1)],
                                 start=(j == 0), stop=(j == 7),
                                 deps=[self.e_attn, e_wp, e_bp, e_bk[half]]
                                 if j == 0 else (), mark=(j == 7))
                tmp = self.scr[:, o + 512 * half:o + 512 * (half + 1)]
                e_s0 = self.stt(tmp, bank, 1.0,
                                bpb[:, 512 * half:512 * (half + 1)],
                                OP.mult, OP.add,
                                deps=[ee, e_tr.get((t % 2, half))])
                e_bk[half] = e_s0
                e_t2 = self.tt(tmp, tmp,
                               self.scrb[:, o + 512 * half:o + 512 * (half + 1)],
                               OP.mult, deps=[e_s0, e_gl])
                e_gr[(t % 2, half)] = e_t2
                e_x1[half] = self.tt(prx[:, o + 512 * half:o + 512 * (half + 1)],
                                     prx[:, o + 512 * half:o + 512 * (half + 1)],
                                     tmp, OP.add, deps=[e_t2, e_xl])
                e_tr[(t % 2, half)] = e_x1[half]
            e_pw[t % 2] = self.dma("YS"[t % 2],
                                   d["park_x"][128 * t:128 * (t + 1), :],
                                   prx[:, o:o + D], deps=[e_x1[0], e_x1[1]])
        self.e_x1w = self.p.dma_all("Y") + self.p.dma_all("S")
        p.free("prx", "bpb", "wps")

    def phase_l1_mixer(self, e_r1):
        p, d = self.p, self.d
        e_a1 = self.act(self.smallf[:, 24:32], self.fm8[:, 16:24], AF.Exp,
                        deps=[e_r1])
        e_a2 = self.act(self.smallf[:, 24:32], self.smallf[:, 24:32], AF.Ln,
                        bias=1.0, deps=[e_a1])
        e_a = self.ts(self.smallf[:, 24:32], self.smallf[:, 24:32], -1.0, None,
                      OP.mult, deps=[e_a2])

        self.phase_xnT(e_r1)
        p.barrier()
        self.xsF = p.alloc("xsF", (128, 8 * TPC), BF16)
        zf32 = self.zbf
        e_z = self.memset(zf32[:, :], 0.0)
        abc = p.alloc("abc", (128, 2048), F32)
        hmid = p.alloc("hmid", (128, 8), F32)
        bmsb = p.alloc("bmsb", (128, 8 * D), BF16)
        bmt = d["bmatT"].ap().rearrange("(n p) m -> n p m", p=128)
        for i in range(8):
            self.dma("G", bmsb[:, D * i:D * (i + 1)], bmt[i])
        e_bm = self.p.dma_all("G")

        e_am = None
        for t in range(NT):
            buf = self.scr[:, 0:D] if t % 2 == 0 else self.scr[:, D:2 * D]
            e1 = self.ts(buf, self.xsb[:, D * t:D * (t + 1)],
                         self.stats[:, t:t + 1], self.stats[:, 16 + t:17 + t],
                         OP.subtract, OP.mult, deps=[e_r1, e_am])
            e_am = self.red(self.smallf[:, 40 + t:41 + t], buf, OP.max,
                            deps=[e1], absv=True)
        e_am2 = self.red(self.smallf[:, 16:17], self.smallf[:, 40:56], OP.max,
                         deps=[e_am])
        e_am3 = self.part_reduce(self.smallf[:, 16:17], 0, OP.max, e_am2)
        e_ar = self.ar_scalar("sx", 0, 1, OP.max, e_am3)

        e_scan = {}
        e_hl = None
        for nch in range(8):
            e_bu = None
            for tc in range(4):
                pt = self.psA if tc % 2 == 0 else self.psB
                bank = pt[:, 0:512] if tc < 2 else pt[:, 512:1024]
                ee = None
                for i in range(8):
                    ee = self.mm(bank,
                                 bmsb[:, D * i + 128 * nch:D * i + 128 * (nch + 1)],
                                 self.xnT[:, TPC * i + 512 * tc:TPC * i + 512 * (tc + 1)],
                                 start=(i == 0), stop=(i == 7),
                                 deps=[self.e_xnT, e_bm, e_scan.get(nch - 1)]
                                 if i == 0 else (), mark=(i == 7))
                e_bu = self.copy(self.scr[:, 512 * tc:512 * (tc + 1)], bank,
                                 deps=[ee])
            e_ab = self.ts(abc[:, :], zf32[:, :], self.smallf[:, 24 + nch:25 + nch],
                           None, OP.add, deps=[e_z, e_a, e_bu])
            e_sc1 = self.scan(self.xsF[:, TPC * nch:TPC * (nch + 1)], abc[:, :],
                              self.scr[:, 0:2048], self.fm8[:, 24 + nch:25 + nch],
                              deps=[e_ab])
            e_scan[nch] = e_sc1
            e_hl = self.copy(hmid[:, nch:nch + 1],
                             self.xsF[:, TPC * nch + 2047:TPC * nch + 2048],
                             deps=[e_sc1])
        e_hd = self.dma("G", d["cc_hm_i"][:, :], hmid[:, :], deps=[e_hl])
        e_hc = self.coll("AllGather", OP.bypass, [d["cc_hm_i"].ap().opt()],
                         [d["cc_hm_o"].ap().opt()], PAIRS, deps=[e_hd])
        e_hr = self.dma("G", hmid[:, :], d["cc_hm_o"][0:128, :], deps=[e_hc])
        e_hm = self.tt(hmid[:, :], hmid[:, :], self.fm8[:, 32:40], OP.mult,
                       deps=[e_hr])
        for nch in range(8):
            e_ab = self.ts(abc[:, :], zf32[:, :], self.smallf[:, 24 + nch:25 + nch],
                           None, OP.add, deps=[e_hm])
            e_c1 = self.scan(self.scr[:, 0:2048], abc[:, :], zf32[:, :],
                             hmid[:, nch:nch + 1], deps=[e_ab])
            e_scan[nch] = self.tt(self.xsF[:, TPC * nch:TPC * (nch + 1)],
                                  self.xsF[:, TPC * nch:TPC * (nch + 1)],
                                  self.scr[:, 0:2048], OP.add, deps=[e_c1])
        self.e_xs = e_scan[7]
        e_s = self.ts(self.smallf[0:1, 2:3], self.smallf[0:1, 1:2], 1.0 / QMAX1,
                      1e-12, OP.mult, OP.max, deps=[e_ar])
        e_r = self.recip(self.smallf[0:1, 3:4], self.smallf[0:1, 2:3], deps=[e_s])
        e_rc = self.bcast_col(3, 4, e_r)
        self.e_sxcols = self.bcast_col(2, 5, e_rc)
        p.free("bmsb", "hmid", "abc")

    def phase_l1_cd_setup(self):
        p, d = self.p, self.d
        cmsb = p.alloc("cmsb", (128, 8 * D), BF16)
        dmsb = p.alloc("dmsb", (128, 8 * D), BF16)
        lnb = p.alloc("lnb", (128, 2 * D), F32)
        cmt = d["cmatT"].ap().rearrange("(n p) m -> n p m", p=128)
        dmt = d["dmatT"].ap().rearrange("(n p) m -> n p m", p=128)
        for i in range(8):
            self.dma("G", cmsb[:, D * i:D * (i + 1)], cmt[i])
            self.dma("G", dmsb[:, D * i:D * (i + 1)], dmt[i])
        self.cd = dict(cmsb=cmsb, dmsb=dmsb, lnb=lnb,
                       e_wc=self.p.dma_all("G"),
                       e_lg=self.dma("Y", lnb[:, 0:D], d["lng_bc"][:, :]),
                       e_lb=self.dma("Y", lnb[:, D:2 * D], d["lnb_bc"][:, :]),
                       e_prev={0: None, 1: None})

    def phase_l1_cd(self, t0, t1):
        p, d = self.p, self.d
        cmsb, dmsb, lnb = self.cd["cmsb"], self.cd["dmsb"], self.cd["lnb"]
        e_wc, e_lg, e_lb = self.cd["e_wc"], self.cd["e_lg"], self.cd["e_lb"]
        e_prev = self.cd["e_prev"]
        e_xs = self.e_xs
        for t in range(t0, t1):
            for half in range(2):
                bank = self.psC[:, 0:512] if half == 0 else self.psC[:, 512:1024]
                ee = None
                for i in range(8):
                    ee = self.mm(bank,
                                 self.xsF[:, TPC * i + 128 * t:TPC * i + 128 * (t + 1)],
                                 cmsb[:, D * i + 512 * half:D * i + 512 * (half + 1)],
                                 start=(i == 0), stop=False,
                                 deps=[e_xs, e_wc, e_prev[half]] if i == 0 else (),
                                 mark=False)
                for i in range(8):
                    ee = self.mm(bank,
                                 self.xnT[:, TPC * i + 128 * t:TPC * i + 128 * (t + 1)],
                                 dmsb[:, D * i + 512 * half:D * i + 512 * (half + 1)],
                                 start=False, stop=(i == 7),
                                 deps=(), mark=(i == 7))
                e_prev[half] = self.act(self.scr[:, 512 * half:512 * (half + 1)],
                                        bank, AF.Gelu, deps=[ee])
            e_m = self.red(self.smallf[:, 40:41], self.scr[:, 0:D], OP.add,
                           deps=[e_prev[0], e_prev[1]])
            e_m2 = self.ts(self.smallf[:, 40:41], self.smallf[:, 40:41], 1.0 / D,
                           None, OP.mult, deps=[e_m])
            e_sq = self.act(self.scrb[:, 0:D], self.scr[:, 0:D], AF.Square,
                            deps=[e_m2])
            e_q = self.red(self.smallf[:, 41:42], self.scrb[:, 0:D], OP.add,
                           deps=[e_sq])
            e_q2 = self.ts(self.smallf[:, 41:42], self.smallf[:, 41:42], 1.0 / D,
                           None, OP.mult, deps=[e_q])
            e_v1 = self.tt(self.smallf[:, 42:43], self.smallf[:, 40:41],
                           self.smallf[:, 40:41], OP.mult, deps=[e_q2])
            e_v2 = self.tt(self.smallf[:, 42:43], self.smallf[:, 41:42],
                           self.smallf[:, 42:43], OP.subtract, deps=[e_v1])
            e_sd = self.act(self.smallf[:, 43:44], self.smallf[:, 42:43], AF.Sqrt,
                            bias=LN_EPS, deps=[e_v2])
            e_rr = self.recip(self.smallf[:, 44:45], self.smallf[:, 43:44],
                              deps=[e_sd])
            e_n1 = self.ts(self.scr[:, 0:D], self.scr[:, 0:D],
                           self.smallf[:, 40:41], self.smallf[:, 44:45],
                           OP.subtract, OP.mult, deps=[e_rr])
            e_n2 = self.tt(self.scr[:, 0:D], self.scr[:, 0:D], lnb[:, 0:D],
                           OP.mult, deps=[e_n1, e_lg])
            e_n3 = self.tt(self.scr[:, 0:D], self.scr[:, 0:D], lnb[:, D:2 * D],
                           OP.add, deps=[e_n2, e_lb])
            e_pk = self.dma("YG"[t % 2], d["park_s"][128 * t:128 * (t + 1), :],
                            self.scr[:, 0:D], deps=[e_n3])
            e_prev[0] = e_pk
            e_prev[1] = e_pk
        if t1 == NT:
            self.e_s_parked = self.p.dma_all("Y") + self.p.dma_all("G")

    def phase_xq(self):
        p = self.p
        self.xqT = p.alloc("xqT", (128, 8 * TPC), BF16)
        e_tp = {}
        for t in range(NT):
            xt = self.xsb[:, D * t:D * (t + 1)]
            buf = self.scr[:, 0:D] if (t % 2 == 0) else self.scr[:, D:2 * D]
            dep = [self.e_r1b]
            if t >= 2:
                dep.append(e_tp[t - 2]["tp"])
            e1 = self.ts(buf, xt, self.stats[:, t:t + 1],
                         self.stats[:, 16 + t:17 + t], OP.subtract, OP.mult,
                         deps=dep)
            pt = self.psA if (t % 2 == 0) else self.psB
            e_tpl = None
            for i in range(8):
                dep2 = [e1]
                if t >= 2:
                    dep2.append(e_tp[t - 2]["ev"])
                e_tpl = self.transp(pt[:, 128 * i:128 * (i + 1)],
                                    buf[:, 128 * i:128 * (i + 1)],
                                    self.idf[:, :], deps=dep2 if i == 0 else (),
                                    mark=(i == 7))
            e_ev = None
            for i in range(8):
                e_ev = self.act(buf[:, 128 * i:128 * (i + 1)],
                                pt[:, 128 * i:128 * (i + 1)], AF.Identity,
                                bias=MAGIC, scale=self.smallf[:, 4:5],
                                deps=[e_tpl, self.e_sxcols])
            e_rn = None
            for i in range(8):
                e_rn = self.ts(self.xqT[:, TPC * i + 128 * t:TPC * i + 128 * (t + 1)],
                               buf[:, 128 * i:128 * (i + 1)], MAGIC, None,
                               OP.subtract, deps=[e_ev])
            e_tp[t] = {"tp": e_tpl, "ev": e_rn}
        self.e_xqT = self.p.now("V")
        self.p.free("xsb2")

    def phase_l1_p1(self):
        p, d = self.p, self.d
        self.l1w = p.alloc("l1w", (128, 8192), F32)
        l1w = self.l1w
        wq = p.alloc("wqT", (128, 8 * DI), BF16)
        self.wqT = wq
        e_amax = {}
        e_parked = {0: None, 1: None}
        e_pv = {}
        ccw, park = d["cc_wu_o"], d["park_up"]
        for i in range(8):
            self.dmat("Y", wq[:, DI * i:DI * (i + 1)],
                      ccw[:, 128 * i:128 * (i + 1)],
                      deps=[self.e_wq_ag["wu"]])
        e_w = self.p.dma_all("Y")
        e_am = self.memset(self.smallf[:, 46:47], 0.0)
        for R in range(24):
            buf = self.scr[:, 0:2048] if R % 2 == 0 else l1w[:, 0:2048]
            ee4 = [None] * 4
            for i in range(8):
                for tc in range(4):
                    pt = (self.psA, self.psB, self.psC, self.psD)[tc]
                    bank = pt[:, 0:512]
                    ee4[tc] = self.mm(
                        bank,
                        wq[:, DI * i + 128 * R:DI * i + 128 * (R + 1)],
                        self.xqT[:, TPC * i + 512 * tc:TPC * i + 512 * (tc + 1)],
                        start=(i == 0), stop=(i == 7),
                        deps=[self.e_xqT, e_w, e_pv.get(tc)]
                        if i == 0 else (), mark=(i == 7))
            for tc in range(4):
                pt = (self.psA, self.psB, self.psC, self.psD)[tc]
                e_c = self.act(buf[:, 512 * tc:512 * (tc + 1)],
                               pt[:, 0:512], AF.Identity,
                               deps=[ee4[tc], e_parked[R % 2]])
                e_pv[tc] = e_c
            e_r = self.red(self.smallf[:, 48:49], buf, OP.max,
                           deps=[e_pv[0], e_pv[1], e_pv[2], e_pv[3]], absv=True)
            e_mx = self.tt(self.smallf[:, 46:47],
                           self.smallf[:, 46:47],
                           self.smallf[:, 48:49], OP.max, deps=[e_r, e_am])
            e_pk = self.dma("YG"[R % 2], park[128 * R:128 * (R + 1), :],
                            buf, deps=[e_mx])
            e_parked[R % 2] = e_pk
            e_am = e_mx
        e_amax[0] = e_am
        e_amax["park0"] = e_pk
        e_up_last = list(ee4)
        for i in range(8):
            self.dmat("Y", wq[:, DI * i:DI * (i + 1)],
                      d["cc_wg_o"][:, 128 * i:128 * (i + 1)],
                      deps=[self.e_wq_ag["wg"], e_up_last])
        e_w = self.p.dma_all("Y")
        self.ff = dict(e_amax=e_amax, e_wg=e_w,
                       e_up_park=[e_parked[0], e_parked[1]])
        e_am = self.memset(self.smallf[:, 47:48], 0.0)
        e_acc = {}
        for R in range(24):
            ee4 = [None] * 4
            for i in range(8):
                for tc in range(4):
                    pt = (self.psA, self.psB, self.psC, self.psD)[tc]
                    bank = pt[:, 0:512]
                    ee4[tc] = self.mm(
                        bank,
                        wq[:, DI * i + 128 * R:DI * i + 128 * (R + 1)],
                        self.xqT[:, TPC * i + 512 * tc:TPC * i + 512 * (tc + 1)],
                        start=(i == 0), stop=(i == 7),
                        deps=[e_w, e_pv.get(tc)]
                        if i == 0 else (), mark=(i == 7))
            for tc in range(4):
                pt = (self.psA, self.psB, self.psC, self.psD)[tc]
                e_rr = self.red(self.smallf[:, 41 + tc:42 + tc], pt[:, 0:512],
                                OP.max, deps=[ee4[tc], e_acc.get(tc)], absv=True)
                e_pv[tc] = e_rr
                e_acc[tc] = self.tt(self.smallf[:, 47:48], self.smallf[:, 47:48],
                                    self.smallf[:, 41 + tc:42 + tc], OP.max,
                                    deps=[e_rr, e_am])
                e_am = e_acc[tc]
        e_amax[1] = e_am
        self.ff["e_pv_p1"] = dict(e_pv)
        self.ff["e_amax"] = e_amax

    def phase_l1_wqfree(self):
        self.p.free("wqT")

    def phase_l1_ugar(self):
        d, e_amax = self.d, self.ff["e_amax"]
        e_u1 = self.part_reduce(self.smallf[:, 46:47], 24, OP.max, e_amax[0])
        e_g1 = self.part_reduce(self.smallf[:, 47:48], 25, OP.max, e_amax[1])
        e_p0 = self.copy(self.smallf[0:1, 56:58], self.smallf[0:1, 24:26],
                         deps=[e_u1, e_g1])
        e_p1 = self.dma("G", d["cc_ug_i"][0:1, 0:8], self.smallf[0:1, 56:64],
                        deps=[e_p0])
        e_c1 = self.coll("AllReduce", OP.max, [d["cc_ug_i"].ap().opt()],
                         [d["cc_ug_o"].ap().opt()], ALL8, deps=[e_p1])
        self.ff["e_ug_r"] = self.dma("G", self.smallf[0:1, 26:28],
                                     d["cc_ug_o"][0:1, 0:2], deps=[e_c1])

    def phase_l1_p2(self):
        p, d = self.p, self.d
        l1w, e_amax = self.l1w, self.ff["e_amax"]
        e_f1 = self.ts(self.smallf[0:1, 26:27], self.smallf[0:1, 26:27],
                       self.smallf[0:1, 2:3], None, OP.mult,
                       deps=[self.ff["e_ug_r"]])
        e_f2 = self.ts(self.smallf[0:1, 27:28], self.smallf[0:1, 27:28],
                       self.smallf[0:1, 2:3], None, OP.mult, deps=[e_f1])
        e_su = self.ts(self.smallf[0:1, 28:29], self.smallf[0:1, 26:27],
                       1.0 / QMAX1, 1e-12, OP.mult, OP.max, deps=[e_f2])
        e_sg = self.ts(self.smallf[0:1, 30:31], self.smallf[0:1, 27:28],
                       1.0 / QMAX1, 1e-12, OP.mult, OP.max, deps=[e_su])
        e_ru = self.recip(self.smallf[0:1, 29:30], self.smallf[0:1, 28:29],
                          deps=[e_sg])
        e_rg = self.recip(self.smallf[0:1, 31:32], self.smallf[0:1, 30:31],
                          deps=[e_ru])
        e_ru2 = self.ts(self.smallf[0:1, 29:30], self.smallf[0:1, 29:30],
                        self.smallf[0:1, 2:3], None, OP.mult, deps=[e_rg])
        e_rg2 = self.ts(self.smallf[0:1, 31:32], self.smallf[0:1, 31:32],
                        self.smallf[0:1, 2:3], None, OP.mult, deps=[e_ru2])
        self.ff["e_ruc"] = self.bcast_col(29, 33, e_rg2)
        e_rgc = self.bcast_col(31, 34, self.ff["e_ruc"])
        e_sgc = self.bcast_col(30, 35, e_rgc)

        e_am0 = self.memset(self.smallf[:, 49:50], 0.0, deps=[e_rgc])
        st = {"am": e_am0, "pv": {0: self.ff["e_up_park"],
                                  1: self.ff["e_up_park"]},
              "t2": {0: None, 1: None}, "mm": {}, "ev": {}, "h": {}}
        st["ev"][-1] = self.ff["e_pv_p1"]
        wq = self.wqT
        banks = (self.psA, self.psB, self.psC, self.psD)

        def p2_slots(R):
            return (l1w[:, 2048 * (R % 2):2048 * (R % 2) + 2048],
                    l1w[:, 4096 + 2048 * (R % 2):4096 + 2048 * (R % 2) + 2048])

        def p2_mm(R):
            ee4 = [None] * 4
            for i in range(8):
                for tc in range(4):
                    ee4[tc] = self.mm(
                        banks[tc][:, 0:512],
                        wq[:, DI * i + 128 * R:DI * i + 128 * (R + 1)],
                        self.xqT[:, TPC * i + 512 * tc:TPC * i + 512 * (tc + 1)],
                        start=(i == 0), stop=(i == 7),
                        deps=[st["ev"][R - 1][tc]] if i == 0 else (),
                        mark=(i == 7))
            st["mm"][R] = ee4

        def p2_head(R):
            gi, t2 = p2_slots(R)
            evs = []
            for tc in range(4):
                evs.append(self.act(gi[:, 512 * tc:512 * (tc + 1)],
                                    banks[tc][:, 0:512],
                                    AF.Identity, bias=MAGIC,
                                    scale=self.smallf[:, 34:35],
                                    deps=[st["mm"][R][tc], e_sgc,
                                          st["pv"][R % 2]]))
            st["ev"][R] = evs
            e_2 = self.ts(gi, gi, MAGIC, None, OP.subtract, deps=[evs[3]])
            e_3 = self.act(t2, gi, AF.Sigmoid,
                           scale=self.smallf[:, 35:36],
                           deps=[e_2, st["t2"][R % 2]])
            st["h"][R] = self.act(t2, t2,
                                  AF.Identity, bias=MAGIC, scale=65536.0,
                                  deps=[e_3])

        def p2_tail(R):
            gi, t2 = p2_slots(R)
            e_5 = self.ts(t2, t2, MAGIC,
                          65535.0, OP.subtract, OP.min, deps=[st["h"][R]])
            e_6 = self.tt(gi, gi, t2, OP.mult, deps=[e_5])
            st["t2"][R % 2] = e_6
            e_7 = self.red(self.smallf[:, 48:49], gi, OP.max,
                           deps=[e_6], absv=True)
            e_8 = self.tt(self.smallf[:, 49:50], self.smallf[:, 49:50],
                          self.smallf[:, 48:49], OP.max, deps=[e_7, st["am"]])
            st["am"] = e_8
            st["pv"][R % 2] = self.dma("GY"[R % 2],
                                       d["park_g2"][128 * R:128 * (R + 1), :],
                                       gi, deps=[e_8])

        p2_mm(0)
        for R in range(24):
            if R >= 1:
                p2_tail(R - 1)
            p2_head(R)
            if R + 1 < 24:
                p2_mm(R + 1)
        p2_tail(23)
        self.ff["e_g2park"] = [st["pv"][0], st["pv"][1]]
        self.ff["e_t2f"] = [st["t2"][0], st["t2"][1]]
        self.ff["e_g2am"] = st["am"]

    def phase_l1_g2ar(self):
        e_g2m = self.part_reduce(self.smallf[:, 49:50], 26, OP.max,
                                 self.ff["e_g2am"])
        self.ff["e_g2r"] = self.ar_scalar("g2", 26, 27, OP.max, e_g2m)

    def phase_l1_p3(self):
        p, d = self.p, self.d
        l1w, e_amax = self.l1w, self.ff["e_amax"]
        e_ruc = self.ff["e_ruc"]
        e_c2 = self.tt(self.smallf[0:1, 36:37], self.smallf[0:1, 27:28],
                       self.smallf[0:1, 30:31], OP.mult, deps=[self.ff["e_g2r"]])
        e_c3 = self.ts(self.smallf[0:1, 36:37], self.smallf[0:1, 36:37],
                       2.0 ** -16 / QMAX1, 1e-12, OP.mult, OP.max, deps=[e_c2])
        e_c4 = self.recip(self.smallf[0:1, 37:38], self.smallf[0:1, 36:37],
                          deps=[e_c3])
        e_c5 = self.tt(self.smallf[0:1, 38:39], self.smallf[0:1, 37:38],
                       self.smallf[0:1, 30:31], OP.mult, deps=[e_c4])
        e_c6 = self.ts(self.smallf[0:1, 38:39], self.smallf[0:1, 38:39],
                       2.0 ** -16, None, OP.mult, deps=[e_c5])
        e_cc = self.bcast_col(38, 39, e_c6)

        e_am0 = self.memset(self.smallf[:, 50:51], 0.0, deps=[e_cc])
        e_pv2 = e_amax["park0"]
        st = {"am": e_am0, "pv": {0: self.ff["e_g2park"], 1: self.ff["e_g2park"]},
              "ui": {0: self.ff["e_t2f"][0], 1: self.ff["e_t2f"][1]},
              "l1": {}, "l2": {}, "h": {}}

        def p3_slots(R):
            return (l1w[:, 2048 * (R % 2):2048 * (R % 2) + 2048],
                    l1w[:, 4096 + 2048 * (R % 2):4096 + 2048 * (R % 2) + 2048])

        def p3_load(R):
            gi, ui = p3_slots(R)
            st["l1"][R] = self.dma("YG"[R % 2], gi,
                                   d["park_g2"][128 * R:128 * (R + 1), :],
                                   deps=[st["pv"][R % 2]])
            st["l2"][R] = self.dma("S", ui,
                                   d["park_up"][128 * R:128 * (R + 1), :],
                                   deps=[st["ui"][R % 2], e_pv2])

        def p3_head(R):
            gi, ui = p3_slots(R)
            e_1 = self.act(gi, gi,
                           AF.Identity, bias=MAGIC, scale=self.smallf[:, 39:40],
                           deps=[st["l1"][R], e_cc])
            e_3 = self.act(ui, ui,
                           AF.Identity, bias=MAGIC, scale=self.smallf[:, 33:34],
                           deps=[st["l2"][R], e_ruc])
            e_2 = self.ts(gi, gi, MAGIC, None, OP.subtract, deps=[e_1])
            e_4 = self.ts(ui, ui, MAGIC, None, OP.subtract, deps=[e_3])
            st["h"][R] = (e_2, e_4)

        def p3_tail(R):
            gi, ui = p3_slots(R)
            e_5 = self.tt(gi, gi, ui, OP.mult, deps=list(st["h"][R]))
            st["ui"][R % 2] = e_5
            e_6 = self.red(self.smallf[:, 48:49], gi, OP.max,
                           deps=[e_5], absv=True)
            e_7 = self.tt(self.smallf[:, 50:51], self.smallf[:, 50:51],
                          self.smallf[:, 48:49], OP.max, deps=[e_6, st["am"]])
            st["am"] = e_7
            st["pv"][R % 2] = self.dma("GY"[R % 2],
                                       d["park_op"][128 * R:128 * (R + 1), :],
                                       gi, deps=[e_7])

        p3_load(0)
        p3_load(1)
        for R in range(24):
            if R >= 1:
                p3_tail(R - 1)
                if R + 1 < 24:
                    p3_load(R + 1)
            p3_head(R)
        p3_tail(23)
        self.ff["e_oppark"] = [st["pv"][0], st["pv"][1]]
        e_om = self.part_reduce(self.smallf[:, 50:51], 80, OP.max, st["am"])
        self.ff["e_oor"] = self.ar_scalar("oo", 80, 81, OP.max, e_om)

    def phase_l1_p4(self):
        p, d = self.p, self.d
        e_oppark = self.ff["e_oppark"]
        p.free("lnb", "dmsb", "cmsb", "l1w", "xqT", "xsF", "xnT")
        wdq = p.alloc("wdT", (128, 24 * D), BF16)
        for j in range(24):
            self.dmat("S", wdq[:, D * j:D * (j + 1)],
                      d["cc_wd_o"][:, 128 * j:128 * (j + 1)],
                      deps=[self.e_wq_ag["wd"]])
        e_wd = self.p.dma_all("S")
        oiF = p.alloc("oiF", (128, 2 * 24 * 1024), BF16)
        ois = p.alloc("oistg", (128, 2048), F32)
        mrg = p.alloc("mrg", (128, 2048), F32)
        e_o1 = self.tt(self.smallf[0:1, 42:43], self.smallf[0:1, 81:82],
                       self.smallf[0:1, 36:37], OP.mult, deps=[self.ff["e_oor"]])
        e_o2 = self.tt(self.smallf[0:1, 42:43], self.smallf[0:1, 42:43],
                       self.smallf[0:1, 28:29], OP.mult, deps=[e_o1])
        e_o3 = self.ts(self.smallf[0:1, 42:43], self.smallf[0:1, 42:43],
                       1.0 / QMAX1, 1e-12, OP.mult, OP.max, deps=[e_o2])
        e_o4 = self.recip(self.smallf[0:1, 43:44], self.smallf[0:1, 42:43],
                          deps=[e_o3])
        e_o5 = self.tt(self.smallf[0:1, 44:45], self.smallf[0:1, 43:44],
                       self.smallf[0:1, 36:37], OP.mult, deps=[e_o4])
        e_o6 = self.tt(self.smallf[0:1, 44:45], self.smallf[0:1, 44:45],
                       self.smallf[0:1, 28:29], OP.mult, deps=[e_o5])
        e_roc = self.bcast_col(44, 51, e_o6)
        e_soc = self.bcast_col(42, 52, e_roc)

        e_bk = {0: None, 1: None}
        e_ymrg = {0: None, 1: None}
        e_sb = {}
        e_pr = {}
        for hf in range(2):
            of = 24576 * hf
            e_oi = None
            e_oid = {0: None, 1: None}
            e_ld = {}
            e_h = {}

            def oi_load(R, hf=hf, of=of):
                e_ld[R] = self.dma("YG"[R % 2],
                                   ois[:, 1024 * (R % 2):1024 * (R % 2) + 1024],
                                   d["park_op"][128 * R:128 * (R + 1),
                                                1024 * hf:1024 * (hf + 1)],
                                   deps=[e_oid[R % 2], e_oppark])

            oi_load(0)
            oi_load(1)
            for R in range(24):
                o0 = 1024 * (R % 2)
                if R >= 1:
                    op = 1024 * ((R - 1) % 2)
                    e_oi = self.ts(oiF[:, of + 1024 * (R - 1):of + 1024 * R],
                                   ois[:, op:op + 1024], MAGIC, None,
                                   OP.subtract, deps=[e_h[R - 1]])
                    e_oid[(R - 1) % 2] = e_oi
                    if R + 1 < 24:
                        oi_load(R + 1)
                e_h[R] = self.act(ois[:, o0:o0 + 1024],
                                  ois[:, o0:o0 + 1024],
                                  AF.Identity, bias=MAGIC,
                                  scale=self.smallf[:, 51:52],
                                  deps=[e_ld[R], e_roc])
            e_oi = self.ts(oiF[:, of + 1024 * 23:of + 1024 * 24],
                           ois[:, 1024:2048], MAGIC, None,
                           OP.subtract, deps=[e_h[23]])
            e_oid[1] = e_oi
            for tt_ in range(8):
                t = 8 * hf + tt_
                xb = mrg[:, 1024 * (t % 2):1024 * (t % 2) + 1024]
                e_xl = self.dma("S", xb, d["park_x"][128 * t:128 * (t + 1), :],
                                deps=[e_ymrg[t % 2], self.e_x1w])
                for half in range(2):
                    sb = self.scrb[:, 1024 * (t % 2) + 512 * half:
                                   1024 * (t % 2) + 512 * (half + 1)]
                    pr = self.scr[:, 512 * (2 * (t % 2) + half):
                                  512 * (2 * (t % 2) + half) + 512]
                    bank = (self.psA if half == 0 else self.psB)[:, 0:512]
                    ee = None
                    for j in range(24):
                        ee = self.mm(bank,
                                     oiF[:, of + 1024 * j + 128 * tt_:
                                         of + 1024 * j + 128 * (tt_ + 1)],
                                     wdq[:, D * j + 512 * half:
                                         D * j + 512 * (half + 1)],
                                     start=(j == 0), stop=(j == 23),
                                     deps=[e_oi, e_wd, e_bk[half]]
                                     if j == 0 else (), mark=(j == 23))
                    e_g = self.act(sb, bank, AF.Silu, scale=self.smallf[:, 52:53],
                                   deps=[ee, e_soc, e_sb.get((t % 2, half))])
                    e_bk[half] = e_g
                    e_sl = self.dma("YG"[half], pr,
                                    d["park_s"][128 * t:128 * (t + 1),
                                                512 * half:512 * (half + 1)],
                                    deps=[self.e_s_parked, e_oi,
                                          e_pr.get((t % 2, half))])
                    e_t2 = self.tt(pr, pr, sb, OP.mult, deps=[e_g, e_sl])
                    e_sb[(t % 2, half)] = e_t2
                    e_x2 = self.tt(xb[:, 512 * half:512 * (half + 1)],
                                   xb[:, 512 * half:512 * (half + 1)],
                                   pr, OP.add, deps=[e_t2, e_xl])
                    e_pr[(t % 2, half)] = e_x2
                e_ymrg[t % 2] = self.dma("YG"[t % 2],
                                         d["y"][128 * t:128 * (t + 1), :],
                                         xb, deps=[e_pr[(t % 2, 0)],
                                                   e_pr[(t % 2, 1)]])
        p.free("mrg", "oistg", "oiF", "wdT")

    def build_all(self, phase_limit=99):
        self.phase_loads()
        e_r1, e_r2 = self.ln_stats(self.e_x)
        if phase_limit >= 2:
            self.phase_f0_absmax(e_r2)
        if phase_limit >= 3:
            self.qT = self.p.alloc("qT", (128, 8 * TPC), BF16)
            self.phase_xnT(e_r1)
        if phase_limit >= 4:
            self.phase_q()
        if phase_limit >= 1:
            self.phase_wdequant(None)
        if phase_limit >= 5:
            self._scores_impl_setup()
            self._scores_impl()
        if phase_limit >= 6:
            self.phase_kv_kmm()
            self.phase_select()
        if phase_limit >= 7:
            self.phase_kv()
            self.p.free("mksel", "ohlocal", "onehot", "kstore", "wqw2",
                        "kvb", "xnT")
        if phase_limit >= 9:
            self.phase_assemble()
        if phase_limit >= 8:
            self.phase_ffn0()
        if phase_limit >= 9:
            self.phase_parkx()
        if phase_limit >= 10:
            self.phase_attention()
        if phase_limit >= 11:
            self.phase_proj()
            self.p.free("attnT", "asb", "asw", "vsel", "kselT", "qT")
        if phase_limit >= 12:
            self.xsb = self.p.alloc("xsb2", (128, NT * D), F32, side="right")
            for t in range(NT):
                self.dma("YG"[t % 2], self.xsb[:, D * t:D * (t + 1)],
                         self.d["park_x"][128 * t:128 * (t + 1), :],
                         deps=[self.e_x1w])
            e_r1b, _ = self.ln_stats(self.p.dma_all("Y") + self.p.dma_all("G"))
            self.e_r1b = e_r1b
            self.phase_l1_mixer(e_r1b)
        if phase_limit >= 13:
            self.phase_xq()
            self.phase_l1_p1()
            self.phase_l1_ugar()
            self.phase_l1_p2()
            self.phase_l1_wqfree()
            self.phase_l1_cd_setup()
            self.phase_l1_g2ar()
            self.phase_l1_cd(0, 12)
            self.phase_l1_p3()
            self.phase_l1_cd(12, 16)
            self.phase_l1_p4()
        else:
            for t in range(NT):
                self.dma("YG"[t % 2], self.d["y"][128 * t:128 * (t + 1), :],
                         self.xsb[:, D * t:D * (t + 1)],
                         deps=[self.p.now("V"), self.p.now("S"),
                               self.p.now("T"), self.p.dma_all("G")])
        self.p.do("G", lambda eng: eng.engine_nop(),
                  deps=self.p.dma_all("Y") + self.p.dma_all("G") +
                       self.p.dma_all("S"))
        self.p.finalize()
        self.p.stack.close()
        return self.nc

    def _scores_impl_setup(self):
        p, d = self.p, self.d
        self.kvb = p.alloc("kvb", (128, 2 * D), BF16)
        e_kb = self.dma("Y", self.kvb[:, 0:D], d["bk_bc"][:, :])
        e_vb = self.dma("Y", self.kvb[:, D:2 * D], d["bv_bc"][:, :])
        self.e_kvb = (e_kb, e_vb)
        wqw = p.alloc("wqw2", (128, 8 * D), BF16)
        wqt = d["wqkv"].ap().rearrange("(n p) m -> n p m", p=128)
        for i in range(8):
            self.dma("G", wqw[:, D * i:D * (i + 1)], wqt[i][:, D:2 * D])
        self.wqw2 = wqw
        self.e_wk = self.p.dma_all("G")


def build(meta, taps=(), phase_limit=99):
    return Builder(meta, taps).build_all(phase_limit)


def kernel(**inputs):
    in_maps, meta = _host_prep(inputs)
    nc = build(meta)
    res = run_bass_kernel_spmd(nc, in_maps, core_ids=list(range(NCORE)))
    y = np.zeros((B, S, D), dtype=np.float32)
    for c in range(NCORE):
        b_, h_ = c // 2, c % 2
        y[b_, h_ * TPC:(h_ + 1) * TPC] = res.results[c]["y"]
    return y


def locate_instr(p, idx):
    n = 0
    for e in ENGS:
        for site, cnt in p.sitemap[e]:
            if idx < n + cnt:
                return e, site
            n += cnt
    return None, "beyond program (%d instrs)" % n
